# version 02
# baseline (speedup 1.0000x reference)
import numpy as np
import ml_dtypes
from contextlib import ExitStack

import concourse.bass as bass
import concourse.mybir as mybir
from concourse.bass_utils import run_bass_kernel_spmd

F32 = mybir.dt.float32
BF16 = mybir.dt.bfloat16
I32 = mybir.dt.int32
I16 = mybir.dt.int16
AF = mybir.ActivationFunctionType
ALU = mybir.AluOpType
AX = mybir.AxisListType

D = 1024
H = 3072
NTOK = 4096
HALO = 512
TOT = NTOK + HALO
MAC = 512
NMAC = TOT // MAC
NRMAC = NMAC - 1
C = 64
NCH = MAC // C
IC = D // 128
OC3 = H // 128
NCORES = 8

EX_UP = 128 * H
EX_GATE = 128 * H
EX_DOWN = 3 * 128 * D
NEX = EX_UP + EX_GATE + EX_DOWN

STUB_F = False
DEBUG_DUMP = False
DEBUG_F = True


class Q:
    def __init__(self, sem):
        self.ops = []
        self.sem = sem
        self.cnt = 0

    def wait(self, tok):
        if tok is None:
            return
        sem, val = tok
        self.ops.append(lambda e, s=sem, v=val: e.wait_ge(s, v))

    def waits(self, *toks):
        for t in toks:
            self.wait(t)

    def run(self, fn, inc=1):
        self.cnt += inc
        tokv = self.cnt

        def cl(e, fn=fn, inc=inc, s=self.sem):
            ins = fn(e)
            if inc:
                ins.then_inc(s, inc)
        self.ops.append(cl)
        return (self.sem, tokv)

    def tok(self):
        return (self.sem, self.cnt)


def build_program():
    nc = bass.Bass("TRN2", target_bir_lowering=False)

    def din(name, shape, dtype=F32):
        return nc.dram_tensor(name, shape, dtype, kind="ExternalInput").ap()

    xT = din("xT", [D, TOT])
    wqT = din("wqT", [D, D])
    wkT = din("wkT", [D, D])
    wvT = din("wvT", [D, D])
    woT = din("woT", [D, D])
    nwfm = din("nwfm", [128, IC])
    eye_h = din("eye_h", [128, 128])
    mask_h = din("mask_h", [C, C], mybir.dt.uint8)
    dq_sl = din("dq_sl", [3 * OC3 * 128, 128])

    outT = nc.dram_tensor("outT", [D, NTOK], F32, kind="ExternalOutput").ap()
    if DEBUG_F:
        dbg_scol = nc.dram_tensor("dbg_scol", [128, 12], F32, kind="ExternalOutput").ap()
        dbg_z = nc.dram_tensor("dbg_z", [128, 3072], F32, kind="ExternalOutput").ap()
        dbg_x2 = nc.dram_tensor("dbg_x2", [128, 4096], F32, kind="ExternalOutput").ap()
        dbg_scal = nc.dram_tensor("dbg_scal", [1, 64], F32, kind="ExternalOutput").ap()
        dbg_rmax = nc.dram_tensor("dbg_rmax", [128, 1], F32, kind="ExternalOutput").ap()
        dbg_scal2 = nc.dram_tensor("dbg_scal2", [8, 1], F32, kind="ExternalOutput").ap()
        dbg_barA = nc.dram_tensor("dbg_barA", [8, 4], F32, kind="ExternalOutput").ap()
        dbg_op = nc.dram_tensor("dbg_op", [128, 3072], F32, kind="ExternalOutput").ap()
        dbg_c2a = nc.dram_tensor("dbg_c2a", [128, 3072], I32, kind="ExternalOutput").ap()
        dbg_c3 = nc.dram_tensor("dbg_c3", [128, 3072], F32, kind="ExternalOutput").ap()
        dbg_uq = nc.dram_tensor("dbg_uq", [128, 3072], F32, kind="ExternalOutput").ap()
        dbg_g2 = nc.dram_tensor("dbg_g2", [128, 3072], F32, kind="ExternalOutput").ap()
    if DEBUG_DUMP:
        dbg_S = nc.dram_tensor("dbg_S", [128, 1024], F32, kind="ExternalOutput").ap()
        dbg_o = nc.dram_tensor("dbg_o", [128, 4096], F32, kind="ExternalOutput").ap()
        dbg_qt = nc.dram_tensor("dbg_qt", [128, 512], F32, kind="ExternalOutput").ap()
        dbg_kh = nc.dram_tensor("dbg_kh", [128, 512], F32, kind="ExternalOutput").ap()
        dbg_dc = nc.dram_tensor("dbg_dc", [128, 8], F32, kind="ExternalOutput").ap()
        dbg_scm = nc.dram_tensor("dbg_scm", [64, 512], F32, kind="ExternalOutput").ap()
        dbg_pst = nc.dram_tensor("dbg_pst", [128, 1024], F32, kind="ExternalOutput").ap()

    y1_d = nc.dram_tensor("y1_d", [D, NTOK], F32)
    xln2_d = nc.dram_tensor("xln2_d", [D, NTOK], F32)
    zup_d = nc.dram_tensor("zup_d", [H, NTOK], F32)
    zgate_d = nc.dram_tensor("zgate_d", [H, NTOK], F32)
    gcode_d = nc.dram_tensor("gcode_d", [H, NTOK], I16)
    g2code_d = nc.dram_tensor("g2code_d", [H, NTOK], I16)
    ucode_d = nc.dram_tensor("ucode_d", [H, NTOK], I16)
    ln2_d = nc.dram_tensor("ln2_d", [2, NTOK], F32)
    wq_bf_d = nc.dram_tensor("wq_bf_d", [D, D], BF16)
    wk_bf_d = nc.dram_tensor("wk_bf_d", [D, D], BF16)
    wv_bf_d = nc.dram_tensor("wv_bf_d", [D, D], BF16)
    wo_bf_d = nc.dram_tensor("wo_bf_d", [D, D], BF16)
    ex_in = nc.dram_tensor("ex_in", [1, NEX], BF16)
    ex_out = nc.dram_tensor("ex_out", [NCORES, NEX], BF16, addr_space="Shared")
    bar_in = {}
    bar_out = {}
    for b in ("A", "B", "C", "D"):
        bar_in[b] = nc.dram_tensor(f"bar{b}_in", [1, 4], F32)
        bar_out[b] = nc.dram_tensor(f"bar{b}_out", [NCORES, 4], F32, addr_space="Shared")

    with ExitStack() as st:
        E = st.enter_context

        def sbt(name, shape, dtype=F32):
            return E(nc.sbuf_tensor(name, shape, dtype))

        ARENA = sbt("ARENA", [128, 24576], F32)
        SCR = sbt("SCR", [128, 4096], F32)
        ar = ARENA[:]
        xb = [ar[:, 0:4096], ar[:, 4096:8192]]
        qf = ar[:, 8192:12288]
        ff = ar[:, 12288:16384]
        Xf = ar[:, 16384:20480]
        y1f = ar[:, 20480:24576]
        sc_ = SCR[:]
        gp1 = sc_[:, 0:512]
        Apr = sc_[:, 512:1024]
        Em = sc_[:, 1024:1536]
        Ep = sc_[:, 1536:2048]
        ktf = sc_[:, 2048:2560]
        tst = sc_[:, 2560:3072]
        tst2 = sc_[:, 3072:3584]
        gps = sc_[:, 3584:4096]

        wbuf = [sbt(f"wbuf{i}", [128, 8192], BF16) for i in range(2)]
        xln_bf = sbt("xln_bf", [128, 4096], BF16)
        o_bf = sbt("o_bf", [128, 4096], BF16)
        v_bf = sbt("v_bf", [C, D], BF16)
        qt_bf = sbt("qt_bf", [128, 8 * C], BF16)
        kt_bf = sbt("kt_bf", [128, 8 * C], BF16)
        kh_bf = sbt("kh_bf", [128, 8 * C], BF16)
        khT_bf = sbt("khT_bf", [C, 8 * 128], BF16)
        scm_bf = sbt("scm_bf", [C, 8 * C], BF16)
        S_f = sbt("S_f", [128, 8 * 128], F32)
        S_bf = sbt("S_bf", [128, 8 * 128], BF16)
        BC = sbt("BC", [128, 2048], F32)
        mb, rb = BC[:, 0:512], BC[:, 512:1024]
        ab, bb = BC[:, 1024:1536], BC[:, 1536:2048]
        sqst = sbt("sqst", [128, 512], F32)
        dcol = sbt("dcol", [128, 8], F32)
        mcol = sbt("mcol", [128, 1], F32)
        cc8 = sbt("cc8", [128, 8], F32)
        eye_bf = sbt("eye_bf", [128, 128], BF16)
        eye32 = sbt("eye32", [128, 128], F32)
        mask_c = sbt("mask_c", [C, C], mybir.dt.uint8)
        zrs = sbt("zrs", [128, C], F32)
        nw_c = sbt("nw_c", [128, IC], F32)
        ones_col = sbt("ones_col", [128, 1], F32)
        ones_row = sbt("ones_row", [1, 128], F32)
        rows = sbt("rows", [1, 2048], F32)
        runmax = sbt("runmax", [128, 1], F32)
        scal = sbt("scal", [1, 64], F32)
        scol = sbt("scol", [128, 12], F32)
        dq_nat = sbt("dq_nat", [128, 128], F32)
        dq_s4 = sbt("dq_s4", [128, 4], F32)
        dq_r4 = sbt("dq_r4", [128, 4], F32)
        dq_t = sbt("dq_t", [128, 128], F32)
        dq_bf = sbt("dq_bf", [128, 128], BF16)
        dq_st = sbt("dq_st", [128, 128], BF16)
        fs3 = sbt("fs3", [128, 3072], F32)
        mcol2 = sbt("mcol2", [128, 1], F32)
        scal2 = sbt("scal2", [8, 1], F32)

        pb_mm = [E(nc.psum_tensor(f"pb_mm{i}", [128, 512], F32)) for i in range(2)]
        pb_o = E(nc.psum_tensor("pb_o", [128, 512], F32))
        pb_sc = E(nc.psum_tensor("pb_sc", [128, 512], F32))
        pb_tr = E(nc.psum_tensor("pb_tr", [128, 1024], BF16))
        pb_st = E(nc.psum_tensor("pb_st", [128, 1024], F32))
        pb_row = E(nc.psum_tensor("pb_row", [1, 512], F32))

        SQ = Q(E(nc.semaphore("sS")))
        GQ = Q(E(nc.semaphore("sG")))
        TQ = Q(E(nc.semaphore("sT")))
        VQ = Q(E(nc.semaphore("sV")))
        AQ = Q(E(nc.semaphore("sA")))
        ccs = E(nc.semaphore("ccs"))
        cc_cnt = [0]
        ALLQ = [SQ, GQ, TQ, VQ, AQ]

        def barrier_all():
            toks = [q.tok() for q in ALLQ]
            for q in ALLQ:
                for t in toks:
                    if t[0] is not q.sem:
                        q.wait(t)

        def mm(out_ap, lhsT_ap, rhs_ap, start, stop, inc=0):
            return TQ.run(lambda e, o=out_ap, l=lhsT_ap, r=rhs_ap, a=start, b=stop:
                          e.matmul(o, l, r, start=a, stop=b), inc=inc)

        def tr(out_ap, in_ap, inc=1):
            return TQ.run(lambda e, o=out_ap, i=in_ap, d=eye_bf[:]:
                          e.transpose(o, i, d), inc=inc)

        def act(q, out_ap, in_ap, func, bias=0.0, scale=1.0, accum=None, inc=1):
            return q.run(lambda e, o=out_ap, i=in_ap, f=func, b=bias, s=scale, a=accum:
                         e.activation(o, i, f, bias=b, scale=s, accum_out=a), inc=inc)

        def tt(q, out_ap, a_ap, b_ap, op, inc=1):
            return q.run(lambda e, o=out_ap, a=a_ap, b=b_ap, p=op:
                         e.tensor_tensor(o, a, b, p), inc=inc)

        def ts(q, out_ap, in_ap, s1, s2, op0, op1=ALU.bypass, inc=1):
            return q.run(lambda e, o=out_ap, i=in_ap, a=s1, b=s2, p0=op0, p1=op1:
                         e.tensor_scalar(o, i, a, b, p0, p1), inc=inc)

        def stt(q, out_ap, in0_ap, scalar, in1_ap, op0, op1, inc=1):
            return q.run(lambda e, o=out_ap, i=in0_ap, s=scalar, j=in1_ap, p0=op0, p1=op1:
                         e.scalar_tensor_tensor(o, i, s, j, p0, p1), inc=inc)

        def cp(q, out_ap, in_ap, inc=1):
            def fn(e, o=out_ap, i=in_ap):
                if hasattr(e, "tensor_copy"):
                    return e.tensor_copy(out=o, in_=i)
                return e.copy(o, i)
            return q.run(fn, inc=inc)

        def red(q, out_ap, in_ap, op, absv=False, inc=1):
            return q.run(lambda e, o=out_ap, i=in_ap, p=op, a=absv:
                         e.tensor_reduce(o, i, axis=AX.X, op=p, apply_absolute_value=a),
                         inc=inc)

        def scan_op(q, out_ap, d0, d1, init, op0, op1, inc=1):
            return q.run(lambda e, o=out_ap, a=d0, b=d1, i=init, p0=op0, p1=op1:
                         e.tensor_tensor_scan(o, a, b, i, p0, p1), inc=inc)

        def dma(q, out_ap, in_ap):
            return q.run(lambda e, o=out_ap, i=in_ap: e.dma_start(out=o, in_=i), inc=16)

        def memset(q, ap_, val, inc=1):
            return q.run(lambda e, a=ap_, v=val: e.memset(a, v), inc=inc)

        def select(q, out_ap, mask_ap, t_ap, f_ap, inc=1):
            return q.run(lambda e, o=out_ap, m=mask_ap, a=t_ap, b=f_ap:
                         e.select(o, m, a, b), inc=inc)

        def recip(q, out_ap, in_ap, inc=1):
            return q.run(lambda e, o=out_ap, i=in_ap: e.reciprocal(o, i), inc=inc)

        def vfence():
            t = VQ.tok()
            GQ.wait(t)
            t2 = GQ.run(lambda e: e.engine_nop(), inc=1)
            VQ.wait(t2)

        def allgather(in_t, out_t):
            cc_cnt[0] += 1
            val = cc_cnt[0]
            GQ.ops.append(lambda e, i=in_t, o=out_t, s=ccs:
                          e.collective_compute("AllGather", ALU.bypass,
                                               replica_groups=[list(range(NCORES))],
                                               ins=[i.ap().opt()], outs=[o.ap().opt()])
                          .then_inc(s, 1))
            return (ccs, val)

        def rrow(dst, src, eps_col):
            act(AQ, dst, src, AF.Ln, bias=cc8[0:1, eps_col:eps_col + 1])
            return act(AQ, dst, dst, AF.Exp, scale=cc8[0:1, 2:3])

        bank_prod = {}
        bank_cons = {}

        def bank_write_begin(bk):
            for t in bank_cons.get(bk, []):
                TQ.wait(t)
            bank_cons[bk] = []

        def bank_write_end(bk):
            bank_prod[bk] = TQ.tok()

        def bank_read(bk, q):
            q.wait(bank_prod.get(bk))

        def bank_read_done(bk, q):
            bank_cons.setdefault(bk, []).append(q.tok())

        def bcast_row(row_ap, dst_ap, src_tok):
            bank_write_begin("sc")
            TQ.wait(src_tok)
            mm(pb_sc[:, 0:512], ones_row[:], row_ap, True, True, inc=1)
            bank_write_end("sc")
            bank_read("sc", VQ)
            t = cp(VQ, dst_ap, pb_sc[:, 0:512])
            bank_read_done("sc", VQ)
            return t

        t_eye = dma(GQ, eye_bf[:], eye_h)
        t_mask = dma(SQ, mask_c[:], mask_h)
        t_eye32 = dma(SQ, eye32[:], eye_h)
        t_nw = dma(SQ, nw_c[:], nwfm)
        memset(VQ, zrs[:], 0.0)
        memset(VQ, ones_col[:], 1.0)
        memset(VQ, ones_row[:], 1.0)
        memset(VQ, S_f[:], 0.0)
        memset(VQ, S_bf[:], 0.0)
        memset(VQ, runmax[:], 0.0)
        memset(VQ, cc8[:, 0:1], 1e-5)
        memset(VQ, cc8[:, 1:2], 1e-6)
        memset(VQ, cc8[:, 2:3], -0.5)

        t_wq = dma(GQ, wq_bf_d.ap(), wqT)
        t_wk = dma(GQ, wk_bf_d.ap(), wkT)
        t_wv = dma(GQ, wv_bf_d.ap(), wvT)
        t_wo = None
        for half in range(2):
            rsl = slice(half * 512, (half + 1) * 512)
            tl = dma(SQ, xb[0].rearrange("p (a b) -> p a b", a=IC),
                     woT.rearrange("(a p) c -> p a c", p=128)[:, :, rsl])
            VQ.waits(tl, t_nw)
            for a in range(IC):
                ts(VQ, xln_bf[:, a * 512:(a + 1) * 512],
                   xb[0][:, a * 512:(a + 1) * 512], nw_c[:, a:a + 1], None, ALU.mult)
            SQ.wait(VQ.tok())
            t_wo = dma(SQ, wo_bf_d.ap().rearrange("(a p) c -> p a c", p=128)[:, :, rsl],
                       xln_bf[:].rearrange("p (a b) -> p a b", a=IC))
            VQ.wait(t_wo)
        t_wcast = (t_wq, t_wk, t_wv, t_wo)

        GQ.wait(t_eye)
        TQ.wait(t_eye)
        tile_row = 0
        t_last_nat_read = None
        t_last_st = None
        for tgroup, base_off, ostride in (("up", 0, H), ("gate", EX_UP, H),
                                          ("down", EX_UP + EX_GATE, D)):
            for j in range(OC3):
                SQ.wait(t_last_nat_read)
                tl = dma(SQ, dq_nat[:], dq_sl[tile_row * 128:(tile_row + 1) * 128, :])
                tile_row += 1
                VQ.wait(tl)
                red(VQ, dq_s4[:], dq_nat[:].rearrange("p (w k) -> p w k", k=32),
                    ALU.max, absv=True)
                vfence()
                ts(VQ, dq_s4[:], dq_s4[:], 1.0 / 7.0, 1e-12, ALU.mult, ALU.max)
                vfence()
                recip(VQ, dq_r4[:], dq_s4[:])
                vfence()
                t_last_nat_read = tt(VQ, dq_t[:].rearrange("p (w k) -> p w k", k=32),
                                     dq_nat[:].rearrange("p (w k) -> p w k", k=32),
                                     dq_r4[:, :, None].broadcast_to([128, 4, 32]), ALU.mult)
                cp(VQ, dq_t[:].bitcast(I32), dq_t[:])
                cp(VQ, dq_t[:], dq_t[:].bitcast(I32))
                t_dq = tt(VQ, dq_bf[:].rearrange("p (w k) -> p w k", k=32),
                          dq_t[:].rearrange("p (w k) -> p w k", k=32),
                          dq_s4[:, :, None].broadcast_to([128, 4, 32]), ALU.mult)
                bank_write_begin("tr")
                TQ.wait(t_dq)
                t_tp = tr(pb_tr[:, 0:128], dq_bf[:])
                bank_write_end("tr")
                VQ.wait(t_tp)
                bank_read("tr", AQ)
                AQ.wait(t_last_st)
                t_ev = cp(AQ, dq_st[:], pb_tr[:, 0:128])
                bank_read_done("tr", AQ)
                SQ.wait(t_ev)
                if tgroup == "down":
                    icl, hc = j // IC, j % IC
                    off = base_off + icl * (128 * D) + hc * 128
                else:
                    off = base_off + j * 128
                t_last_st = SQ.run(
                    lambda e, o=bass.AP(ex_in, off, [[ostride, 128], [1, 128]]),
                    i=dq_st[:]: e.dma_start(out=o, in_=i), inc=16)
        GQ.wait(SQ.tok())
        t_gather = allgather(ex_in, ex_out)

        def load_x(m):
            tok0 = m * MAC
            return dma(SQ, xb[m % 2].rearrange("p (a b) -> p a b", a=IC),
                       bass.AP(xT.tensor, tok0, [[TOT, 128], [128 * TOT, IC], [1, MAC]]))

        def load_w(dram, slot, after_tok):
            SQ.wait(after_tok)
            return dma(SQ, wbuf[slot][:].rearrange("p (a b) -> p a b", a=IC),
                       dram.ap().rearrange("(a p) c -> p a c", p=128))

        def row_stats(src_fn, src_tok):
            bank_write_begin("row")
            TQ.wait(src_tok)
            for i in range(IC):
                mm(pb_row[:, 0:512], ones_col[:], src_fn(i),
                   start=(i == 0), stop=(i == IC - 1), inc=(1 if i == IC - 1 else 0))
            bank_write_end("row")
            bank_read("row", VQ)
            cp(VQ, rows[:, 0:512], pb_row[:, 0:512])
            bank_read_done("row", VQ)
            for i in range(IC):
                t_sq = act(AQ, sqst[:], src_fn(i), AF.Square)
                bank_write_begin("row")
                TQ.wait(t_sq)
                mm(pb_row[:, 0:512], ones_col[:], sqst[:],
                   start=(i == 0), stop=(i == IC - 1), inc=1)
                bank_write_end("row")
                AQ.wait(TQ.tok())
            bank_read("row", VQ)
            cp(VQ, rows[:, 512:1024], pb_row[:, 0:512])
            bank_read_done("row", VQ)

        SQ.waits(*t_wcast)
        SQ.wait(GQ.tok())
        x_tok = [None, None]
        x_tok[0] = load_x(0)
        w_tok = {}

        for m in range(NMAC):
            is_halo = (m == 0)
            par = m % 2
            x_cur = xb[par]
            tok0 = m * MAC
            rtok0 = tok0 - HALO

            w_tok["wk"] = load_w(wk_bf_d, 0, None)
            w_tok["wq"] = load_w(wq_bf_d, 1, None)
            if m + 1 < NMAC:
                x_tok[(m + 1) % 2] = load_x(m + 1)

            VQ.wait(x_tok[par])
            AQ.wait(x_tok[par])

            row_stats(lambda i: x_cur[:, i * 512:(i + 1) * 512], x_tok[par])
            ts(VQ, rows[:, 0:512], rows[:, 0:512], 1.0 / D, None, ALU.mult)
            ts(VQ, rows[:, 512:1024], rows[:, 512:1024], 1.0 / D, None, ALU.mult)
            tt(VQ, rows[:, 1024:1536], rows[:, 0:512], rows[:, 0:512], ALU.mult)
            t_v = tt(VQ, rows[:, 512:1024], rows[:, 512:1024], rows[:, 1024:1536],
                     ALU.subtract)
            AQ.wait(t_v)
            t_r = rrow(rows[:, 512:1024], rows[:, 512:1024], 0)
            VQ.wait(t_r)
            bcast_row(rows[:, 0:512], mb, VQ.tok())
            t_bc = bcast_row(rows[:, 512:1024], rb, t_r)
            for i in range(IC):
                tt(VQ, tst, x_cur[:, i * 512:(i + 1) * 512], mb, ALU.subtract)
                tt(VQ, xln_bf[:, i * 512:(i + 1) * 512], tst, rb, ALU.mult)
            t_xln = VQ.tok()

            def proj_fm(wtok, wslot, out_f32, fn):
                TQ.waits(wtok, t_xln)
                for oc in range(8):
                    bk = "mm%d" % (oc % 2)
                    pb = pb_mm[oc % 2]
                    bank_write_begin(bk)
                    for ic in range(IC):
                        mm(pb[:], wbuf[wslot][:, ic * 1024 + oc * 128: ic * 1024 + (oc + 1) * 128],
                           xln_bf[:, ic * 512:(ic + 1) * 512],
                           start=(ic == 0), stop=(ic == IC - 1), inc=(1 if ic == IC - 1 else 0))
                    bank_write_end(bk)
                    bank_read(bk, AQ)
                    act(AQ, out_f32[:, oc * 512:(oc + 1) * 512], pb[:], fn)
                    bank_read_done(bk, AQ)
                return AQ.tok()

            t_f_done = proj_fm(w_tok["wk"], 0, ff, AF.Sigmoid)
            w_tok["wv"] = load_w(wv_bf_d, 0, TQ.tok())
            if not is_halo:
                t_q_done = proj_fm(w_tok["wq"], 1, qf, AF.Silu)
            else:
                t_q_done = AQ.tok()
            w_tok["wo"] = load_w(wo_bf_d, 1, TQ.tok())

            for ch in range(NCH):
                c0 = ch * C
                fview = ff[:].rearrange("p (s t) -> p s t", s=8)[:, :, c0:c0 + C]
                qview = qf[:].rearrange("p (s t) -> p s t", s=8)[:, :, c0:c0 + C]
                AQ.wait(t_f_done)
                act(AQ, gp1[:].rearrange("p (s t) -> p s t", s=8), fview, AF.Exp, scale=-1.0)
                t_g = act(AQ, gps[:], gp1[:], AF.Ln, bias=1.0)
                VQ.wait(t_g)
                for sl in range(8):
                    scan_op(VQ, Apr[:, sl * C:(sl + 1) * C], gps[:, sl * C:(sl + 1) * C],
                            zrs[:], 0.0, ALU.add, ALU.bypass)
                t_scan = VQ.tok()
                AQ.wait(t_scan)
                cp(AQ, dcol[:, 0:8],
                   Apr[:].rearrange("p (s t) -> p s t", s=8)[:, :, C - 1:C].opt())
                act(AQ, dcol[:, 0:8], dcol[:, 0:8], AF.Exp, scale=-1.0)
                act(AQ, Em[:], Apr[:], AF.Exp, scale=-1.0)
                t_ep = act(AQ, Ep[:], Apr[:], AF.Exp)
                VQ.wait(t_ep)
                if not is_halo:
                    tt(VQ, qt_bf[:].rearrange("p (s t) -> p s t", s=8), qview,
                       Em[:].rearrange("p (s t) -> p s t", s=8), ALU.mult)
                stt(VQ, tst2[:].rearrange("p (s t) -> p s t", s=8), fview, -1.0,
                    Ep[:].rearrange("p (s t) -> p s t", s=8), ALU.mult, ALU.mult)
                tt(VQ, ktf[:], tst2[:], Ep[:], ALU.add)
                if not is_halo:
                    cp(VQ, kt_bf[:], ktf[:])
                for sl in range(8):
                    ts(VQ, kh_bf[:, sl * C:(sl + 1) * C], ktf[:, sl * C:(sl + 1) * C],
                       dcol[:, sl:sl + 1], None, ALU.mult)
                t_kh = VQ.tok()

                TQ.wait(w_tok["wv"])
                for hb in range(2):
                    bk = "mm%d" % hb
                    bank_write_begin(bk)
                    for ic in range(IC):
                        mm(pb_mm[hb][0:C, :],
                           xln_bf[:, ic * 512 + c0: ic * 512 + c0 + C],
                           wbuf[0][:, ic * 1024 + hb * 512: ic * 1024 + (hb + 1) * 512],
                           start=(ic == 0), stop=(ic == IC - 1), inc=(1 if ic == IC - 1 else 0))
                    bank_write_end(bk)
                    bank_read(bk, AQ)
                    act(AQ, v_bf[:, hb * 512:(hb + 1) * 512], pb_mm[hb][0:C, :], AF.Identity)
                    bank_read_done(bk, AQ)
                t_vev = AQ.tok()

                bank_write_begin("tr")
                TQ.wait(t_kh)
                for sl in range(8):
                    tr(pb_tr[0:C, sl * 128:(sl + 1) * 128], kh_bf[:, sl * C:(sl + 1) * C],
                       inc=(1 if sl == 7 else 0))
                bank_write_end("tr")
                bank_read("tr", AQ)
                AQ.wait(t_vev)
                t_khT = cp(AQ, khT_bf[:], pb_tr[0:C, :])
                bank_read_done("tr", AQ)

                if not is_halo:
                    bank_write_begin("sc")
                    TQ.wait(VQ.tok())
                    for sl in range(8):
                        mm(pb_sc[0:C, sl * C:(sl + 1) * C], kt_bf[:, sl * C:(sl + 1) * C],
                           qt_bf[:, sl * C:(sl + 1) * C], start=True, stop=True,
                           inc=(1 if sl == 7 else 0))
                    bank_write_end("sc")
                    bank_read("sc", VQ)
                    VQ.wait(t_mask)
                    t_scm = select(VQ, scm_bf[:].rearrange("p (s t) -> p s t", s=8),
                                   mask_c[:, None, :].broadcast_to([C, 8, C]),
                                   pb_sc[0:C, :].rearrange("p (s t) -> p s t", s=8),
                                   zrs[0:C, None, 0:C].broadcast_to([C, 8, C]))
                    bank_read_done("sc", VQ)

                    bank_write_begin("o")
                    TQ.waits(t_scm, t_vev, t_khT)
                    for sl in range(8):
                        mm(pb_o[:, sl * C:(sl + 1) * C], v_bf[:, sl * 128:(sl + 1) * 128],
                           scm_bf[:, sl * C:(sl + 1) * C], start=True, stop=False)
                        mm(pb_o[:, sl * C:(sl + 1) * C], S_bf[:, sl * 128:(sl + 1) * 128],
                           qt_bf[:, sl * C:(sl + 1) * C], start=False, stop=True,
                           inc=(1 if sl == 7 else 0))
                    bank_write_end("o")

                bank_write_begin("st")
                TQ.waits(t_khT, t_vev)
                for sl in range(8):
                    mm(pb_st[:, sl * 128:(sl + 1) * 128], khT_bf[:, sl * 128:(sl + 1) * 128],
                       v_bf[:, sl * 128:(sl + 1) * 128], start=True, stop=True,
                       inc=(1 if sl == 7 else 0))
                bank_write_end("st")

                if not is_halo:
                    oview = o_bf[:].rearrange("p (s t) -> p s t", s=8)[:, :, c0:c0 + C]
                    bank_read("o", AQ)
                    act(AQ, sqst[:], pb_o[:], AF.Square)
                    act(AQ, oview, pb_o[:].rearrange("p (s t) -> p s t", s=8), AF.Identity)
                    bank_read_done("o", AQ)
                    bank_write_begin("row")
                    TQ.wait(AQ.tok())
                    for sl in range(8):
                        mm(pb_row[:, c0:c0 + C], ones_col[:], sqst[:, sl * C:(sl + 1) * C],
                           start=(sl == 0), stop=(sl == 7), inc=(1 if sl == 7 else 0))
                    bank_write_end("row")
                    AQ.wait(TQ.tok())

                bank_read("st", VQ)
                for sl in range(8):
                    stt(VQ, S_f[:, sl * 128:(sl + 1) * 128], S_f[:, sl * 128:(sl + 1) * 128],
                        dcol[:, sl:sl + 1], pb_st[:, sl * 128:(sl + 1) * 128],
                        ALU.mult, ALU.add)
                t_sup = VQ.tok()
                bank_read_done("st", VQ)
                AQ.wait(t_sup)
                t_sbf = cp(AQ, S_bf[:], S_f[:])
                TQ.wait(t_sbf)
                AQ.wait(VQ.tok())
                VQ.wait(AQ.tok())
                TQ.wait(VQ.tok())

            if not is_halo:
                bank_read("row", VQ)
                cp(VQ, rows[:, 1024:1536], pb_row[:, 0:512])
                bank_read_done("row", VQ)
                t_ms = ts(VQ, rows[:, 1024:1536], rows[:, 1024:1536], 1.0 / D, None, ALU.mult)
                AQ.wait(t_ms)
                t_c = rrow(rows[:, 1024:1536], rows[:, 1024:1536], 1)
                VQ.wait(t_c)

                TQ.waits(w_tok["wo"], AQ.tok())
                for dc in range(IC):
                    bk = "mm%d" % (dc % 2)
                    pb = pb_mm[dc % 2]
                    bank_write_begin(bk)
                    for kc in range(IC):
                        mm(pb[:], wbuf[1][:, kc * 1024 + dc * 128: kc * 1024 + (dc + 1) * 128],
                           o_bf[:, kc * 512:(kc + 1) * 512],
                           start=(kc == 0), stop=(kc == IC - 1), inc=(1 if kc == IC - 1 else 0))
                    bank_write_end(bk)
                    bank_read(bk, AQ)
                    act(AQ, Xf[:, dc * 512:(dc + 1) * 512], pb[:], AF.Identity)
                    bank_read_done(bk, AQ)
                t_X = AQ.tok()

                VQ.wait(t_X)
                row_stats(lambda i: Xf[:, i * 512:(i + 1) * 512], t_X)
                ts(VQ, rows[:, 0:512], rows[:, 0:512], 1.0 / D, None, ALU.mult)
                ts(VQ, rows[:, 512:1024], rows[:, 512:1024], 1.0 / D, None, ALU.mult)
                tt(VQ, rows[:, 1536:2048], rows[:, 0:512], rows[:, 0:512], ALU.mult)
                tt(VQ, rows[:, 512:1024], rows[:, 512:1024], rows[:, 1536:2048], ALU.subtract)
                tt(VQ, rows[:, 1536:2048], rows[:, 1024:1536], rows[:, 1024:1536], ALU.mult)
                t_cv = tt(VQ, rows[:, 512:1024], rows[:, 512:1024], rows[:, 1536:2048], ALU.mult)
                AQ.wait(t_cv)
                t_r3 = rrow(rows[:, 512:1024], rows[:, 512:1024], 0)
                VQ.wait(t_r3)
                tt(VQ, rows[:, 1536:2048], rows[:, 1024:1536], rows[:, 512:1024], ALU.mult)
                tt(VQ, rows[:, 0:512], rows[:, 0:512], rows[:, 1536:2048], ALU.mult)
                ts(VQ, rows[:, 0:512], rows[:, 0:512], -1.0, None, ALU.mult)
                bcast_row(rows[:, 1536:2048], ab, VQ.tok())
                bcast_row(rows[:, 0:512], bb, VQ.tok())

                GQ.wait(VQ.tok())
                for dc in range(IC):
                    t_mu = tt(VQ, tst, Xf[:, dc * 512:(dc + 1) * 512], ab, ALU.mult)
                    GQ.wait(t_mu)
                    tt(GQ, tst2, tst, bb, ALU.add)
                    tt(GQ, y1f[:, dc * 512:(dc + 1) * 512], tst2,
                       x_cur[:, dc * 512:(dc + 1) * 512], ALU.add)
                    VQ.wait(GQ.tok())
                t_y1 = GQ.tok()
                SQ.wait(t_y1)
                dma(SQ, bass.AP(y1_d, rtok0, [[NTOK, 128], [128 * NTOK, IC], [1, MAC]]),
                    y1f[:].rearrange("p (a b) -> p a b", a=IC))

                VQ.wait(t_y1)
                AQ.wait(t_y1)
                row_stats(lambda i: y1f[:, i * 512:(i + 1) * 512], t_y1)
                ts(VQ, rows[:, 0:512], rows[:, 0:512], 1.0 / D, None, ALU.mult)
                ts(VQ, rows[:, 512:1024], rows[:, 512:1024], 1.0 / D, None, ALU.mult)
                tt(VQ, rows[:, 1536:2048], rows[:, 0:512], rows[:, 0:512], ALU.mult)
                t_v2 = tt(VQ, rows[:, 512:1024], rows[:, 512:1024], rows[:, 1536:2048],
                          ALU.subtract)
                AQ.wait(t_v2)
                t_r2 = rrow(rows[:, 512:1024], rows[:, 512:1024], 0)
                VQ.wait(t_r2)
                mi = m - 1
                SQ.wait(VQ.tok())
                dma(SQ, bass.AP(ln2_d, mi * MAC, [[1, 1], [1, MAC]]), rows[0:1, 0:512])
                dma(SQ, bass.AP(ln2_d, NTOK + mi * MAC, [[1, 1], [1, MAC]]),
                    rows[0:1, 512:1024])
                bcast_row(rows[:, 0:512], mb, VQ.tok())
                bcast_row(rows[:, 512:1024], rb, VQ.tok())
                GQ.wait(VQ.tok())
                for dc in range(IC):
                    tt(GQ, tst, y1f[:, dc * 512:(dc + 1) * 512], mb, ALU.subtract)
                    t_x2 = tt(GQ, tst2, tst, rb, ALU.mult)
                    SQ.wait(t_x2)
                    t_st2 = dma(SQ, bass.AP(xln2_d, rtok0 + dc * 128 * NTOK,
                                            [[NTOK, 128], [1, MAC]]), tst2)
                    VQ.wait(t_x2)
                    vfence()
                    red(VQ, mcol[:], tst2, ALU.max, absv=True)
                    vfence()
                    tt(VQ, runmax[:], runmax[:], mcol[:], ALU.max)
                    GQ.waits(VQ.tok(), t_st2)

            if DEBUG_DUMP and m == 1:
                barrier_all()
                cp(VQ, tst, S_f[:, 0:512])
                cp(VQ, tst2, S_f[:, 512:1024])
                t_d1 = VQ.tok()
                SQ.wait(t_d1)
                dma(SQ, bass.AP(dbg_S.tensor, 0, [[1024, 128], [1, 512]]), tst)
                dma(SQ, bass.AP(dbg_S.tensor, 512, [[1024, 128], [1, 512]]), tst2)
                VQ.wait(SQ.tok())
                cp(VQ, tst, pb_st[:, 0:512])
                cp(VQ, tst2, pb_st[:, 512:1024])
                t_d1b = VQ.tok()
                SQ.wait(t_d1b)
                dma(SQ, bass.AP(dbg_pst.tensor, 0, [[1024, 128], [1, 512]]), tst)
                dma(SQ, bass.AP(dbg_pst.tensor, 512, [[1024, 128], [1, 512]]), tst2)
                cp(VQ, SCR[:, 0:512], qt_bf[:])
                cp(VQ, SCR[:, 512:1024], kh_bf[:])
                cp(VQ, SCR[:, 1024:1536].rearrange("p (a b) -> p a b", a=8)[0:64],
                   scm_bf[:].rearrange("p (a b) -> p a b", a=8))
                t_d2 = VQ.tok()
                SQ.wait(t_d2)
                dma(SQ, dbg_qt, SCR[:, 0:512])
                dma(SQ, dbg_kh, SCR[:, 512:1024])
                dma(SQ, dbg_dc, dcol[:])
                dma(SQ, bass.AP(dbg_scm.tensor, 0, [[512, 64], [1, 512]]), SCR[0:64, 1024:1536])
                for dcb in range(IC):
                    cp(VQ, tst, o_bf[:, dcb * 512:(dcb + 1) * 512])
                    t_d3 = VQ.tok()
                    SQ.wait(t_d3)
                    dma(SQ, bass.AP(dbg_o.tensor, dcb * 512, [[4096, 128], [1, 512]]), tst)
                    VQ.wait(SQ.tok())
            barrier_all()

        def finalize_scalar(run_ap, ncols, bar, post_scale, result_cols):
            bank_write_begin("sc")
            TQ.waits(VQ.tok(), t_eye32)
            TQ.run(lambda e, o=pb_sc[0:ncols, 0:128], i=run_ap, d=eye32[:]:
                   e.transpose(o, i, d), inc=1)
            bank_write_end("sc")
            bank_read("sc", VQ)
            t_red = red(VQ, scal2[0:ncols, 0:1], pb_sc[0:ncols, 0:128], ALU.max)
            bank_read_done("sc", VQ)
            SQ.wait(t_red)
            t_bi = dma(SQ, bass.AP(bar_in[bar], 0, [[1, 1], [1, ncols]]), scal2[0:ncols, 0:1])
            GQ.wait(t_bi)
            cc_cnt[0] += 1
            ccv = cc_cnt[0]
            GQ.ops.append(lambda e, i=bar_in[bar], o=bar_out[bar], s=ccs:
                          e.collective_compute("AllGather", ALU.bypass,
                                               replica_groups=[list(range(NCORES))],
                                               ins=[i.ap().opt()], outs=[o.ap().opt()])
                          .then_inc(s, 1))
            GQ.wait((ccs, ccv))
            t_bo = dma(GQ, scal[0:1, 32:64], bass.AP(bar_out[bar], 0, [[1, 1], [1, 32]]))
            VQ.wait(t_bo)
            for j, rc in enumerate(result_cols):
                red(VQ, scal[0:1, j:j + 1],
                    scal[0:1, 32:64].rearrange("p (a b) -> p a b", a=8)[:, :, j:j + 1].opt(),
                    ALU.max)
                vfence()
                ts(VQ, scal[0:1, j:j + 1], scal[0:1, j:j + 1], post_scale, 1e-12,
                   ALU.mult, ALU.max)
                vfence()
                recip(VQ, scal[0:1, 16 + j:17 + j], scal[0:1, j:j + 1])
                vfence()
                for src, dstc in ((scal[0:1, j:j + 1], rc), (scal[0:1, 16 + j:17 + j], rc + 1)):
                    bank_write_begin("sc")
                    TQ.wait(VQ.tok())
                    mm(pb_sc[:, 0:1], ones_row[:], src, True, True, inc=1)
                    bank_write_end("sc")
                    bank_read("sc", VQ)
                    cp(VQ, scol[:, dstc:dstc + 1], pb_sc[:, 0:1])
                    bank_read_done("sc", VQ)

        if DEBUG_F:
            SQ.wait(VQ.tok())
            dma(SQ, dbg_rmax, runmax[:])
        finalize_scalar(runmax[:], 1, "A", 1.0 / 32767.0, [0])
        if DEBUG_F:
            SQ.wait(VQ.tok())
            dma(SQ, dbg_scal, scal[0:1, 0:64])
            dma(SQ, dbg_scal2, scal2[:])
            t_ba = dma(SQ, dbg_barA, bar_out["A"].ap())
            VQ.wait(SQ.tok())

        upW = ARENA[:].bitcast(BF16)[:, 0:24576]
        gateW = ARENA[:].bitcast(BF16)[:, 24576:49152]
        GQ.wait(t_gather)
        SQ.wait(GQ.tok())
        wload = []
        for icx in range(IC):
            wload.append(dma(SQ, upW[:, icx * H:(icx + 1) * H],
                             bass.AP(ex_out, icx * NEX, [[H, 128], [1, H]])))
            wload.append(dma(SQ, gateW[:, icx * H:(icx + 1) * H],
                             bass.AP(ex_out, icx * NEX + EX_UP, [[H, 128], [1, H]])))
        t_wl = SQ.tok()
        TQ.wait(t_wl)
        xln2f = SCR[:]
        i32v = wbuf[0][:].bitcast(I32)
        codebf = xln_bf[:]
        zstage = wbuf[1][:].bitcast(F32)
        memset(VQ, dcol[:], 0.0)
        for mi in range(NRMAC):
            rt0 = mi * MAC
            t_x2l = dma(SQ, xln2f.rearrange("p (a b) -> p a b", a=IC),
                        bass.AP(xln2_d, rt0, [[NTOK, 128], [128 * NTOK, IC], [1, MAC]]))
            VQ.wait(t_x2l)
            if DEBUG_F and mi == 0:
                dma(SQ, dbg_x2, xln2f)
            ts(VQ, i32v[:, 0:4096], xln2f, scol[:, 1:2], None, ALU.mult)
            t_code = cp(VQ, codebf, i32v[:, 0:4096])
            TQ.wait(t_code)
            for tix, (Wv_, zd, runc) in enumerate(((upW, zup_d, 0), (gateW, zgate_d, 1))):
                for g in range(4):
                    for oc6 in range(6):
                        oc = g * 6 + oc6
                        bk = "mm%d" % (oc % 2)
                        pb = pb_mm[oc % 2]
                        bank_write_begin(bk)
                        for icx in range(IC):
                            mm(pb[:], Wv_[:, icx * H + oc * 128: icx * H + (oc + 1) * 128],
                               codebf[:, icx * 512:(icx + 1) * 512],
                               start=(icx == 0), stop=(icx == IC - 1),
                               inc=(1 if icx == IC - 1 else 0))
                        bank_write_end(bk)
                        bank_read(bk, AQ)
                        act(AQ, zstage[:, oc6 * 512:(oc6 + 1) * 512], pb[:],
                            AF.Identity, scale=scol[:, 0:1])
                        bank_read_done(bk, AQ)
                    t_ev = AQ.tok()
                    VQ.wait(t_ev)
                    vfence()
                    red(VQ, mcol[:], zstage[:, 0:3072], ALU.max, absv=True)
                    vfence()
                    tt(VQ, dcol[:, runc:runc + 1], dcol[:, runc:runc + 1], mcol[:], ALU.max)
                    SQ.wait(t_ev)
                    t_zst = dma(SQ, bass.AP(zd, g * 6 * 128 * NTOK + rt0,
                                            [[NTOK, 128], [128 * NTOK, 6], [1, MAC]]),
                                zstage[:, 0:3072].rearrange("p (a b) -> p a b", a=6))
                    AQ.wait(t_zst)
                    AQ.wait(VQ.tok())
                    if DEBUG_F and mi == 0 and tix == 0 and g == 0:
                        dma(SQ, dbg_z, zstage[:, 0:3072])
            SQ.wait(VQ.tok())
            SQ.wait(TQ.tok())
            VQ.wait(TQ.tok())
        barrier_all()

        finalize_scalar(dcol[:, 0:2], 2, "B", 1.0 / 32767.0, [2, 4])

        zgv = wbuf[1][:].bitcast(F32)
        gq_f = SCR[:, 0:3072]
        i32g = wbuf[0][:].bitcast(I32)
        for mi in range(NRMAC):
            rt0 = mi * MAC
            for g in range(4):
                t_zl = dma(SQ, zgv[:, 0:3072].rearrange("p (a b) -> p a b", a=6),
                           bass.AP(zgate_d, g * 6 * 128 * NTOK + rt0,
                                   [[NTOK, 128], [128 * NTOK, 6], [1, MAC]]))
                VQ.wait(t_zl)
                t_ci = ts(VQ, i32g[:, 0:3072], zgv[:, 0:3072], scol[:, 5:6], None, ALU.mult)
                if DEBUG_F and mi == 0 and g == 0:
                    vfence()
                    SQ.wait(VQ.tok())
                    t_dc2a = dma(SQ, dbg_c2a, i32g[:, 0:3072])
                    VQ.wait(t_dc2a)
                    GQ.wait(t_dc2a)
                GQ.wait(t_ci)
                t_gst = dma(GQ, bass.AP(gcode_d, g * 6 * 128 * NTOK + rt0,
                                        [[NTOK, 128], [128 * NTOK, 6], [1, MAC]]),
                            i32g[:, 0:3072].rearrange("p (a b) -> p a b", a=6))
                t_gq = ts(VQ, gq_f, i32g[:, 0:3072], scol[:, 4:5], None, ALU.mult)
                AQ.wait(t_gq)
                t_sg = act(AQ, fs3[:], gq_f, AF.Sigmoid)
                VQ.wait(t_sg)
                VQ.wait(t_gst)
                ts(VQ, i32g[:, 0:3072], fs3[:], 65536.0, 65535.0, ALU.mult, ALU.min)
                cp(VQ, fs3[:], i32g[:, 0:3072])
                t_g2 = tt(VQ, gq_f, gq_f, fs3[:], ALU.mult)
                vfence()
                red(VQ, mcol[:], gq_f, ALU.max, absv=True)
                vfence()
                tt(VQ, dcol[:, 2:3], dcol[:, 2:3], mcol[:], ALU.max)
                AQ.wait(VQ.tok())
                SQ.wait(VQ.tok())
        barrier_all()

        finalize_scalar(dcol[:, 2:3], 1, "C", (1.0 / 65536.0) / 32767.0, [6])
        vfence()
        ts(VQ, scol[:, 10:11], scol[:, 7:8], 1.0 / 65536.0, None, ALU.mult)
        vfence()

        for mi in range(NRMAC):
            rt0 = mi * MAC
            for g in range(4):
                base = g * 6 * 128 * NTOK + rt0
                apz = [[NTOK, 128], [128 * NTOK, 6], [1, MAC]]
                t_gl = dma(GQ, gq_f.rearrange("p (a b) -> p a b", a=6),
                           bass.AP(gcode_d, base, apz))
                t_ul = dma(SQ, zgv[:, 0:3072].rearrange("p (a b) -> p a b", a=6),
                           bass.AP(zup_d, base, apz))
                VQ.wait(t_gl)
                if DEBUG_F and mi == 0 and g == 0:
                    SQ.wait(t_gl)
                    t_dc3 = dma(SQ, dbg_c3, gq_f)
                    VQ.wait(t_dc3)
                ts(VQ, gq_f, gq_f, scol[:, 4:5], None, ALU.mult)
                AQ.wait(VQ.tok())
                t_sg = act(AQ, fs3[:], gq_f, AF.Sigmoid)
                VQ.wait(t_sg)
                ts(VQ, i32g[:, 0:3072], fs3[:], 65536.0, 65535.0, ALU.mult, ALU.min)
                cp(VQ, fs3[:], i32g[:, 0:3072])
                tt(VQ, gq_f, gq_f, fs3[:], ALU.mult)
                t_g2c = ts(VQ, i32g[:, 0:3072], gq_f, scol[:, 10:11], None, ALU.mult)
                GQ.wait(t_g2c)
                t_g2st = dma(GQ, bass.AP(g2code_d, base, apz),
                             i32g[:, 0:3072].rearrange("p (a b) -> p a b", a=6))
                ts(VQ, fs3[:], i32g[:, 0:3072], scol[:, 6:7], None, ALU.mult)
                VQ.wait(t_ul)
                VQ.wait(t_g2st)
                t_uc = ts(VQ, i32g[:, 0:3072], zgv[:, 0:3072], scol[:, 3:4], None, ALU.mult)
                GQ.wait(t_uc)
                t_ust = dma(GQ, bass.AP(ucode_d, base, apz),
                            i32g[:, 0:3072].rearrange("p (a b) -> p a b", a=6))
                ts(VQ, zgv[:, 0:3072], i32g[:, 0:3072], scol[:, 2:3], None, ALU.mult)
                t_op = tt(VQ, gq_f, fs3[:], zgv[:, 0:3072], ALU.mult)
                vfence()
                if DEBUG_F and mi == 0 and g == 0:
                    SQ.wait(VQ.tok())
                    t_dop = dma(SQ, dbg_op, gq_f)
                    t_duq = dma(SQ, dbg_uq, zgv[:, 0:3072])
                    t_dg2 = dma(SQ, dbg_g2, fs3[:])
                    VQ.waits(t_dop, t_duq, t_dg2)
                    GQ.waits(t_dop, t_duq, t_dg2)
                red(VQ, mcol2[:], gq_f, ALU.max, absv=True)
                vfence()
                tt(VQ, dcol[:, 3:4], dcol[:, 3:4], mcol2[:], ALU.max)
                VQ.wait(t_ust)
                AQ.wait(VQ.tok())
                GQ.wait(VQ.tok())
                SQ.wait(VQ.tok())
        barrier_all()

        finalize_scalar(dcol[:, 3:4], 1, "D", 1.0 / 32767.0, [8])

        downW = ARENA[:].bitcast(BF16)[:, 0:24576]
        GQ.wait(SQ.tok())
        dwl = []
        for icx in range(NCORES):
            for icl in range(3):
                icg = icx * 3 + icl
                dwl.append(dma(SQ, downW[:, icg * D:(icg + 1) * D],
                               bass.AP(ex_out, icx * NEX + EX_UP + EX_GATE + icl * 128 * D,
                                       [[D, 128], [1, D]])))
        t_dwl = SQ.tok()
        TQ.wait(t_dwl)
        Xs = ARENA[:, 12288:16384]
        y1s = ARENA[:, 16384:20480]
        y2s = ARENA[:, 20480:24576]

        def codebf_sl(ch):
            if ch < 16:
                return wbuf[1][:, ch * 512:(ch + 1) * 512]
            return o_bf[:, (ch - 16) * 512:(ch - 15) * 512]

        for mi in range(NRMAC):
            rt0 = mi * MAC
            t_y1l = dma(SQ, y1s.rearrange("p (a b) -> p a b", a=IC),
                        bass.AP(y1_d, rt0, [[NTOK, 128], [128 * NTOK, IC], [1, MAC]]))
            for g in range(4):
                base = g * 6 * 128 * NTOK + rt0
                apz = [[NTOK, 128], [128 * NTOK, 6], [1, MAC]]
                t_g2l = dma(GQ, fs3[:].rearrange("p (a b) -> p a b", a=6),
                            bass.AP(g2code_d, base, apz))
                t_ucl = dma(GQ, gq_f.rearrange("p (a b) -> p a b", a=6),
                            bass.AP(ucode_d, base, apz))
                VQ.waits(t_g2l, t_ucl)
                ts(VQ, fs3[:], fs3[:], scol[:, 6:7], None, ALU.mult)
                ts(VQ, gq_f, gq_f, scol[:, 2:3], None, ALU.mult)
                tt(VQ, fs3[:], fs3[:], gq_f, ALU.mult)
                t_oc = ts(VQ, i32g[:, 0:3072], fs3[:], scol[:, 9:10], None, ALU.mult)
                for oc6 in range(6):
                    ch = g * 6 + oc6
                    cp(VQ, codebf_sl(ch), i32g[:, oc6 * 512:(oc6 + 1) * 512])
                GQ.wait(VQ.tok())
            t_codes = VQ.tok()
            TQ.waits(t_codes, t_dwl)
            for dc in range(IC):
                bk = "mm%d" % (dc % 2)
                pb = pb_mm[dc % 2]
                bank_write_begin(bk)
                for ch in range(24):
                    mm(pb[:], downW[:, ch * D + dc * 128: ch * D + (dc + 1) * 128],
                       codebf_sl(ch), start=(ch == 0), stop=(ch == 23),
                       inc=(1 if ch == 23 else 0))
                bank_write_end(bk)
                bank_read(bk, AQ)
                act(AQ, Xs[:, dc * 512:(dc + 1) * 512], pb[:], AF.Identity,
                    scale=scol[:, 8:9])
                bank_read_done(bk, AQ)
            t_X4 = AQ.tok()
            VQ.wait(t_X4)
            row_stats(lambda i: Xs[:, i * 512:(i + 1) * 512], t_X4)
            ts(VQ, rows[:, 0:512], rows[:, 0:512], 1.0 / D, None, ALU.mult)
            ts(VQ, rows[:, 512:1024], rows[:, 512:1024], 1.0 / D, None, ALU.mult)
            tt(VQ, rows[:, 1024:1536], rows[:, 0:512], rows[:, 0:512], ALU.mult)
            t_v4 = tt(VQ, rows[:, 512:1024], rows[:, 512:1024], rows[:, 1024:1536],
                      ALU.subtract)
            AQ.wait(t_v4)
            t_r4 = rrow(rows[:, 512:1024], rows[:, 512:1024], 0)
            VQ.wait(t_r4)
            tt(VQ, rows[:, 0:512], rows[:, 0:512], rows[:, 512:1024], ALU.mult)
            ts(VQ, rows[:, 0:512], rows[:, 0:512], -1.0, None, ALU.mult)
            bcast_row(rows[:, 512:1024], ab, VQ.tok())
            bcast_row(rows[:, 0:512], bb, VQ.tok())
            VQ.wait(t_y1l)
            GQ.wait(VQ.tok())
            for dc in range(IC):
                t_m4 = tt(VQ, tst, Xs[:, dc * 512:(dc + 1) * 512], ab, ALU.mult)
                GQ.wait(t_m4)
                tt(GQ, tst2, tst, bb, ALU.add)
                t_y2 = tt(GQ, y2s[:, dc * 512:(dc + 1) * 512], tst2,
                          y1s[:, dc * 512:(dc + 1) * 512], ALU.add)
                VQ.wait(GQ.tok())
            t_y2d = GQ.tok()
            SQ.wait(t_y2d)
            dma(SQ, bass.AP(outT.tensor, rt0, [[NTOK, 128], [128 * NTOK, IC], [1, MAC]]),
                y2s.rearrange("p (a b) -> p a b", a=IC))
            GQ.wait(SQ.tok())
            VQ.wait(SQ.tok())
        if DEBUG_F:
            dma(SQ, dbg_scol, scol[:, 0:12])
        SQ.wait(SQ.tok())
        GQ.wait(GQ.tok())

        with nc.Block() as block:
            @block.sync
            def _(q):
                for f_ in SQ.ops:
                    f_(q)

            @block.gpsimd
            def _(q):
                for f_ in GQ.ops:
                    f_(q)

            @block.tensor
            def _(q):
                for f_ in TQ.ops:
                    f_(q)

            @block.vector
            def _(q):
                for f_ in VQ.ops:
                    f_(q)

            @block.scalar
            def _(q):
                for f_ in AQ.ops:
                    f_(q)

    return nc


def _stage_inputs(inputs):
    x = np.asarray(inputs["L0_attn_x"])
    xTs = [np.ascontiguousarray(x[s].T) for s in range(2)]
    eye = np.eye(128, dtype=np.float32)
    mask = np.zeros((C, C), dtype=np.uint8)
    for s_ in range(C):
        mask[s_, s_:] = 1
    nw = np.ascontiguousarray(np.asarray(inputs["L0_attn_norm_w"]).reshape(IC, 128).T)
    nw = np.ascontiguousarray(nw.astype(np.float32))
    w_up = np.ascontiguousarray(np.asarray(inputs["L0_ffn_w_up"], dtype=np.float32))
    w_gate = np.ascontiguousarray(np.asarray(inputs["L0_ffn_w_gate"], dtype=np.float32))
    w_down = np.ascontiguousarray(np.asarray(inputs["L0_ffn_w_down"], dtype=np.float32))
    base = dict(
        wqT=np.ascontiguousarray(np.asarray(inputs["L0_attn_Wq"], dtype=np.float32).T),
        wkT=np.ascontiguousarray(np.asarray(inputs["L0_attn_Wk"], dtype=np.float32).T),
        wvT=np.ascontiguousarray(np.asarray(inputs["L0_attn_Wv"], dtype=np.float32).T),
        woT=np.ascontiguousarray(np.asarray(inputs["L0_attn_Wo"], dtype=np.float32).T),
        nwfm=nw, eye_h=eye, mask_h=mask,
    )
    maps = []
    for core in range(NCORES):
        seq, seg = core // 4, core % 4
        t0 = seg * NTOK
        xs = np.zeros((D, TOT), dtype=np.float32)
        lo = t0 - HALO
        if lo < 0:
            xs[:, HALO:] = xTs[seq][:, t0:t0 + NTOK]
        else:
            xs[:, :] = xTs[seq][:, lo:t0 + NTOK]
        sl = []
        for j in range(OC3):
            sl.append(w_up[j * 128:(j + 1) * 128, core * 128:(core + 1) * 128])
        for j in range(OC3):
            sl.append(w_gate[j * 128:(j + 1) * 128, core * 128:(core + 1) * 128])
        for icl in range(3):
            icg = core * 3 + icl
            for hc in range(IC):
                sl.append(w_down[hc * 128:(hc + 1) * 128, icg * 128:(icg + 1) * 128])
        m = dict(base)
        m["xT"] = np.ascontiguousarray(xs)
        m["dq_sl"] = np.ascontiguousarray(np.concatenate(sl, axis=0))
        maps.append(m)
    return maps


_NC_CACHE = None


def kernel(L0_attn_x, L0_attn_Wq, L0_attn_Wk, L0_attn_Wv, L0_attn_Wo,
           L0_attn_norm_w, L0_ffn_w_gate, L0_ffn_w_up, L0_ffn_w_down):
    global _NC_CACHE
    inputs = dict(L0_attn_x=L0_attn_x, L0_attn_Wq=L0_attn_Wq, L0_attn_Wk=L0_attn_Wk,
                  L0_attn_Wv=L0_attn_Wv, L0_attn_Wo=L0_attn_Wo,
                  L0_attn_norm_w=L0_attn_norm_w, L0_ffn_w_gate=L0_ffn_w_gate,
                  L0_ffn_w_up=L0_ffn_w_up, L0_ffn_w_down=L0_ffn_w_down)
    if _NC_CACHE is None:
        _NC_CACHE = build_program()
    nc = _NC_CACHE
    maps = _stage_inputs(inputs)
    res = run_bass_kernel_spmd(nc, maps, core_ids=list(range(NCORES)))
    out = np.empty((2, 16384, D), dtype=np.float32)
    for core in range(NCORES):
        seq, seg = core // 4, core % 4
        o = np.asarray(res.results[core]["outT"])
        out[seq, seg * NTOK:(seg + 1) * NTOK, :] = o.T
    return out


# version 04
# speedup vs baseline: 1.0295x; 1.0295x over previous
import numpy as np
import ml_dtypes
from contextlib import ExitStack

import concourse.bass as bass
import concourse.mybir as mybir
from concourse.bass_utils import run_bass_kernel_spmd

F32 = mybir.dt.float32
BF16 = mybir.dt.bfloat16
I32 = mybir.dt.int32
I16 = mybir.dt.int16
AF = mybir.ActivationFunctionType
ALU = mybir.AluOpType
AX = mybir.AxisListType

D = 1024
H = 3072
NTOK = 4096
HALO = 512
TOT = NTOK + HALO
MAC = 512
NMAC = TOT // MAC
NRMAC = NMAC - 1
C = 64
NCH = MAC // C
IC = D // 128
OC3 = H // 128
NCORES = 8

EX_UP = 128 * H
EX_GATE = 128 * H
EX_DOWN = 3 * 128 * D
NEX = EX_UP + EX_GATE + EX_DOWN

STUB_F = False
DEBUG_DUMP = False
DEBUG_F = False


class Q:
    def __init__(self, sem):
        self.ops = []
        self.sem = sem
        self.cnt = 0

    def wait(self, tok):
        if tok is None:
            return
        sem, val = tok
        self.ops.append(lambda e, s=sem, v=val: e.wait_ge(s, v))

    def waits(self, *toks):
        for t in toks:
            self.wait(t)

    def run(self, fn, inc=1):
        self.cnt += inc
        tokv = self.cnt

        def cl(e, fn=fn, inc=inc, s=self.sem):
            ins = fn(e)
            if inc:
                ins.then_inc(s, inc)
        self.ops.append(cl)
        return (self.sem, tokv)

    def tok(self):
        return (self.sem, self.cnt)


def build_program():
    nc = bass.Bass("TRN2", target_bir_lowering=False)

    def din(name, shape, dtype=F32):
        return nc.dram_tensor(name, shape, dtype, kind="ExternalInput").ap()

    xT = din("xT", [D, TOT])
    wqT = din("wqT", [D, D])
    wkT = din("wkT", [D, D])
    wvT = din("wvT", [D, D])
    woT = din("woT", [D, D])
    nwfm = din("nwfm", [128, IC])
    eye_h = din("eye_h", [128, 128])
    mask_h = din("mask_h", [C, C], mybir.dt.uint8)
    dq_sl = din("dq_sl", [3 * OC3 * 128, 128])

    outT = nc.dram_tensor("outT", [D, NTOK], F32, kind="ExternalOutput").ap()
    if DEBUG_DUMP:
        dbg_S = nc.dram_tensor("dbg_S", [128, 1024], F32, kind="ExternalOutput").ap()
        dbg_o = nc.dram_tensor("dbg_o", [128, 4096], F32, kind="ExternalOutput").ap()
        dbg_qt = nc.dram_tensor("dbg_qt", [128, 512], F32, kind="ExternalOutput").ap()
        dbg_kh = nc.dram_tensor("dbg_kh", [128, 512], F32, kind="ExternalOutput").ap()
        dbg_dc = nc.dram_tensor("dbg_dc", [128, 8], F32, kind="ExternalOutput").ap()
        dbg_scm = nc.dram_tensor("dbg_scm", [64, 512], F32, kind="ExternalOutput").ap()
        dbg_pst = nc.dram_tensor("dbg_pst", [128, 1024], F32, kind="ExternalOutput").ap()

    y1_d = nc.dram_tensor("y1_d", [D, NTOK], F32)
    xln2_d = nc.dram_tensor("xln2_d", [D, NTOK], F32)
    zup_d = nc.dram_tensor("zup_d", [H, NTOK], F32)
    zgate_d = nc.dram_tensor("zgate_d", [H, NTOK], F32)
    gcode_d = nc.dram_tensor("gcode_d", [H, NTOK], I16)
    g2code_d = nc.dram_tensor("g2code_d", [H, NTOK], I16)
    ucode_d = nc.dram_tensor("ucode_d", [H, NTOK], I16)
    ln2_d = nc.dram_tensor("ln2_d", [2, NTOK], F32)
    wq_bf_d = nc.dram_tensor("wq_bf_d", [D, D], BF16)
    wk_bf_d = nc.dram_tensor("wk_bf_d", [D, D], BF16)
    wv_bf_d = nc.dram_tensor("wv_bf_d", [D, D], BF16)
    wo_bf_d = nc.dram_tensor("wo_bf_d", [D, D], BF16)
    ex_in = nc.dram_tensor("ex_in", [1, NEX], BF16)
    ex_out = nc.dram_tensor("ex_out", [NCORES, NEX], BF16, addr_space="Shared")
    bar_in = {}
    bar_out = {}
    for b in ("A", "B", "C", "D"):
        bar_in[b] = nc.dram_tensor(f"bar{b}_in", [1, 4], F32)
        bar_out[b] = nc.dram_tensor(f"bar{b}_out", [NCORES, 4], F32, addr_space="Shared")

    with ExitStack() as st:
        E = st.enter_context

        def sbt(name, shape, dtype=F32):
            return E(nc.sbuf_tensor(name, shape, dtype))

        ARENA = sbt("ARENA", [128, 24576], F32)
        SCR = sbt("SCR", [128, 4096], F32)
        ar = ARENA[:]
        xb = [ar[:, 0:4096], ar[:, 4096:8192]]
        qf = ar[:, 8192:12288]
        ff = ar[:, 12288:16384]
        Xf = ar[:, 16384:20480]
        y1f = ar[:, 20480:24576]
        sc_ = SCR[:]
        gp1 = sc_[:, 0:512]
        Apr = sc_[:, 512:1024]
        Em = sc_[:, 1024:1536]
        Ep = sc_[:, 1536:2048]
        ktf = sc_[:, 2048:2560]
        tst = sc_[:, 2560:3072]
        tst2 = sc_[:, 3072:3584]
        gps = sc_[:, 3584:4096]

        wbuf = [sbt(f"wbuf{i}", [128, 8192], BF16) for i in range(2)]
        xln_bf = sbt("xln_bf", [128, 4096], BF16)
        o_bf = sbt("o_bf", [128, 4096], BF16)
        v_bf = sbt("v_bf", [C, D], BF16)
        qt_bf = sbt("qt_bf", [128, 8 * C], BF16)
        kt_bf = sbt("kt_bf", [128, 8 * C], BF16)
        kh_bf = sbt("kh_bf", [128, 8 * C], BF16)
        khT_bf = sbt("khT_bf", [C, 8 * 128], BF16)
        scm_bf = sbt("scm_bf", [C, 8 * C], BF16)
        S_f = sbt("S_f", [128, 8 * 128], F32)
        S_bf = sbt("S_bf", [128, 8 * 128], BF16)
        BC = sbt("BC", [128, 2048], F32)
        mb, rb = BC[:, 0:512], BC[:, 512:1024]
        ab, bb = BC[:, 1024:1536], BC[:, 1536:2048]
        sqst = sbt("sqst", [128, 512], F32)
        dcol = sbt("dcol", [128, 8], F32)
        cc8 = sbt("cc8", [128, 4], F32)
        eye_bf = sbt("eye_bf", [128, 128], BF16)
        eye32 = sbt("eye32", [128, 128], F32)
        mask_c = sbt("mask_c", [C, C], mybir.dt.uint8)
        zrs = sbt("zrs", [128, C], F32)
        nw_c = sbt("nw_c", [128, IC], F32)
        ones_col = sbt("ones_col", [128, 1], F32)
        ones_row = sbt("ones_row", [1, 128], F32)
        rows = sbt("rows", [1, 2048], F32)
        runmax = sbt("runmax", [128, 1], F32)
        scal = sbt("scal", [1, 64], F32)
        scol = sbt("scol", [128, 12], F32)
        dq_nat = sbt("dq_nat", [128, 128], F32)
        dq_s4 = sbt("dq_s4", [128, 4], F32)
        dq_r4 = sbt("dq_r4", [128, 4], F32)
        dq_t = sbt("dq_t", [128, 128], F32)
        dq_bf = sbt("dq_bf", [128, 128], BF16)
        dq_st = sbt("dq_st", [128, 128], BF16)
        maxst = sbt("maxst", [128, 64], F32)
        scal2 = sbt("scal2", [8, 1], F32)

        pb_mm = [E(nc.psum_tensor(f"pb_mm{i}", [128, 512], F32)) for i in range(2)]
        pb_o = E(nc.psum_tensor("pb_o", [128, 512], F32))
        pb_sc = E(nc.psum_tensor("pb_sc", [128, 512], F32))
        pb_tr = E(nc.psum_tensor("pb_tr", [128, 1024], BF16))
        pb_st = E(nc.psum_tensor("pb_st", [128, 1024], F32))
        pb_row = E(nc.psum_tensor("pb_row", [1, 512], F32))

        SQ = Q(E(nc.semaphore("sS")))
        GQ = Q(E(nc.semaphore("sG")))
        TQ = Q(E(nc.semaphore("sT")))
        VQ = Q(E(nc.semaphore("sV")))
        AQ = Q(E(nc.semaphore("sA")))
        ccs = E(nc.semaphore("ccs"))
        cc_cnt = [0]
        ALLQ = [SQ, GQ, TQ, VQ, AQ]

        def barrier_all():
            toks = [q.tok() for q in ALLQ]
            for q in ALLQ:
                for t in toks:
                    if t[0] is not q.sem:
                        q.wait(t)

        def mm(out_ap, lhsT_ap, rhs_ap, start, stop, inc=0):
            return TQ.run(lambda e, o=out_ap, l=lhsT_ap, r=rhs_ap, a=start, b=stop:
                          e.matmul(o, l, r, start=a, stop=b), inc=inc)

        def tr(out_ap, in_ap, inc=1):
            return TQ.run(lambda e, o=out_ap, i=in_ap, d=eye_bf[:]:
                          e.transpose(o, i, d), inc=inc)

        def act(q, out_ap, in_ap, func, bias=0.0, scale=1.0, accum=None, inc=1):
            return q.run(lambda e, o=out_ap, i=in_ap, f=func, b=bias, s=scale, a=accum:
                         e.activation(o, i, f, bias=b, scale=s, accum_out=a), inc=inc)

        def tt(q, out_ap, a_ap, b_ap, op, inc=1):
            return q.run(lambda e, o=out_ap, a=a_ap, b=b_ap, p=op:
                         e.tensor_tensor(o, a, b, p), inc=inc)

        def ts(q, out_ap, in_ap, s1, s2, op0, op1=ALU.bypass, inc=1):
            return q.run(lambda e, o=out_ap, i=in_ap, a=s1, b=s2, p0=op0, p1=op1:
                         e.tensor_scalar(o, i, a, b, p0, p1), inc=inc)

        def stt(q, out_ap, in0_ap, scalar, in1_ap, op0, op1, inc=1):
            return q.run(lambda e, o=out_ap, i=in0_ap, s=scalar, j=in1_ap, p0=op0, p1=op1:
                         e.scalar_tensor_tensor(o, i, s, j, p0, p1), inc=inc)

        def cp(q, out_ap, in_ap, inc=1):
            def fn(e, o=out_ap, i=in_ap):
                if hasattr(e, "tensor_copy"):
                    return e.tensor_copy(out=o, in_=i)
                return e.copy(o, i)
            return q.run(fn, inc=inc)

        def red(q, out_ap, in_ap, op, absv=False, inc=1):
            return q.run(lambda e, o=out_ap, i=in_ap, p=op, a=absv:
                         e.tensor_reduce(o, i, axis=AX.X, op=p, apply_absolute_value=a),
                         inc=inc)

        def scan_op(q, out_ap, d0, d1, init, op0, op1, inc=1):
            return q.run(lambda e, o=out_ap, a=d0, b=d1, i=init, p0=op0, p1=op1:
                         e.tensor_tensor_scan(o, a, b, i, p0, p1), inc=inc)

        def dma(q, out_ap, in_ap):
            return q.run(lambda e, o=out_ap, i=in_ap: e.dma_start(out=o, in_=i), inc=16)

        def memset(q, ap_, val, inc=1):
            return q.run(lambda e, a=ap_, v=val: e.memset(a, v), inc=inc)

        def select(q, out_ap, mask_ap, t_ap, f_ap, inc=1):
            return q.run(lambda e, o=out_ap, m=mask_ap, a=t_ap, b=f_ap:
                         e.select(o, m, a, b), inc=inc)

        def recip(q, out_ap, in_ap, inc=1):
            return q.run(lambda e, o=out_ap, i=in_ap: e.reciprocal(o, i), inc=inc)

        def vfence():
            t = VQ.tok()
            GQ.wait(t)
            t2 = GQ.run(lambda e: e.engine_nop(), inc=1)
            VQ.wait(t2)

        def allgather(in_t, out_t):
            cc_cnt[0] += 1
            val = cc_cnt[0]
            GQ.ops.append(lambda e, i=in_t, o=out_t, s=ccs:
                          e.collective_compute("AllGather", ALU.bypass,
                                               replica_groups=[list(range(NCORES))],
                                               ins=[i.ap().opt()], outs=[o.ap().opt()])
                          .then_inc(s, 1))
            return (ccs, val)

        def rrow(dst, src, eps_col):
            act(AQ, dst, src, AF.Ln, bias=cc8[0:1, eps_col:eps_col + 1])
            return act(AQ, dst, dst, AF.Exp, scale=cc8[0:1, 2:3])

        bank_prod = {}
        bank_cons = {}

        def bank_write_begin(bk):
            for t in bank_cons.get(bk, []):
                TQ.wait(t)
            bank_cons[bk] = []

        def bank_write_end(bk):
            bank_prod[bk] = TQ.tok()

        def bank_read(bk, q):
            q.wait(bank_prod.get(bk))

        def bank_read_done(bk, q):
            bank_cons.setdefault(bk, []).append(q.tok())

        def bcast_row(row_ap, dst_ap, src_tok):
            bank_write_begin("sc")
            TQ.wait(src_tok)
            mm(pb_sc[:, 0:512], ones_row[:], row_ap, True, True, inc=1)
            bank_write_end("sc")
            bank_read("sc", VQ)
            t = cp(VQ, dst_ap, pb_sc[:, 0:512])
            bank_read_done("sc", VQ)
            return t

        t_eye = dma(GQ, eye_bf[:], eye_h)
        t_mask = dma(SQ, mask_c[:], mask_h)
        t_eye32 = dma(SQ, eye32[:], eye_h)
        t_nw = dma(SQ, nw_c[:], nwfm)
        memset(VQ, zrs[:], 0.0)
        memset(VQ, ones_col[:], 1.0)
        memset(VQ, ones_row[:], 1.0)
        memset(VQ, S_f[:], 0.0)
        memset(VQ, S_bf[:], 0.0)
        memset(VQ, runmax[:], 0.0)
        memset(VQ, cc8[:, 0:1], 1e-5)
        memset(VQ, cc8[:, 1:2], 1e-6)
        memset(VQ, cc8[:, 2:3], -0.5)

        t_wq = dma(GQ, wq_bf_d.ap(), wqT)
        t_wk = dma(GQ, wk_bf_d.ap(), wkT)
        t_wv = dma(GQ, wv_bf_d.ap(), wvT)
        t_wo = None
        for half in range(2):
            rsl = slice(half * 512, (half + 1) * 512)
            tl = dma(SQ, xb[0].rearrange("p (a b) -> p a b", a=IC),
                     woT.rearrange("(a p) c -> p a c", p=128)[:, :, rsl])
            VQ.waits(tl, t_nw)
            for a in range(IC):
                ts(VQ, xln_bf[:, a * 512:(a + 1) * 512],
                   xb[0][:, a * 512:(a + 1) * 512], nw_c[:, a:a + 1], None, ALU.mult)
            SQ.wait(VQ.tok())
            t_wo = dma(SQ, wo_bf_d.ap().rearrange("(a p) c -> p a c", p=128)[:, :, rsl],
                       xln_bf[:].rearrange("p (a b) -> p a b", a=IC))
            VQ.wait(t_wo)
        t_wcast = (t_wq, t_wk, t_wv, t_wo)

        GQ.wait(t_eye)
        TQ.wait(t_eye)
        tile_row = 0
        t_last_nat_read = None
        t_last_st = None
        for tgroup, base_off, ostride in (("up", 0, H), ("gate", EX_UP, H),
                                          ("down", EX_UP + EX_GATE, D)):
            for j in range(OC3):
                SQ.wait(t_last_nat_read)
                tl = dma(SQ, dq_nat[:], dq_sl[tile_row * 128:(tile_row + 1) * 128, :])
                tile_row += 1
                VQ.wait(tl)
                red(VQ, dq_s4[:], dq_nat[:].rearrange("p (w k) -> p w k", k=32),
                    ALU.max, absv=True)
                vfence()
                ts(VQ, dq_s4[:], dq_s4[:], 1.0 / 7.0, 1e-12, ALU.mult, ALU.max)
                vfence()
                recip(VQ, dq_r4[:], dq_s4[:])
                vfence()
                t_last_nat_read = tt(VQ, dq_t[:].rearrange("p (w k) -> p w k", k=32),
                                     dq_nat[:].rearrange("p (w k) -> p w k", k=32),
                                     dq_r4[:, :, None].broadcast_to([128, 4, 32]), ALU.mult)
                cp(VQ, dq_t[:].bitcast(I32), dq_t[:])
                cp(VQ, dq_t[:], dq_t[:].bitcast(I32))
                t_dq = tt(VQ, dq_bf[:].rearrange("p (w k) -> p w k", k=32),
                          dq_t[:].rearrange("p (w k) -> p w k", k=32),
                          dq_s4[:, :, None].broadcast_to([128, 4, 32]), ALU.mult)
                bank_write_begin("tr")
                TQ.wait(t_dq)
                t_tp = tr(pb_tr[:, 0:128], dq_bf[:])
                bank_write_end("tr")
                VQ.wait(t_tp)
                bank_read("tr", AQ)
                AQ.wait(t_last_st)
                t_ev = cp(AQ, dq_st[:], pb_tr[:, 0:128])
                bank_read_done("tr", AQ)
                SQ.wait(t_ev)
                if tgroup == "down":
                    icl, hc = j // IC, j % IC
                    off = base_off + icl * (128 * D) + hc * 128
                else:
                    off = base_off + j * 128
                t_last_st = SQ.run(
                    lambda e, o=bass.AP(ex_in, off, [[ostride, 128], [1, 128]]),
                    i=dq_st[:]: e.dma_start(out=o, in_=i), inc=16)
        GQ.wait(SQ.tok())
        t_gather = allgather(ex_in, ex_out)

        def load_x(m):
            tok0 = m * MAC
            return dma(SQ, xb[m % 2].rearrange("p (a b) -> p a b", a=IC),
                       bass.AP(xT.tensor, tok0, [[TOT, 128], [128 * TOT, IC], [1, MAC]]))

        def load_w(dram, slot, after_tok):
            SQ.wait(after_tok)
            return dma(SQ, wbuf[slot][:].rearrange("p (a b) -> p a b", a=IC),
                       dram.ap().rearrange("(a p) c -> p a c", p=128))

        def row_stats(src_fn, src_tok):
            bank_write_begin("row")
            TQ.wait(src_tok)
            for i in range(IC):
                mm(pb_row[:, 0:512], ones_col[:], src_fn(i),
                   start=(i == 0), stop=(i == IC - 1), inc=(1 if i == IC - 1 else 0))
            bank_write_end("row")
            bank_read("row", VQ)
            cp(VQ, rows[:, 0:512], pb_row[:, 0:512])
            bank_read_done("row", VQ)
            sq_mm_tok = [None, None]
            for i in range(IC):
                buf = sqst[:] if i % 2 == 0 else tst
                AQ.wait(sq_mm_tok[i % 2])
                t_sq = act(AQ, buf, src_fn(i), AF.Square)
                bank_write_begin("row")
                TQ.wait(t_sq)
                mm(pb_row[:, 0:512], ones_col[:], buf,
                   start=(i == 0), stop=(i == IC - 1), inc=1)
                bank_write_end("row")
                sq_mm_tok[i % 2] = TQ.tok()
            bank_read("row", VQ)
            cp(VQ, rows[:, 512:1024], pb_row[:, 0:512])
            bank_read_done("row", VQ)

        SQ.waits(*t_wcast)
        SQ.wait(GQ.tok())
        x_tok = [None, None]
        x_tok[0] = load_x(0)
        w_tok = {}

        for m in range(NMAC):
            is_halo = (m == 0)
            par = m % 2
            x_cur = xb[par]
            tok0 = m * MAC
            rtok0 = tok0 - HALO

            w_tok["wk"] = load_w(wk_bf_d, 0, None)
            w_tok["wq"] = load_w(wq_bf_d, 1, None)
            if m + 1 < NMAC:
                x_tok[(m + 1) % 2] = load_x(m + 1)

            VQ.wait(x_tok[par])
            AQ.wait(x_tok[par])

            row_stats(lambda i: x_cur[:, i * 512:(i + 1) * 512], x_tok[par])
            ts(VQ, rows[:, 0:512], rows[:, 0:512], 1.0 / D, None, ALU.mult)
            ts(VQ, rows[:, 512:1024], rows[:, 512:1024], 1.0 / D, None, ALU.mult)
            tt(VQ, rows[:, 1024:1536], rows[:, 0:512], rows[:, 0:512], ALU.mult)
            t_v = tt(VQ, rows[:, 512:1024], rows[:, 512:1024], rows[:, 1024:1536],
                     ALU.subtract)
            AQ.wait(t_v)
            t_r = rrow(rows[:, 512:1024], rows[:, 512:1024], 0)
            VQ.wait(t_r)
            bcast_row(rows[:, 0:512], mb, VQ.tok())
            t_bc = bcast_row(rows[:, 512:1024], rb, t_r)
            for i in range(IC):
                tt(VQ, tst, x_cur[:, i * 512:(i + 1) * 512], mb, ALU.subtract)
                tt(VQ, xln_bf[:, i * 512:(i + 1) * 512], tst, rb, ALU.mult)
            t_xln = VQ.tok()

            def proj_fm(wtok, wslot, out_f32, fn):
                TQ.waits(wtok, t_xln)
                for oc in range(8):
                    bk = "mm%d" % (oc % 2)
                    pb = pb_mm[oc % 2]
                    bank_write_begin(bk)
                    for ic in range(IC):
                        mm(pb[:], wbuf[wslot][:, ic * 1024 + oc * 128: ic * 1024 + (oc + 1) * 128],
                           xln_bf[:, ic * 512:(ic + 1) * 512],
                           start=(ic == 0), stop=(ic == IC - 1), inc=(1 if ic == IC - 1 else 0))
                    bank_write_end(bk)
                    bank_read(bk, AQ)
                    act(AQ, out_f32[:, oc * 512:(oc + 1) * 512], pb[:], fn)
                    bank_read_done(bk, AQ)
                return AQ.tok()

            t_f_done = proj_fm(w_tok["wk"], 0, ff, AF.Sigmoid)
            w_tok["wv"] = load_w(wv_bf_d, 0, TQ.tok())
            if not is_halo:
                t_q_done = proj_fm(w_tok["wq"], 1, qf, AF.Silu)
            else:
                t_q_done = AQ.tok()
            w_tok["wo"] = load_w(wo_bf_d, 1, TQ.tok())

            pch = {"scan": None, "ep": None, "kh": None, "st": None}
            for ch in range(NCH):
                c0 = ch * C
                fview = ff[:].rearrange("p (s t) -> p s t", s=8)[:, :, c0:c0 + C]
                qview = qf[:].rearrange("p (s t) -> p s t", s=8)[:, :, c0:c0 + C]
                AQ.wait(t_f_done)
                AQ.wait(pch["scan"])
                act(AQ, gp1[:].rearrange("p (s t) -> p s t", s=8), fview, AF.Exp, scale=-1.0)
                t_g = act(AQ, gps[:], gp1[:], AF.Ln, bias=1.0)
                VQ.waits(t_g, pch["ep"])
                for sl in range(8):
                    scan_op(VQ, Apr[:, sl * C:(sl + 1) * C], gps[:, sl * C:(sl + 1) * C],
                            zrs[:], 0.0, ALU.add, ALU.bypass)
                t_scan = VQ.tok()
                pch["scan"] = t_scan
                AQ.waits(t_scan, pch["kh"])
                cp(AQ, dcol[:, 0:8],
                   Apr[:].rearrange("p (s t) -> p s t", s=8)[:, :, C - 1:C].opt())
                act(AQ, dcol[:, 0:8], dcol[:, 0:8], AF.Exp, scale=-1.0)
                act(AQ, Em[:], Apr[:], AF.Exp, scale=-1.0)
                t_ep = act(AQ, Ep[:], Apr[:], AF.Exp)
                pch["ep"] = t_ep
                VQ.waits(t_ep, pch["st"])
                if not is_halo:
                    tt(VQ, qt_bf[:].rearrange("p (s t) -> p s t", s=8), qview,
                       Em[:].rearrange("p (s t) -> p s t", s=8), ALU.mult)
                stt(VQ, tst2[:].rearrange("p (s t) -> p s t", s=8), fview, -1.0,
                    Ep[:].rearrange("p (s t) -> p s t", s=8), ALU.mult, ALU.mult)
                tt(VQ, ktf[:], tst2[:], Ep[:], ALU.add)
                if not is_halo:
                    cp(VQ, kt_bf[:], ktf[:])
                for sl in range(8):
                    ts(VQ, kh_bf[:, sl * C:(sl + 1) * C], ktf[:, sl * C:(sl + 1) * C],
                       dcol[:, sl:sl + 1], None, ALU.mult)
                t_kh = VQ.tok()

                TQ.wait(w_tok["wv"])
                for hb in range(2):
                    bk = "mm%d" % hb
                    bank_write_begin(bk)
                    for ic in range(IC):
                        mm(pb_mm[hb][0:C, :],
                           xln_bf[:, ic * 512 + c0: ic * 512 + c0 + C],
                           wbuf[0][:, ic * 1024 + hb * 512: ic * 1024 + (hb + 1) * 512],
                           start=(ic == 0), stop=(ic == IC - 1), inc=(1 if ic == IC - 1 else 0))
                    bank_write_end(bk)
                    bank_read(bk, AQ)
                    act(AQ, v_bf[:, hb * 512:(hb + 1) * 512], pb_mm[hb][0:C, :], AF.Identity)
                    bank_read_done(bk, AQ)
                t_vev = AQ.tok()

                bank_write_begin("tr")
                TQ.wait(t_kh)
                for sl in range(8):
                    tr(pb_tr[0:C, sl * 128:(sl + 1) * 128], kh_bf[:, sl * C:(sl + 1) * C],
                       inc=(1 if sl == 7 else 0))
                bank_write_end("tr")
                bank_read("tr", AQ)
                AQ.wait(t_vev)
                t_khT = cp(AQ, khT_bf[:], pb_tr[0:C, :])
                bank_read_done("tr", AQ)

                if not is_halo:
                    bank_write_begin("sc")
                    TQ.wait(VQ.tok())
                    for sl in range(8):
                        mm(pb_sc[0:C, sl * C:(sl + 1) * C], kt_bf[:, sl * C:(sl + 1) * C],
                           qt_bf[:, sl * C:(sl + 1) * C], start=True, stop=True,
                           inc=(1 if sl == 7 else 0))
                    bank_write_end("sc")
                    bank_read("sc", VQ)
                    VQ.wait(t_mask)
                    t_scm = select(VQ, scm_bf[:].rearrange("p (s t) -> p s t", s=8),
                                   mask_c[:, None, :].broadcast_to([C, 8, C]),
                                   pb_sc[0:C, :].rearrange("p (s t) -> p s t", s=8),
                                   zrs[0:C, None, 0:C].broadcast_to([C, 8, C]))
                    bank_read_done("sc", VQ)

                    bank_write_begin("o")
                    TQ.waits(t_scm, t_vev, t_khT)
                    for sl in range(8):
                        mm(pb_o[:, sl * C:(sl + 1) * C], v_bf[:, sl * 128:(sl + 1) * 128],
                           scm_bf[:, sl * C:(sl + 1) * C], start=True, stop=False)
                        mm(pb_o[:, sl * C:(sl + 1) * C], S_bf[:, sl * 128:(sl + 1) * 128],
                           qt_bf[:, sl * C:(sl + 1) * C], start=False, stop=True,
                           inc=(1 if sl == 7 else 0))
                    bank_write_end("o")

                bank_write_begin("st")
                TQ.waits(t_khT, t_vev)
                for sl in range(8):
                    mm(pb_st[:, sl * 128:(sl + 1) * 128], khT_bf[:, sl * 128:(sl + 1) * 128],
                       v_bf[:, sl * 128:(sl + 1) * 128], start=True, stop=True,
                       inc=(1 if sl == 7 else 0))
                bank_write_end("st")
                pch["st"] = TQ.tok()

                if not is_halo:
                    oview = o_bf[:].rearrange("p (s t) -> p s t", s=8)[:, :, c0:c0 + C]
                    bank_read("o", AQ)
                    act(AQ, sqst[:], pb_o[:], AF.Square)
                    act(AQ, oview, pb_o[:].rearrange("p (s t) -> p s t", s=8), AF.Identity)
                    bank_read_done("o", AQ)
                    bank_write_begin("row")
                    TQ.wait(AQ.tok())
                    for sl in range(8):
                        mm(pb_row[:, c0:c0 + C], ones_col[:], sqst[:, sl * C:(sl + 1) * C],
                           start=(sl == 0), stop=(sl == 7), inc=(1 if sl == 7 else 0))
                    bank_write_end("row")
                    AQ.wait(TQ.tok())

                bank_read("st", VQ)
                for sl in range(8):
                    stt(VQ, S_f[:, sl * 128:(sl + 1) * 128], S_f[:, sl * 128:(sl + 1) * 128],
                        dcol[:, sl:sl + 1], pb_st[:, sl * 128:(sl + 1) * 128],
                        ALU.mult, ALU.add)
                t_sup = VQ.tok()
                bank_read_done("st", VQ)
                AQ.wait(t_sup)
                t_sbf = cp(AQ, S_bf[:], S_f[:])
                TQ.wait(t_sbf)
                pch["kh"] = t_kh

            if not is_halo:
                bank_read("row", VQ)
                cp(VQ, rows[:, 1024:1536], pb_row[:, 0:512])
                bank_read_done("row", VQ)
                t_ms = ts(VQ, rows[:, 1024:1536], rows[:, 1024:1536], 1.0 / D, None, ALU.mult)
                AQ.wait(t_ms)
                t_c = rrow(rows[:, 1024:1536], rows[:, 1024:1536], 1)
                VQ.wait(t_c)

                TQ.waits(w_tok["wo"], AQ.tok())
                for dc in range(IC):
                    bk = "mm%d" % (dc % 2)
                    pb = pb_mm[dc % 2]
                    bank_write_begin(bk)
                    for kc in range(IC):
                        mm(pb[:], wbuf[1][:, kc * 1024 + dc * 128: kc * 1024 + (dc + 1) * 128],
                           o_bf[:, kc * 512:(kc + 1) * 512],
                           start=(kc == 0), stop=(kc == IC - 1), inc=(1 if kc == IC - 1 else 0))
                    bank_write_end(bk)
                    bank_read(bk, AQ)
                    act(AQ, Xf[:, dc * 512:(dc + 1) * 512], pb[:], AF.Identity)
                    bank_read_done(bk, AQ)
                t_X = AQ.tok()

                VQ.wait(t_X)
                row_stats(lambda i: Xf[:, i * 512:(i + 1) * 512], t_X)
                ts(VQ, rows[:, 0:512], rows[:, 0:512], 1.0 / D, None, ALU.mult)
                ts(VQ, rows[:, 512:1024], rows[:, 512:1024], 1.0 / D, None, ALU.mult)
                tt(VQ, rows[:, 1536:2048], rows[:, 0:512], rows[:, 0:512], ALU.mult)
                tt(VQ, rows[:, 512:1024], rows[:, 512:1024], rows[:, 1536:2048], ALU.subtract)
                tt(VQ, rows[:, 1536:2048], rows[:, 1024:1536], rows[:, 1024:1536], ALU.mult)
                t_cv = tt(VQ, rows[:, 512:1024], rows[:, 512:1024], rows[:, 1536:2048], ALU.mult)
                AQ.wait(t_cv)
                t_r3 = rrow(rows[:, 512:1024], rows[:, 512:1024], 0)
                VQ.wait(t_r3)
                tt(VQ, rows[:, 1536:2048], rows[:, 1024:1536], rows[:, 512:1024], ALU.mult)
                tt(VQ, rows[:, 0:512], rows[:, 0:512], rows[:, 1536:2048], ALU.mult)
                ts(VQ, rows[:, 0:512], rows[:, 0:512], -1.0, None, ALU.mult)
                bcast_row(rows[:, 1536:2048], ab, VQ.tok())
                bcast_row(rows[:, 0:512], bb, VQ.tok())

                GQ.wait(VQ.tok())
                for dc in range(IC):
                    t_mu = tt(VQ, tst, Xf[:, dc * 512:(dc + 1) * 512], ab, ALU.mult)
                    GQ.wait(t_mu)
                    tt(GQ, tst2, tst, bb, ALU.add)
                    tt(GQ, y1f[:, dc * 512:(dc + 1) * 512], tst2,
                       x_cur[:, dc * 512:(dc + 1) * 512], ALU.add)
                    VQ.wait(GQ.tok())
                t_y1 = GQ.tok()
                SQ.wait(t_y1)
                dma(SQ, bass.AP(y1_d, rtok0, [[NTOK, 128], [128 * NTOK, IC], [1, MAC]]),
                    y1f[:].rearrange("p (a b) -> p a b", a=IC))

                VQ.wait(t_y1)
                AQ.wait(t_y1)
                row_stats(lambda i: y1f[:, i * 512:(i + 1) * 512], t_y1)
                ts(VQ, rows[:, 0:512], rows[:, 0:512], 1.0 / D, None, ALU.mult)
                ts(VQ, rows[:, 512:1024], rows[:, 512:1024], 1.0 / D, None, ALU.mult)
                tt(VQ, rows[:, 1536:2048], rows[:, 0:512], rows[:, 0:512], ALU.mult)
                t_v2 = tt(VQ, rows[:, 512:1024], rows[:, 512:1024], rows[:, 1536:2048],
                          ALU.subtract)
                AQ.wait(t_v2)
                t_r2 = rrow(rows[:, 512:1024], rows[:, 512:1024], 0)
                VQ.wait(t_r2)
                mi = m - 1
                SQ.wait(VQ.tok())
                dma(SQ, bass.AP(ln2_d, mi * MAC, [[1, 1], [1, MAC]]), rows[0:1, 0:512])
                dma(SQ, bass.AP(ln2_d, NTOK + mi * MAC, [[1, 1], [1, MAC]]),
                    rows[0:1, 512:1024])
                bcast_row(rows[:, 0:512], mb, VQ.tok())
                bcast_row(rows[:, 512:1024], rb, VQ.tok())
                GQ.wait(VQ.tok())
                for dc in range(IC):
                    tt(GQ, tst, y1f[:, dc * 512:(dc + 1) * 512], mb, ALU.subtract)
                    t_x2 = tt(GQ, tst2, tst, rb, ALU.mult)
                    SQ.wait(t_x2)
                    t_st2 = dma(SQ, bass.AP(xln2_d, rtok0 + dc * 128 * NTOK,
                                            [[NTOK, 128], [1, MAC]]), tst2)
                    VQ.wait(t_x2)
                    red(VQ, maxst[:, (m - 1) * 8 + dc:(m - 1) * 8 + dc + 1], tst2,
                        ALU.max, absv=True)
                    GQ.waits(VQ.tok(), t_st2)

            if DEBUG_DUMP and m == 1:
                barrier_all()
                cp(VQ, tst, S_f[:, 0:512])
                cp(VQ, tst2, S_f[:, 512:1024])
                t_d1 = VQ.tok()
                SQ.wait(t_d1)
                dma(SQ, bass.AP(dbg_S.tensor, 0, [[1024, 128], [1, 512]]), tst)
                dma(SQ, bass.AP(dbg_S.tensor, 512, [[1024, 128], [1, 512]]), tst2)
                VQ.wait(SQ.tok())
                cp(VQ, tst, pb_st[:, 0:512])
                cp(VQ, tst2, pb_st[:, 512:1024])
                t_d1b = VQ.tok()
                SQ.wait(t_d1b)
                dma(SQ, bass.AP(dbg_pst.tensor, 0, [[1024, 128], [1, 512]]), tst)
                dma(SQ, bass.AP(dbg_pst.tensor, 512, [[1024, 128], [1, 512]]), tst2)
                cp(VQ, SCR[:, 0:512], qt_bf[:])
                cp(VQ, SCR[:, 512:1024], kh_bf[:])
                cp(VQ, SCR[:, 1024:1536].rearrange("p (a b) -> p a b", a=8)[0:64],
                   scm_bf[:].rearrange("p (a b) -> p a b", a=8))
                t_d2 = VQ.tok()
                SQ.wait(t_d2)
                dma(SQ, dbg_qt, SCR[:, 0:512])
                dma(SQ, dbg_kh, SCR[:, 512:1024])
                dma(SQ, dbg_dc, dcol[:])
                dma(SQ, bass.AP(dbg_scm.tensor, 0, [[512, 64], [1, 512]]), SCR[0:64, 1024:1536])
                for dcb in range(IC):
                    cp(VQ, tst, o_bf[:, dcb * 512:(dcb + 1) * 512])
                    t_d3 = VQ.tok()
                    SQ.wait(t_d3)
                    dma(SQ, bass.AP(dbg_o.tensor, dcb * 512, [[4096, 128], [1, 512]]), tst)
                    VQ.wait(SQ.tok())
            barrier_all()

        def finalize_scalar(run_ap, ncols, bar, post_scale, result_cols):
            bank_write_begin("sc")
            TQ.waits(VQ.tok(), t_eye32)
            TQ.run(lambda e, o=pb_sc[0:ncols, 0:128], i=run_ap, d=eye32[:]:
                   e.transpose(o, i, d), inc=1)
            bank_write_end("sc")
            bank_read("sc", VQ)
            t_red = red(VQ, scal2[0:ncols, 0:1], pb_sc[0:ncols, 0:128], ALU.max)
            bank_read_done("sc", VQ)
            SQ.wait(t_red)
            t_bi = dma(SQ, bass.AP(bar_in[bar], 0, [[1, 1], [1, ncols]]), scal2[0:ncols, 0:1])
            GQ.wait(t_bi)
            cc_cnt[0] += 1
            ccv = cc_cnt[0]
            GQ.ops.append(lambda e, i=bar_in[bar], o=bar_out[bar], s=ccs:
                          e.collective_compute("AllGather", ALU.bypass,
                                               replica_groups=[list(range(NCORES))],
                                               ins=[i.ap().opt()], outs=[o.ap().opt()])
                          .then_inc(s, 1))
            GQ.wait((ccs, ccv))
            t_bo = dma(GQ, scal[0:1, 32:64], bass.AP(bar_out[bar], 0, [[1, 1], [1, 32]]))
            VQ.wait(t_bo)
            for j, rc in enumerate(result_cols):
                red(VQ, scal[0:1, j:j + 1],
                    scal[0:1, 32:64].rearrange("p (a b) -> p a b", a=8)[:, :, j:j + 1].opt(),
                    ALU.max)
                vfence()
                ts(VQ, scal[0:1, j:j + 1], scal[0:1, j:j + 1], post_scale, 1e-12,
                   ALU.mult, ALU.max)
                vfence()
                recip(VQ, scal[0:1, 16 + j:17 + j], scal[0:1, j:j + 1])
                vfence()
                for src, dstc in ((scal[0:1, j:j + 1], rc), (scal[0:1, 16 + j:17 + j], rc + 1)):
                    bank_write_begin("sc")
                    TQ.wait(VQ.tok())
                    mm(pb_sc[:, 0:1], ones_row[:], src, True, True, inc=1)
                    bank_write_end("sc")
                    bank_read("sc", VQ)
                    cp(VQ, scol[:, dstc:dstc + 1], pb_sc[:, 0:1])
                    bank_read_done("sc", VQ)

        vfence()
        red(VQ, runmax[:], maxst[:], ALU.max)
        vfence()
        finalize_scalar(runmax[:], 1, "A", 1.0 / 32767.0, [0])

        upW = ARENA[:].bitcast(BF16)[:, 0:24576]
        gateW = ARENA[:].bitcast(BF16)[:, 24576:49152]
        GQ.wait(t_gather)
        SQ.wait(GQ.tok())
        wload = []
        for icx in range(IC):
            wload.append(dma(SQ, upW[:, icx * H:(icx + 1) * H],
                             bass.AP(ex_out, icx * NEX, [[H, 128], [1, H]])))
            wload.append(dma(SQ, gateW[:, icx * H:(icx + 1) * H],
                             bass.AP(ex_out, icx * NEX + EX_UP, [[H, 128], [1, H]])))
        t_wl = SQ.tok()
        TQ.wait(t_wl)
        xln2f = SCR[:]
        i32v = wbuf[0][:].bitcast(I32)
        codebf = xln_bf[:]
        zstage = wbuf[1][:].bitcast(F32)
        memset(VQ, dcol[:], 0.0)
        for mi in range(NRMAC):
            rt0 = mi * MAC
            t_x2l = dma(SQ, xln2f.rearrange("p (a b) -> p a b", a=IC),
                        bass.AP(xln2_d, rt0, [[NTOK, 128], [128 * NTOK, IC], [1, MAC]]))
            VQ.wait(t_x2l)
            ts(VQ, i32v[:, 0:4096], xln2f, scol[:, 1:2], None, ALU.mult)
            t_code = cp(VQ, codebf, i32v[:, 0:4096])
            TQ.wait(t_code)
            for tix, (Wv_, zd, runc) in enumerate(((upW, zup_d, 0), (gateW, zgate_d, 1))):
                for g in range(4):
                    for oc6 in range(6):
                        oc = g * 6 + oc6
                        bk = "mm%d" % (oc % 2)
                        pb = pb_mm[oc % 2]
                        bank_write_begin(bk)
                        for icx in range(IC):
                            mm(pb[:], Wv_[:, icx * H + oc * 128: icx * H + (oc + 1) * 128],
                               codebf[:, icx * 512:(icx + 1) * 512],
                               start=(icx == 0), stop=(icx == IC - 1),
                               inc=(1 if icx == IC - 1 else 0))
                        bank_write_end(bk)
                        bank_read(bk, AQ)
                        act(AQ, zstage[:, oc6 * 512:(oc6 + 1) * 512], pb[:],
                            AF.Identity, scale=scol[:, 0:1])
                        bank_read_done(bk, AQ)
                    t_ev = AQ.tok()
                    VQ.wait(t_ev)
                    red(VQ, maxst[:, tix * 32 + mi * 4 + g: tix * 32 + mi * 4 + g + 1],
                        zstage[:, 0:3072], ALU.max, absv=True)
                    SQ.wait(t_ev)
                    t_zst = dma(SQ, bass.AP(zd, g * 6 * 128 * NTOK + rt0,
                                            [[NTOK, 128], [128 * NTOK, 6], [1, MAC]]),
                                zstage[:, 0:3072].rearrange("p (a b) -> p a b", a=6))
                    AQ.wait(t_zst)
                    AQ.wait(VQ.tok())
            SQ.wait(VQ.tok())
            SQ.wait(TQ.tok())
            VQ.wait(TQ.tok())
        barrier_all()

        vfence()
        red(VQ, dcol[:, 0:1], maxst[:, 0:32], ALU.max)
        red(VQ, dcol[:, 1:2], maxst[:, 32:64], ALU.max)
        vfence()
        finalize_scalar(dcol[:, 0:2], 2, "B", 1.0 / 32767.0, [2, 4])

        zgv = wbuf[1][:].bitcast(F32)
        gq_f = SCR[:, 0:2048]
        fs3 = SCR[:, 2048:4096]
        GW = 4
        i32g = wbuf[0][:].bitcast(I32)
        for mi in range(NRMAC):
            rt0 = mi * MAC
            for g in range(6):
                t_zl = dma(SQ, zgv[:, 0:GW * 512].rearrange("p (a b) -> p a b", a=GW),
                           bass.AP(zgate_d, g * GW * 128 * NTOK + rt0,
                                   [[NTOK, 128], [128 * NTOK, GW], [1, MAC]]))
                VQ.wait(t_zl)
                t_ci = ts(VQ, i32g[:, 0:GW * 512], zgv[:, 0:GW * 512], scol[:, 5:6], None, ALU.mult)
                GQ.wait(t_ci)
                t_gst = dma(GQ, bass.AP(gcode_d, g * GW * 128 * NTOK + rt0,
                                        [[NTOK, 128], [128 * NTOK, GW], [1, MAC]]),
                            i32g[:, 0:GW * 512].rearrange("p (a b) -> p a b", a=GW))
                t_gq = ts(VQ, gq_f, i32g[:, 0:GW * 512], scol[:, 4:5], None, ALU.mult)
                AQ.wait(t_gq)
                t_sg = act(AQ, fs3, gq_f, AF.Sigmoid)
                VQ.wait(t_sg)
                VQ.wait(t_gst)
                ts(VQ, i32g[:, 0:GW * 512], fs3, 65536.0, 65535.0, ALU.mult, ALU.min)
                cp(VQ, fs3, i32g[:, 0:GW * 512])
                t_g2 = tt(VQ, gq_f, gq_f, fs3, ALU.mult)
                red(VQ, maxst[:, mi * 6 + g: mi * 6 + g + 1], gq_f, ALU.max, absv=True)
                AQ.wait(VQ.tok())
                SQ.wait(VQ.tok())
        barrier_all()

        vfence()
        red(VQ, dcol[:, 2:3], maxst[:, 0:48], ALU.max)
        vfence()
        finalize_scalar(dcol[:, 2:3], 1, "C", (1.0 / 65536.0) / 32767.0, [6])
        vfence()
        ts(VQ, scol[:, 10:11], scol[:, 7:8], 1.0 / 65536.0, None, ALU.mult)
        vfence()

        for mi in range(NRMAC):
            rt0 = mi * MAC
            for g in range(6):
                base = g * GW * 128 * NTOK + rt0
                apz = [[NTOK, 128], [128 * NTOK, GW], [1, MAC]]
                t_gl = dma(GQ, gq_f.rearrange("p (a b) -> p a b", a=GW),
                           bass.AP(gcode_d, base, apz))
                t_ul = dma(SQ, zgv[:, 0:GW * 512].rearrange("p (a b) -> p a b", a=GW),
                           bass.AP(zup_d, base, apz))
                VQ.wait(t_gl)
                ts(VQ, gq_f, gq_f, scol[:, 4:5], None, ALU.mult)
                AQ.wait(VQ.tok())
                t_sg = act(AQ, fs3, gq_f, AF.Sigmoid)
                VQ.wait(t_sg)
                ts(VQ, i32g[:, 0:GW * 512], fs3, 65536.0, 65535.0, ALU.mult, ALU.min)
                cp(VQ, fs3, i32g[:, 0:GW * 512])
                tt(VQ, gq_f, gq_f, fs3, ALU.mult)
                t_g2c = ts(VQ, i32g[:, 0:GW * 512], gq_f, scol[:, 10:11], None, ALU.mult)
                GQ.wait(t_g2c)
                t_g2st = dma(GQ, bass.AP(g2code_d, base, apz),
                             i32g[:, 0:GW * 512].rearrange("p (a b) -> p a b", a=GW))
                ts(VQ, fs3, i32g[:, 0:GW * 512], scol[:, 6:7], None, ALU.mult)
                VQ.wait(t_ul)
                VQ.wait(t_g2st)
                t_uc = ts(VQ, i32g[:, 0:GW * 512], zgv[:, 0:GW * 512], scol[:, 3:4], None, ALU.mult)
                GQ.wait(t_uc)
                t_ust = dma(GQ, bass.AP(ucode_d, base, apz),
                            i32g[:, 0:GW * 512].rearrange("p (a b) -> p a b", a=GW))
                ts(VQ, zgv[:, 0:GW * 512], i32g[:, 0:GW * 512], scol[:, 2:3], None, ALU.mult)
                t_op = tt(VQ, gq_f, fs3, zgv[:, 0:GW * 512], ALU.mult)
                red(VQ, maxst[:, mi * 6 + g: mi * 6 + g + 1], gq_f, ALU.max, absv=True)
                VQ.wait(t_ust)
                AQ.wait(VQ.tok())
                GQ.wait(VQ.tok())
                SQ.wait(VQ.tok())
        barrier_all()

        vfence()
        red(VQ, dcol[:, 3:4], maxst[:, 0:32], ALU.max)
        vfence()
        finalize_scalar(dcol[:, 3:4], 1, "D", 1.0 / 32767.0, [8])

        downW = ARENA[:].bitcast(BF16)[:, 0:24576]
        GQ.wait(SQ.tok())
        dwl = []
        for icx in range(NCORES):
            for icl in range(3):
                icg = icx * 3 + icl
                dwl.append(dma(SQ, downW[:, icg * D:(icg + 1) * D],
                               bass.AP(ex_out, icx * NEX + EX_UP + EX_GATE + icl * 128 * D,
                                       [[D, 128], [1, D]])))
        t_dwl = SQ.tok()
        TQ.wait(t_dwl)
        Xs = ARENA[:, 12288:16384]
        y1s = ARENA[:, 16384:20480]
        y2s = ARENA[:, 20480:24576]

        def codebf_sl(ch):
            if ch < 16:
                return wbuf[1][:, ch * 512:(ch + 1) * 512]
            return o_bf[:, (ch - 16) * 512:(ch - 15) * 512]

        for mi in range(NRMAC):
            rt0 = mi * MAC
            t_y1l = dma(SQ, y1s.rearrange("p (a b) -> p a b", a=IC),
                        bass.AP(y1_d, rt0, [[NTOK, 128], [128 * NTOK, IC], [1, MAC]]))
            for g in range(6):
                base = g * GW * 128 * NTOK + rt0
                apz = [[NTOK, 128], [128 * NTOK, GW], [1, MAC]]
                t_g2l = dma(GQ, fs3.rearrange("p (a b) -> p a b", a=GW),
                            bass.AP(g2code_d, base, apz))
                t_ucl = dma(GQ, gq_f.rearrange("p (a b) -> p a b", a=GW),
                            bass.AP(ucode_d, base, apz))
                VQ.waits(t_g2l, t_ucl)
                ts(VQ, fs3, fs3, scol[:, 6:7], None, ALU.mult)
                ts(VQ, gq_f, gq_f, scol[:, 2:3], None, ALU.mult)
                tt(VQ, fs3, fs3, gq_f, ALU.mult)
                t_oc = ts(VQ, i32g[:, 0:GW * 512], fs3, scol[:, 9:10], None, ALU.mult)
                for oc6 in range(GW):
                    ch = g * GW + oc6
                    cp(VQ, codebf_sl(ch), i32g[:, oc6 * 512:(oc6 + 1) * 512])
                GQ.wait(VQ.tok())
            t_codes = VQ.tok()
            TQ.waits(t_codes, t_dwl)
            for dc in range(IC):
                bk = "mm%d" % (dc % 2)
                pb = pb_mm[dc % 2]
                bank_write_begin(bk)
                for ch in range(24):
                    mm(pb[:], downW[:, ch * D + dc * 128: ch * D + (dc + 1) * 128],
                       codebf_sl(ch), start=(ch == 0), stop=(ch == 23),
                       inc=(1 if ch == 23 else 0))
                bank_write_end(bk)
                bank_read(bk, AQ)
                act(AQ, Xs[:, dc * 512:(dc + 1) * 512], pb[:], AF.Identity,
                    scale=scol[:, 8:9])
                bank_read_done(bk, AQ)
            t_X4 = AQ.tok()
            VQ.wait(t_X4)
            row_stats(lambda i: Xs[:, i * 512:(i + 1) * 512], t_X4)
            ts(VQ, rows[:, 0:512], rows[:, 0:512], 1.0 / D, None, ALU.mult)
            ts(VQ, rows[:, 512:1024], rows[:, 512:1024], 1.0 / D, None, ALU.mult)
            tt(VQ, rows[:, 1024:1536], rows[:, 0:512], rows[:, 0:512], ALU.mult)
            t_v4 = tt(VQ, rows[:, 512:1024], rows[:, 512:1024], rows[:, 1024:1536],
                      ALU.subtract)
            AQ.wait(t_v4)
            t_r4 = rrow(rows[:, 512:1024], rows[:, 512:1024], 0)
            VQ.wait(t_r4)
            tt(VQ, rows[:, 0:512], rows[:, 0:512], rows[:, 512:1024], ALU.mult)
            ts(VQ, rows[:, 0:512], rows[:, 0:512], -1.0, None, ALU.mult)
            bcast_row(rows[:, 512:1024], ab, VQ.tok())
            bcast_row(rows[:, 0:512], bb, VQ.tok())
            VQ.wait(t_y1l)
            GQ.wait(VQ.tok())
            for dc in range(IC):
                t_m4 = tt(VQ, tst, Xs[:, dc * 512:(dc + 1) * 512], ab, ALU.mult)
                GQ.wait(t_m4)
                tt(GQ, tst2, tst, bb, ALU.add)
                t_y2 = tt(GQ, y2s[:, dc * 512:(dc + 1) * 512], tst2,
                          y1s[:, dc * 512:(dc + 1) * 512], ALU.add)
                VQ.wait(GQ.tok())
            t_y2d = GQ.tok()
            SQ.wait(t_y2d)
            dma(SQ, bass.AP(outT.tensor, rt0, [[NTOK, 128], [128 * NTOK, IC], [1, MAC]]),
                y2s.rearrange("p (a b) -> p a b", a=IC))
            GQ.wait(SQ.tok())
            VQ.wait(SQ.tok())
        SQ.wait(SQ.tok())
        GQ.wait(GQ.tok())

        with nc.Block() as block:
            @block.sync
            def _(q):
                for f_ in SQ.ops:
                    f_(q)

            @block.gpsimd
            def _(q):
                for f_ in GQ.ops:
                    f_(q)

            @block.tensor
            def _(q):
                for f_ in TQ.ops:
                    f_(q)

            @block.vector
            def _(q):
                for f_ in VQ.ops:
                    f_(q)

            @block.scalar
            def _(q):
                for f_ in AQ.ops:
                    f_(q)

    return nc


def _stage_inputs(inputs):
    x = np.asarray(inputs["L0_attn_x"])
    xTs = [np.ascontiguousarray(x[s].T) for s in range(2)]
    eye = np.eye(128, dtype=np.float32)
    mask = np.zeros((C, C), dtype=np.uint8)
    for s_ in range(C):
        mask[s_, s_:] = 1
    nw = np.ascontiguousarray(np.asarray(inputs["L0_attn_norm_w"]).reshape(IC, 128).T)
    nw = np.ascontiguousarray(nw.astype(np.float32))
    w_up = np.ascontiguousarray(np.asarray(inputs["L0_ffn_w_up"], dtype=np.float32))
    w_gate = np.ascontiguousarray(np.asarray(inputs["L0_ffn_w_gate"], dtype=np.float32))
    w_down = np.ascontiguousarray(np.asarray(inputs["L0_ffn_w_down"], dtype=np.float32))
    base = dict(
        wqT=np.ascontiguousarray(np.asarray(inputs["L0_attn_Wq"], dtype=np.float32).T),
        wkT=np.ascontiguousarray(np.asarray(inputs["L0_attn_Wk"], dtype=np.float32).T),
        wvT=np.ascontiguousarray(np.asarray(inputs["L0_attn_Wv"], dtype=np.float32).T),
        woT=np.ascontiguousarray(np.asarray(inputs["L0_attn_Wo"], dtype=np.float32).T),
        nwfm=nw, eye_h=eye, mask_h=mask,
    )
    maps = []
    for core in range(NCORES):
        seq, seg = core // 4, core % 4
        t0 = seg * NTOK
        xs = np.zeros((D, TOT), dtype=np.float32)
        lo = t0 - HALO
        if lo < 0:
            xs[:, HALO:] = xTs[seq][:, t0:t0 + NTOK]
        else:
            xs[:, :] = xTs[seq][:, lo:t0 + NTOK]
        sl = []
        for j in range(OC3):
            sl.append(w_up[j * 128:(j + 1) * 128, core * 128:(core + 1) * 128])
        for j in range(OC3):
            sl.append(w_gate[j * 128:(j + 1) * 128, core * 128:(core + 1) * 128])
        for icl in range(3):
            icg = core * 3 + icl
            for hc in range(IC):
                sl.append(w_down[hc * 128:(hc + 1) * 128, icg * 128:(icg + 1) * 128])
        m = dict(base)
        m["xT"] = np.ascontiguousarray(xs)
        m["dq_sl"] = np.ascontiguousarray(np.concatenate(sl, axis=0))
        maps.append(m)
    return maps


_NC_CACHE = None


def kernel(L0_attn_x, L0_attn_Wq, L0_attn_Wk, L0_attn_Wv, L0_attn_Wo,
           L0_attn_norm_w, L0_ffn_w_gate, L0_ffn_w_up, L0_ffn_w_down):
    global _NC_CACHE
    inputs = dict(L0_attn_x=L0_attn_x, L0_attn_Wq=L0_attn_Wq, L0_attn_Wk=L0_attn_Wk,
                  L0_attn_Wv=L0_attn_Wv, L0_attn_Wo=L0_attn_Wo,
                  L0_attn_norm_w=L0_attn_norm_w, L0_ffn_w_gate=L0_ffn_w_gate,
                  L0_ffn_w_up=L0_ffn_w_up, L0_ffn_w_down=L0_ffn_w_down)
    if _NC_CACHE is None:
        _NC_CACHE = build_program()
    nc = _NC_CACHE
    maps = _stage_inputs(inputs)
    res = run_bass_kernel_spmd(nc, maps, core_ids=list(range(NCORES)))
    out = np.empty((2, 16384, D), dtype=np.float32)
    for core in range(NCORES):
        seq, seg = core // 4, core % 4
        o = np.asarray(res.results[core]["outT"])
        out[seq, seg * NTOK:(seg + 1) * NTOK, :] = o.T
    return out


# version 08 (final)
# speedup vs baseline: 1.0810x; 1.0501x over previous
import numpy as np
import ml_dtypes
from contextlib import ExitStack

import concourse.bass as bass
import concourse.mybir as mybir
from concourse.bass_utils import run_bass_kernel_spmd

F32 = mybir.dt.float32
BF16 = mybir.dt.bfloat16
I32 = mybir.dt.int32
I16 = mybir.dt.int16
AF = mybir.ActivationFunctionType
ALU = mybir.AluOpType
AX = mybir.AxisListType

D = 1024
H = 3072
NTOK = 4096
HALO = 512
TOT = NTOK + HALO
MAC = 512
NMAC = TOT // MAC
NRMAC = NMAC - 1
C = 64
NCH = MAC // C
IC = D // 128
OC3 = H // 128
NCORES = 8

EX_UP = 128 * H
EX_GATE = 128 * H
EX_DOWN = 3 * 128 * D
NEX = EX_UP + EX_GATE + EX_DOWN

STUB_F = False
DEBUG_DUMP = False
DEBUG_F = False


class Q:
    def __init__(self, sem):
        self.ops = []
        self.sem = sem
        self.cnt = 0

    def wait(self, tok):
        if tok is None:
            return
        sem, val = tok
        self.ops.append(lambda e, s=sem, v=val: e.wait_ge(s, v))

    def waits(self, *toks):
        for t in toks:
            self.wait(t)

    def run(self, fn, inc=1):
        self.cnt += inc
        tokv = self.cnt

        def cl(e, fn=fn, inc=inc, s=self.sem):
            ins = fn(e)
            if inc:
                ins.then_inc(s, inc)
        self.ops.append(cl)
        return (self.sem, tokv)

    def tok(self):
        return (self.sem, self.cnt)


def build_program():
    nc = bass.Bass("TRN2", target_bir_lowering=False)

    def din(name, shape, dtype=F32):
        return nc.dram_tensor(name, shape, dtype, kind="ExternalInput").ap()

    xT = din("xT", [D, TOT])
    wqT = din("wqT", [D, D])
    wkT = din("wkT", [D, D])
    wvT = din("wvT", [D, D])
    woT = din("woT", [D, D])
    nwfm = din("nwfm", [128, IC])
    eye_h = din("eye_h", [128, 128])
    mask_h = din("mask_h", [C, C], mybir.dt.uint8)
    dq_sl = din("dq_sl", [3 * OC3 * 128, 128])

    outT = nc.dram_tensor("outT", [D, NTOK], F32, kind="ExternalOutput").ap()
    if DEBUG_F:
        dbg_scol = nc.dram_tensor("dbg_scol", [128, 12], F32, kind="ExternalOutput").ap()
        dbg_z = nc.dram_tensor("dbg_z", [128, 2048], F32, kind="ExternalOutput").ap()
        dbg_w = nc.dram_tensor("dbg_w", [128, 3072], mybir.dt.bfloat16, kind="ExternalOutput").ap()
        dbg_mx = nc.dram_tensor("dbg_mx", [128, 96], F32, kind="ExternalOutput").ap()
        dbg_y1 = nc.dram_tensor("dbg_y1", [D, NTOK], F32, kind="ExternalOutput").ap()
        dbg_zu = nc.dram_tensor("dbg_zu", [H, 1024], F32, kind="ExternalOutput").ap()
        dbg_uc = nc.dram_tensor("dbg_uc", [H, 1024], F32, kind="ExternalOutput").ap()
        dbg_g2c = nc.dram_tensor("dbg_g2c", [H, 1024], F32, kind="ExternalOutput").ap()
        dbg_xl2 = nc.dram_tensor("dbg_xl2", [D, NTOK], F32, kind="ExternalOutput").ap()
    if DEBUG_DUMP:
        dbg_S = nc.dram_tensor("dbg_S", [128, 1024], F32, kind="ExternalOutput").ap()
        dbg_o = nc.dram_tensor("dbg_o", [128, 4096], F32, kind="ExternalOutput").ap()
        dbg_qt = nc.dram_tensor("dbg_qt", [128, 512], F32, kind="ExternalOutput").ap()
        dbg_kh = nc.dram_tensor("dbg_kh", [128, 512], F32, kind="ExternalOutput").ap()
        dbg_dc = nc.dram_tensor("dbg_dc", [128, 8], F32, kind="ExternalOutput").ap()
        dbg_scm = nc.dram_tensor("dbg_scm", [64, 512], F32, kind="ExternalOutput").ap()
        dbg_pst = nc.dram_tensor("dbg_pst", [128, 1024], F32, kind="ExternalOutput").ap()

    y1_d = nc.dram_tensor("y1_d", [D, NTOK], F32)
    xln2_d = nc.dram_tensor("xln2_d", [D, NTOK], F32)
    zup_d = nc.dram_tensor("zup_d", [H, NTOK], F32)
    zgate_d = nc.dram_tensor("zgate_d", [H, NTOK], F32)
    gcode_d = nc.dram_tensor("gcode_d", [H, NTOK], I16)
    g2code_d = nc.dram_tensor("g2code_d", [H, NTOK], I16)
    ucode_d = nc.dram_tensor("ucode_d", [H, NTOK], I16)
    ln2_d = nc.dram_tensor("ln2_d", [2, NTOK], F32)
    wq_bf_d = nc.dram_tensor("wq_bf_d", [D, D], BF16)
    wk_bf_d = nc.dram_tensor("wk_bf_d", [D, D], BF16)
    wv_bf_d = nc.dram_tensor("wv_bf_d", [D, D], BF16)
    wo_bf_d = nc.dram_tensor("wo_bf_d", [D, D], BF16)
    ex_in = nc.dram_tensor("ex_in", [1, NEX], BF16)
    ex_out = nc.dram_tensor("ex_out", [NCORES, NEX], BF16, addr_space="Shared")
    barsc = nc.dram_tensor("barsc", [4, 128], F32)
    bar_in = {}
    bar_out = {}
    for b in ("A", "B", "C", "D"):
        bar_in[b] = nc.dram_tensor(f"bar{b}_in", [1, 4], F32)
        bar_out[b] = nc.dram_tensor(f"bar{b}_out", [NCORES, 4], F32, addr_space="Shared")

    with ExitStack() as st:
        E = st.enter_context

        def sbt(name, shape, dtype=F32):
            return E(nc.sbuf_tensor(name, shape, dtype))

        ARENA = sbt("ARENA", [128, 24576], F32)
        SCR = sbt("SCR", [128, 4096], F32)
        ar = ARENA[:]
        xb = [ar[:, 0:4096], ar[:, 4096:8192]]
        qf = ar[:, 8192:12288]
        ff = ar[:, 12288:16384]
        Xf = ar[:, 16384:20480]
        y1f = ar[:, 20480:24576]
        sc_ = SCR[:]
        gp1 = sc_[:, 0:512]
        Apr = sc_[:, 512:1024]
        Em = sc_[:, 1024:1536]
        Ep = sc_[:, 1536:2048]
        ktf = sc_[:, 2048:2560]
        tst = sc_[:, 2560:3072]
        tst2 = sc_[:, 3072:3584]
        gps = sc_[:, 3584:4096]

        wbuf = [sbt(f"wbuf{i}", [128, 8192], BF16) for i in range(2)]
        xln_bf = sbt("xln_bf", [128, 4096], BF16)
        o_bf = sbt("o_bf", [128, 4096], BF16)
        v_bf = sbt("v_bf", [C, D], BF16)
        qt_bfp = [sbt(f"qt_bf{i}", [128, 8 * C], BF16) for i in range(2)]
        kt_bfp = [sbt(f"kt_bf{i}", [128, 8 * C], BF16) for i in range(2)]
        kh_bf = sbt("kh_bf", [128, 8 * C], BF16)
        khT_bf = sbt("khT_bf", [C, 8 * 128], BF16)
        scm_bf = sbt("scm_bf", [C, 8 * C], BF16)
        S_f = sbt("S_f", [128, 8 * 128], F32)
        S_bf = sbt("S_bf", [128, 8 * 128], BF16)
        BC = sbt("BC", [128, 2048], F32)
        mb, rb = BC[:, 0:512], BC[:, 512:1024]
        ab, bb = BC[:, 1024:1536], BC[:, 1536:2048]
        sqst = sbt("sqst", [128, 512], F32)
        dcolp = [sbt(f"dcol{i}", [128, 8], F32) for i in range(2)]
        dcol = dcolp[0]
        cc8 = sbt("cc8", [128, 4], F32)
        eye_bf = sbt("eye_bf", [128, 128], BF16)
        mask_c = sbt("mask_c", [C, C], mybir.dt.uint8)
        zrs = sbt("zrs", [128, C], F32)
        nw_c = sbt("nw_c", [128, IC], F32)
        ones_col = sbt("ones_col", [128, 1], F32)
        ones_row = sbt("ones_row", [1, 128], F32)
        rows = sbt("rows", [1, 1536], F32)
        runmax = sbt("runmax", [128, 1], F32)
        scal = sbt("scal", [1, 64], F32)
        scol = sbt("scol", [128, 12], F32)
        dq_nat = SCR[:, 0:128]
        dq_s4 = SCR[:, 256:260]
        dq_r4 = SCR[:, 260:264]
        dq_t = SCR[:, 128:256]
        dq_bf = SCR[:].bitcast(BF16)[:, 528:656]
        dq_st = SCR[:].bitcast(BF16)[:, 656:784]
        maxst = sbt("maxst", [128, 96], F32)

        pb_mm = [E(nc.psum_tensor(f"pb_mm{i}", [128, 512], F32)) for i in range(2)]
        pb_o = E(nc.psum_tensor("pb_o", [128, 512], F32))
        pb_sc = E(nc.psum_tensor("pb_sc", [128, 512], F32))
        pb_tr = E(nc.psum_tensor("pb_tr", [128, 1024], BF16))
        pb_st = E(nc.psum_tensor("pb_st", [128, 1024], F32))
        pb_row = E(nc.psum_tensor("pb_row", [1, 512], F32))

        SQ = Q(E(nc.semaphore("sS")))
        GQ = Q(E(nc.semaphore("sG")))
        TQ = Q(E(nc.semaphore("sT")))
        VQ = Q(E(nc.semaphore("sV")))
        AQ = Q(E(nc.semaphore("sA")))
        ccs = E(nc.semaphore("ccs"))
        cc_cnt = [0]
        ALLQ = [SQ, GQ, TQ, VQ, AQ]

        def barrier_all():
            toks = [q.tok() for q in ALLQ]
            for q in ALLQ:
                for t in toks:
                    if t[0] is not q.sem:
                        q.wait(t)

        def mm(out_ap, lhsT_ap, rhs_ap, start, stop, inc=0):
            return TQ.run(lambda e, o=out_ap, l=lhsT_ap, r=rhs_ap, a=start, b=stop:
                          e.matmul(o, l, r, start=a, stop=b), inc=inc)

        def tr(out_ap, in_ap, inc=1):
            return TQ.run(lambda e, o=out_ap, i=in_ap, d=eye_bf[:]:
                          e.transpose(o, i, d), inc=inc)

        def act(q, out_ap, in_ap, func, bias=0.0, scale=1.0, accum=None, inc=1):
            return q.run(lambda e, o=out_ap, i=in_ap, f=func, b=bias, s=scale, a=accum:
                         e.activation(o, i, f, bias=b, scale=s, accum_out=a), inc=inc)

        def tt(q, out_ap, a_ap, b_ap, op, inc=1):
            return q.run(lambda e, o=out_ap, a=a_ap, b=b_ap, p=op:
                         e.tensor_tensor(o, a, b, p), inc=inc)

        def ts(q, out_ap, in_ap, s1, s2, op0, op1=ALU.bypass, inc=1):
            return q.run(lambda e, o=out_ap, i=in_ap, a=s1, b=s2, p0=op0, p1=op1:
                         e.tensor_scalar(o, i, a, b, p0, p1), inc=inc)

        def stt(q, out_ap, in0_ap, scalar, in1_ap, op0, op1, inc=1):
            return q.run(lambda e, o=out_ap, i=in0_ap, s=scalar, j=in1_ap, p0=op0, p1=op1:
                         e.scalar_tensor_tensor(o, i, s, j, p0, p1), inc=inc)

        def cp(q, out_ap, in_ap, inc=1):
            def fn(e, o=out_ap, i=in_ap):
                if hasattr(e, "tensor_copy"):
                    return e.tensor_copy(out=o, in_=i)
                return e.copy(o, i)
            return q.run(fn, inc=inc)

        def red(q, out_ap, in_ap, op, absv=False, inc=1):
            return q.run(lambda e, o=out_ap, i=in_ap, p=op, a=absv:
                         e.tensor_reduce(o, i, axis=AX.X, op=p, apply_absolute_value=a),
                         inc=inc)

        def scan_op(q, out_ap, d0, d1, init, op0, op1, inc=1):
            return q.run(lambda e, o=out_ap, a=d0, b=d1, i=init, p0=op0, p1=op1:
                         e.tensor_tensor_scan(o, a, b, i, p0, p1), inc=inc)

        def dma(q, out_ap, in_ap):
            return q.run(lambda e, o=out_ap, i=in_ap: e.dma_start(out=o, in_=i), inc=16)

        def memset(q, ap_, val, inc=1):
            return q.run(lambda e, a=ap_, v=val: e.memset(a, v), inc=inc)

        def select(q, out_ap, mask_ap, t_ap, f_ap, inc=1):
            return q.run(lambda e, o=out_ap, m=mask_ap, a=t_ap, b=f_ap:
                         e.select(o, m, a, b), inc=inc)

        def recip(q, out_ap, in_ap, inc=1):
            return q.run(lambda e, o=out_ap, i=in_ap: e.reciprocal(o, i), inc=inc)

        def vfence():
            t = VQ.tok()
            GQ.wait(t)
            t2 = GQ.run(lambda e: e.engine_nop(), inc=1)
            VQ.wait(t2)

        def allgather(in_t, out_t):
            cc_cnt[0] += 1
            val = cc_cnt[0]
            GQ.ops.append(lambda e, i=in_t, o=out_t, s=ccs:
                          e.collective_compute("AllGather", ALU.bypass,
                                               replica_groups=[list(range(NCORES))],
                                               ins=[i.ap().opt()], outs=[o.ap().opt()])
                          .then_inc(s, 1))
            return (ccs, val)

        def rrow(dst, src, eps_col):
            act(AQ, dst, src, AF.Ln, bias=cc8[0:1, eps_col:eps_col + 1])
            return act(AQ, dst, dst, AF.Exp, scale=cc8[0:1, 2:3])

        bank_prod = {}
        bank_cons = {}

        def bank_write_begin(bk):
            for t in bank_cons.get(bk, []):
                TQ.wait(t)
            bank_cons[bk] = []

        def bank_write_end(bk):
            bank_prod[bk] = TQ.tok()

        def bank_read(bk, q):
            q.wait(bank_prod.get(bk))

        def bank_read_done(bk, q):
            bank_cons.setdefault(bk, []).append(q.tok())

        def bcast_row(row_ap, dst_ap, src_tok):
            bank_write_begin("sc")
            TQ.wait(src_tok)
            mm(pb_sc[:, 0:512], ones_row[:], row_ap, True, True, inc=1)
            bank_write_end("sc")
            bank_read("sc", VQ)
            t = cp(VQ, dst_ap, pb_sc[:, 0:512])
            bank_read_done("sc", VQ)
            return t

        t_eye = dma(GQ, eye_bf[:], eye_h)
        t_mask = dma(SQ, mask_c[:], mask_h)
        t_nw = dma(SQ, nw_c[:], nwfm)
        memset(VQ, zrs[:], 0.0)
        memset(VQ, ones_col[:], 1.0)
        memset(VQ, ones_row[:], 1.0)
        memset(VQ, S_f[:], 0.0)
        memset(VQ, S_bf[:], 0.0)
        memset(VQ, runmax[:], 0.0)
        memset(VQ, cc8[:, 0:1], 1e-5)
        memset(VQ, cc8[:, 1:2], 1e-6)
        memset(VQ, cc8[:, 2:3], -0.5)

        t_wq = dma(GQ, wq_bf_d.ap(), wqT)
        t_wk = dma(GQ, wk_bf_d.ap(), wkT)
        t_wv = dma(GQ, wv_bf_d.ap(), wvT)
        t_wo = None
        for half in range(2):
            rsl = slice(half * 512, (half + 1) * 512)
            tl = dma(SQ, xb[0].rearrange("p (a b) -> p a b", a=IC),
                     woT.rearrange("(a p) c -> p a c", p=128)[:, :, rsl])
            VQ.waits(tl, t_nw)
            for a in range(IC):
                ts(VQ, xln_bf[:, a * 512:(a + 1) * 512],
                   xb[0][:, a * 512:(a + 1) * 512], nw_c[:, a:a + 1], None, ALU.mult)
            SQ.wait(VQ.tok())
            t_wo = dma(SQ, wo_bf_d.ap().rearrange("(a p) c -> p a c", p=128)[:, :, rsl],
                       xln_bf[:].rearrange("p (a b) -> p a b", a=IC))
            VQ.wait(t_wo)
        t_wcast = (t_wq, t_wk, t_wv, t_wo)

        GQ.wait(t_eye)
        TQ.wait(t_eye)
        tile_row = 0
        t_last_nat_read = None
        t_last_st = None
        for tgroup, base_off, ostride in (("up", 0, H), ("gate", EX_UP, H),
                                          ("down", EX_UP + EX_GATE, D)):
            for j in range(OC3):
                SQ.wait(t_last_nat_read)
                tl = dma(SQ, dq_nat[:], dq_sl[tile_row * 128:(tile_row + 1) * 128, :])
                tile_row += 1
                VQ.wait(tl)
                red(VQ, dq_s4[:], dq_nat[:].rearrange("p (w k) -> p w k", k=32),
                    ALU.max, absv=True)
                vfence()
                ts(VQ, dq_s4[:], dq_s4[:], 1.0 / 7.0, 1e-12, ALU.mult, ALU.max)
                vfence()
                recip(VQ, dq_r4[:], dq_s4[:])
                vfence()
                t_last_nat_read = tt(VQ, dq_t[:].rearrange("p (w k) -> p w k", k=32),
                                     dq_nat[:].rearrange("p (w k) -> p w k", k=32),
                                     dq_r4[:, :, None].broadcast_to([128, 4, 32]), ALU.mult)
                cp(VQ, dq_t[:].bitcast(I32), dq_t[:])
                cp(VQ, dq_t[:], dq_t[:].bitcast(I32))
                t_dq = tt(VQ, dq_bf[:].rearrange("p (w k) -> p w k", k=32),
                          dq_t[:].rearrange("p (w k) -> p w k", k=32),
                          dq_s4[:, :, None].broadcast_to([128, 4, 32]), ALU.mult)
                bank_write_begin("tr")
                TQ.wait(t_dq)
                t_tp = tr(pb_tr[:, 0:128], dq_bf[:])
                bank_write_end("tr")
                VQ.wait(t_tp)
                bank_read("tr", AQ)
                AQ.wait(t_last_st)
                t_ev = cp(AQ, dq_st[:], pb_tr[:, 0:128])
                bank_read_done("tr", AQ)
                SQ.wait(t_ev)
                if tgroup == "down":
                    icl, hc = j // IC, j % IC
                    off = base_off + icl * (128 * D) + hc * 128
                else:
                    off = base_off + j * 128
                t_last_st = SQ.run(
                    lambda e, o=bass.AP(ex_in, off, [[ostride, 128], [1, 128]]),
                    i=dq_st[:]: e.dma_start(out=o, in_=i), inc=16)
        GQ.wait(SQ.tok())
        t_gather = allgather(ex_in, ex_out)

        def load_x(m):
            tok0 = m * MAC
            return dma(SQ, xb[m % 2].rearrange("p (a b) -> p a b", a=IC),
                       bass.AP(xT.tensor, tok0, [[TOT, 128], [128 * TOT, IC], [1, MAC]]))

        def load_w(dram, slot, after_tok):
            SQ.wait(after_tok)
            return dma(SQ, wbuf[slot][:].rearrange("p (a b) -> p a b", a=IC),
                       dram.ap().rearrange("(a p) c -> p a c", p=128))

        def row_stats(src_fn, src_tok):
            bank_write_begin("row")
            TQ.wait(src_tok)
            for i in range(IC):
                mm(pb_row[:, 0:512], ones_col[:], src_fn(i),
                   start=(i == 0), stop=(i == IC - 1), inc=(1 if i == IC - 1 else 0))
            bank_write_end("row")
            bank_read("row", VQ)
            cp(VQ, rows[:, 0:512], pb_row[:, 0:512])
            bank_read_done("row", VQ)
            sq_mm_tok = [None, None]
            for i in range(IC):
                buf = sqst[:] if i % 2 == 0 else tst
                AQ.wait(sq_mm_tok[i % 2])
                t_sq = act(AQ, buf, src_fn(i), AF.Square)
                bank_write_begin("row")
                TQ.wait(t_sq)
                mm(pb_row[:, 0:512], ones_col[:], buf,
                   start=(i == 0), stop=(i == IC - 1), inc=1)
                bank_write_end("row")
                sq_mm_tok[i % 2] = TQ.tok()
            bank_read("row", VQ)
            cp(VQ, rows[:, 512:1024], pb_row[:, 0:512])
            bank_read_done("row", VQ)

        SQ.waits(*t_wcast)
        SQ.wait(GQ.tok())
        x_tok = [None, None]
        x_tok[0] = load_x(0)
        w_tok = {}

        for m in range(NMAC):
            is_halo = (m == 0)
            par = m % 2
            x_cur = xb[par]
            tok0 = m * MAC
            rtok0 = tok0 - HALO

            w_tok["wk"] = load_w(wk_bf_d, 0, None)
            w_tok["wq"] = load_w(wq_bf_d, 1, None)
            if m + 1 < NMAC:
                x_tok[(m + 1) % 2] = load_x(m + 1)

            VQ.wait(x_tok[par])
            AQ.wait(x_tok[par])

            row_stats(lambda i: x_cur[:, i * 512:(i + 1) * 512], x_tok[par])
            ts(VQ, rows[:, 0:512], rows[:, 0:512], 1.0 / D, None, ALU.mult)
            ts(VQ, rows[:, 512:1024], rows[:, 512:1024], 1.0 / D, None, ALU.mult)
            tt(VQ, rows[:, 1024:1536], rows[:, 0:512], rows[:, 0:512], ALU.mult)
            t_v = tt(VQ, rows[:, 512:1024], rows[:, 512:1024], rows[:, 1024:1536],
                     ALU.subtract)
            AQ.wait(t_v)
            t_r = rrow(rows[:, 512:1024], rows[:, 512:1024], 0)
            VQ.wait(t_r)
            bcast_row(rows[:, 0:512], mb, VQ.tok())
            t_bc = bcast_row(rows[:, 512:1024], rb, t_r)
            for i in range(IC):
                tt(VQ, tst, x_cur[:, i * 512:(i + 1) * 512], mb, ALU.subtract)
                tt(VQ, xln_bf[:, i * 512:(i + 1) * 512], tst, rb, ALU.mult)
            t_xln = VQ.tok()

            def proj_fm(wtok, wslot, out_f32, fn):
                TQ.waits(wtok, t_xln)
                for oc in range(8):
                    bk = "mm%d" % (oc % 2)
                    pb = pb_mm[oc % 2]
                    bank_write_begin(bk)
                    for ic in range(IC):
                        mm(pb[:], wbuf[wslot][:, ic * 1024 + oc * 128: ic * 1024 + (oc + 1) * 128],
                           xln_bf[:, ic * 512:(ic + 1) * 512],
                           start=(ic == 0), stop=(ic == IC - 1), inc=(1 if ic == IC - 1 else 0))
                    bank_write_end(bk)
                    bank_read(bk, AQ)
                    act(AQ, out_f32[:, oc * 512:(oc + 1) * 512], pb[:], fn)
                    bank_read_done(bk, AQ)
                return AQ.tok()

            t_f_done = proj_fm(w_tok["wk"], 0, ff, AF.Sigmoid)
            w_tok["wv"] = load_w(wv_bf_d, 0, TQ.tok())
            if not is_halo:
                t_q_done = proj_fm(w_tok["wq"], 1, qf, AF.Silu)
            else:
                t_q_done = AQ.tok()
            w_tok["wo"] = load_w(wo_bf_d, 1, TQ.tok())

            pch = {"scan": None, "ep": None, "kh": None, "tr": None, "omm": [None, None],
                   "stMM": None, "sup": None, "sbf": None}
            pend_state = None
            for ch in range(NCH):
                par_c = ch % 2
                qt_bf = qt_bfp[par_c]
                kt_bf = kt_bfp[par_c]
                dcol = dcolp[par_c]
                c0 = ch * C
                fview = ff[:].rearrange("p (s t) -> p s t", s=8)[:, :, c0:c0 + C]
                qview = qf[:].rearrange("p (s t) -> p s t", s=8)[:, :, c0:c0 + C]
                AQ.wait(t_f_done)
                AQ.wait(pch["scan"])
                act(AQ, gp1[:].rearrange("p (s t) -> p s t", s=8), fview, AF.Exp, scale=-1.0)
                t_g = act(AQ, gps[:], gp1[:], AF.Ln, bias=1.0)
                VQ.waits(t_g, pch["ep"])
                for sl in range(8):
                    scan_op(VQ, Apr[:, sl * C:(sl + 1) * C], gps[:, sl * C:(sl + 1) * C],
                            zrs[:], 0.0, ALU.add, ALU.bypass)
                t_scan = VQ.tok()
                pch["scan"] = t_scan
                AQ.waits(t_scan, pch["kh"], pch["sup"])
                cp(AQ, dcol[:, 0:8],
                   Apr[:].rearrange("p (s t) -> p s t", s=8)[:, :, C - 1:C].opt())
                act(AQ, dcol[:, 0:8], dcol[:, 0:8], AF.Exp, scale=-1.0)
                act(AQ, Em[:], Apr[:], AF.Exp, scale=-1.0)
                t_ep = act(AQ, Ep[:], Apr[:], AF.Exp)
                pch["ep"] = t_ep
                VQ.waits(t_ep, pch["omm"][par_c])
                if not is_halo:
                    tt(VQ, qt_bf[:].rearrange("p (s t) -> p s t", s=8), qview,
                       Em[:].rearrange("p (s t) -> p s t", s=8), ALU.mult)
                stt(VQ, tst2[:].rearrange("p (s t) -> p s t", s=8), fview, -1.0,
                    Ep[:].rearrange("p (s t) -> p s t", s=8), ALU.mult, ALU.mult)
                tt(VQ, ktf[:], tst2[:], Ep[:], ALU.add)
                if not is_halo:
                    cp(VQ, kt_bf[:], ktf[:])
                VQ.wait(pch["tr"])
                for sl in range(8):
                    ts(VQ, kh_bf[:, sl * C:(sl + 1) * C], ktf[:, sl * C:(sl + 1) * C],
                       dcol[:, sl:sl + 1], None, ALU.mult)
                t_kh = VQ.tok()
                pch["kh"] = t_kh

                if pend_state is not None:
                    pv_par, pv_dcol = pend_state
                    bank_read("st", VQ)
                    for sl in range(8):
                        stt(VQ, S_f[:, sl * 128:(sl + 1) * 128], S_f[:, sl * 128:(sl + 1) * 128],
                            pv_dcol[:, sl:sl + 1], pb_st[:, sl * 128:(sl + 1) * 128],
                            ALU.mult, ALU.add)
                    t_sup = VQ.tok()
                    pch["sup"] = t_sup
                    bank_read_done("st", VQ)
                    AQ.wait(t_sup)
                    t_sbf = cp(AQ, S_bf[:], S_f[:])
                    pch["sbf"] = t_sbf
                    pend_state = None

                TQ.wait(w_tok["wv"])
                for hb in range(2):
                    bk = "mm%d" % hb
                    bank_write_begin(bk)
                    for ic in range(IC):
                        mm(pb_mm[hb][0:C, :],
                           xln_bf[:, ic * 512 + c0: ic * 512 + c0 + C],
                           wbuf[0][:, ic * 1024 + hb * 512: ic * 1024 + (hb + 1) * 512],
                           start=(ic == 0), stop=(ic == IC - 1), inc=(1 if ic == IC - 1 else 0))
                    bank_write_end(bk)
                    bank_read(bk, AQ)
                    act(AQ, v_bf[:, hb * 512:(hb + 1) * 512], pb_mm[hb][0:C, :], AF.Identity)
                    bank_read_done(bk, AQ)
                t_vev = AQ.tok()

                bank_write_begin("tr")
                TQ.wait(t_kh)
                for sl in range(8):
                    tr(pb_tr[0:C, sl * 128:(sl + 1) * 128], kh_bf[:, sl * C:(sl + 1) * C],
                       inc=(1 if sl == 7 else 0))
                bank_write_end("tr")
                pch["tr"] = TQ.tok()
                bank_read("tr", AQ)
                AQ.wait(t_vev)
                t_khT = cp(AQ, khT_bf[:], pb_tr[0:C, :])
                bank_read_done("tr", AQ)

                if not is_halo:
                    bank_write_begin("sc")
                    TQ.wait(t_kh)
                    for sl in range(8):
                        mm(pb_sc[0:C, sl * C:(sl + 1) * C], kt_bf[:, sl * C:(sl + 1) * C],
                           qt_bf[:, sl * C:(sl + 1) * C], start=True, stop=True,
                           inc=(1 if sl == 7 else 0))
                    bank_write_end("sc")
                    bank_read("sc", VQ)
                    VQ.wait(t_mask)
                    t_scm = select(VQ, scm_bf[:].rearrange("p (s t) -> p s t", s=8),
                                   mask_c[:, None, :].broadcast_to([C, 8, C]),
                                   pb_sc[0:C, :].rearrange("p (s t) -> p s t", s=8),
                                   zrs[0:C, None, 0:C].broadcast_to([C, 8, C]))
                    bank_read_done("sc", VQ)

                    bank_write_begin("o")
                    TQ.waits(t_scm, t_vev, t_khT, pch["sbf"])
                    for sl in range(8):
                        mm(pb_o[:, sl * C:(sl + 1) * C], v_bf[:, sl * 128:(sl + 1) * 128],
                           scm_bf[:, sl * C:(sl + 1) * C], start=True, stop=False)
                        mm(pb_o[:, sl * C:(sl + 1) * C], S_bf[:, sl * 128:(sl + 1) * 128],
                           qt_bf[:, sl * C:(sl + 1) * C], start=False, stop=True,
                           inc=(1 if sl == 7 else 0))
                    bank_write_end("o")
                    pch["omm"][par_c] = TQ.tok()
                else:
                    pch["omm"][par_c] = TQ.tok()

                bank_write_begin("st")
                TQ.waits(t_khT, t_vev, pch["sup"])
                for sl in range(8):
                    mm(pb_st[:, sl * 128:(sl + 1) * 128], khT_bf[:, sl * 128:(sl + 1) * 128],
                       v_bf[:, sl * 128:(sl + 1) * 128], start=True, stop=True,
                       inc=(1 if sl == 7 else 0))
                bank_write_end("st")
                pch["stMM"] = TQ.tok()
                pend_state = (par_c, dcol)

                if not is_halo:
                    oview = o_bf[:].rearrange("p (s t) -> p s t", s=8)[:, :, c0:c0 + C]
                    bank_read("o", AQ)
                    act(AQ, sqst[:], pb_o[:], AF.Square)
                    act(AQ, oview, pb_o[:].rearrange("p (s t) -> p s t", s=8), AF.Identity)
                    bank_read_done("o", AQ)
                    bank_write_begin("row")
                    TQ.wait(AQ.tok())
                    for sl in range(8):
                        mm(pb_row[:, c0:c0 + C], ones_col[:], sqst[:, sl * C:(sl + 1) * C],
                           start=(sl == 0), stop=(sl == 7), inc=(1 if sl == 7 else 0))
                    bank_write_end("row")
                    AQ.wait(TQ.tok())

            pv_par, pv_dcol = pend_state
            bank_read("st", VQ)
            for sl in range(8):
                stt(VQ, S_f[:, sl * 128:(sl + 1) * 128], S_f[:, sl * 128:(sl + 1) * 128],
                    pv_dcol[:, sl:sl + 1], pb_st[:, sl * 128:(sl + 1) * 128],
                    ALU.mult, ALU.add)
            t_sup = VQ.tok()
            bank_read_done("st", VQ)
            AQ.wait(t_sup)
            t_sbf = cp(AQ, S_bf[:], S_f[:])
            TQ.wait(t_sbf)
            pend_state = None

            if not is_halo:
                c_row = mb[0:1, 0:512]
                bank_read("row", VQ)
                cp(VQ, c_row, pb_row[:, 0:512])
                bank_read_done("row", VQ)
                t_ms = ts(VQ, c_row, c_row, 1.0 / D, None, ALU.mult)
                AQ.wait(t_ms)
                t_c = rrow(c_row, c_row, 1)
                VQ.wait(t_c)

                TQ.waits(w_tok["wo"], AQ.tok())
                for dc in range(IC):
                    bk = "mm%d" % (dc % 2)
                    pb = pb_mm[dc % 2]
                    bank_write_begin(bk)
                    for kc in range(IC):
                        mm(pb[:], wbuf[1][:, kc * 1024 + dc * 128: kc * 1024 + (dc + 1) * 128],
                           o_bf[:, kc * 512:(kc + 1) * 512],
                           start=(kc == 0), stop=(kc == IC - 1), inc=(1 if kc == IC - 1 else 0))
                    bank_write_end(bk)
                    bank_read(bk, AQ)
                    act(AQ, Xf[:, dc * 512:(dc + 1) * 512], pb[:], AF.Identity)
                    bank_read_done(bk, AQ)
                t_X = AQ.tok()

                VQ.wait(t_X)
                row_stats(lambda i: Xf[:, i * 512:(i + 1) * 512], t_X)
                ts(VQ, rows[:, 0:512], rows[:, 0:512], 1.0 / D, None, ALU.mult)
                ts(VQ, rows[:, 512:1024], rows[:, 512:1024], 1.0 / D, None, ALU.mult)
                tt(VQ, rows[:, 1024:1536], rows[:, 0:512], rows[:, 0:512], ALU.mult)
                tt(VQ, rows[:, 512:1024], rows[:, 512:1024], rows[:, 1024:1536], ALU.subtract)
                tt(VQ, rows[:, 1024:1536], c_row, c_row, ALU.mult)
                t_cv = tt(VQ, rows[:, 512:1024], rows[:, 512:1024], rows[:, 1024:1536], ALU.mult)
                AQ.wait(t_cv)
                t_r3 = rrow(rows[:, 512:1024], rows[:, 512:1024], 0)
                VQ.wait(t_r3)
                tt(VQ, rows[:, 1024:1536], c_row, rows[:, 512:1024], ALU.mult)
                tt(VQ, rows[:, 0:512], rows[:, 0:512], rows[:, 1024:1536], ALU.mult)
                ts(VQ, rows[:, 0:512], rows[:, 0:512], -1.0, None, ALU.mult)
                bcast_row(rows[:, 1024:1536], ab, VQ.tok())
                bcast_row(rows[:, 0:512], bb, VQ.tok())

                for dc in range(IC):
                    tt(VQ, tst, Xf[:, dc * 512:(dc + 1) * 512], ab, ALU.mult)
                    tt(VQ, tst2, tst, bb, ALU.add)
                    tt(VQ, y1f[:, dc * 512:(dc + 1) * 512], tst2,
                       x_cur[:, dc * 512:(dc + 1) * 512], ALU.add)
                t_y1 = VQ.tok()
                SQ.wait(t_y1)
                dma(SQ, bass.AP(y1_d, rtok0, [[NTOK, 128], [128 * NTOK, IC], [1, MAC]]),
                    y1f[:].rearrange("p (a b) -> p a b", a=IC))

                VQ.wait(t_y1)
                AQ.wait(t_y1)
                row_stats(lambda i: y1f[:, i * 512:(i + 1) * 512], t_y1)
                ts(VQ, rows[:, 0:512], rows[:, 0:512], 1.0 / D, None, ALU.mult)
                ts(VQ, rows[:, 512:1024], rows[:, 512:1024], 1.0 / D, None, ALU.mult)
                tt(VQ, rows[:, 1024:1536], rows[:, 0:512], rows[:, 0:512], ALU.mult)
                t_v2 = tt(VQ, rows[:, 512:1024], rows[:, 512:1024], rows[:, 1024:1536],
                          ALU.subtract)
                AQ.wait(t_v2)
                t_r2 = rrow(rows[:, 512:1024], rows[:, 512:1024], 0)
                VQ.wait(t_r2)
                mi = m - 1
                SQ.wait(VQ.tok())
                dma(SQ, bass.AP(ln2_d, mi * MAC, [[1, 1], [1, MAC]]), rows[0:1, 0:512])
                dma(SQ, bass.AP(ln2_d, NTOK + mi * MAC, [[1, 1], [1, MAC]]),
                    rows[0:1, 512:1024])
                bcast_row(rows[:, 0:512], mb, VQ.tok())
                bcast_row(rows[:, 512:1024], rb, VQ.tok())
                for dc in range(IC):
                    tt(VQ, tst, y1f[:, dc * 512:(dc + 1) * 512], mb, ALU.subtract)
                    t_x2 = tt(VQ, tst2, tst, rb, ALU.mult)
                    SQ.wait(t_x2)
                    t_st2 = dma(SQ, bass.AP(xln2_d, rtok0 + dc * 128 * NTOK,
                                            [[NTOK, 128], [1, MAC]]), tst2)
                    red(VQ, maxst[:, (m - 1) * 8 + dc:(m - 1) * 8 + dc + 1], tst2,
                        ALU.max, absv=True)
                    VQ.wait(t_st2)

            if DEBUG_DUMP and m == 1:
                barrier_all()
                cp(VQ, tst, S_f[:, 0:512])
                cp(VQ, tst2, S_f[:, 512:1024])
                t_d1 = VQ.tok()
                SQ.wait(t_d1)
                dma(SQ, bass.AP(dbg_S.tensor, 0, [[1024, 128], [1, 512]]), tst)
                dma(SQ, bass.AP(dbg_S.tensor, 512, [[1024, 128], [1, 512]]), tst2)
                VQ.wait(SQ.tok())
                cp(VQ, tst, pb_st[:, 0:512])
                cp(VQ, tst2, pb_st[:, 512:1024])
                t_d1b = VQ.tok()
                SQ.wait(t_d1b)
                dma(SQ, bass.AP(dbg_pst.tensor, 0, [[1024, 128], [1, 512]]), tst)
                dma(SQ, bass.AP(dbg_pst.tensor, 512, [[1024, 128], [1, 512]]), tst2)
                cp(VQ, SCR[:, 0:512], qt_bf[:])
                cp(VQ, SCR[:, 512:1024], kh_bf[:])
                cp(VQ, SCR[:, 1024:1536].rearrange("p (a b) -> p a b", a=8)[0:64],
                   scm_bf[:].rearrange("p (a b) -> p a b", a=8))
                t_d2 = VQ.tok()
                SQ.wait(t_d2)
                dma(SQ, dbg_qt, SCR[:, 0:512])
                dma(SQ, dbg_kh, SCR[:, 512:1024])
                dma(SQ, dbg_dc, dcol[:])
                dma(SQ, bass.AP(dbg_scm.tensor, 0, [[512, 64], [1, 512]]), SCR[0:64, 1024:1536])
                for dcb in range(IC):
                    cp(VQ, tst, o_bf[:, dcb * 512:(dcb + 1) * 512])
                    t_d3 = VQ.tok()
                    SQ.wait(t_d3)
                    dma(SQ, bass.AP(dbg_o.tensor, dcb * 512, [[4096, 128], [1, 512]]), tst)
                    VQ.wait(SQ.tok())
            barrier_all()

        def finalize_scalar(run_ap, ncols, bar, post_scale, result_cols):
            SQ.wait(VQ.tok())
            for c_ in range(ncols):
                dma(SQ, bass.AP(barsc, c_ * 128, [[1, 1], [1, 128]]),
                    run_ap[:, c_:c_ + 1])
            t_all = SQ.tok()
            SQ.wait(t_all)
            VQ.wait(t_all)
            for c_ in range(ncols):
                t_bl = dma(SQ, mb[0:1, 0:128], bass.AP(barsc, c_ * 128, [[1, 1], [1, 128]]))
                VQ.wait(t_bl)
                t_rd = red(VQ, scal[0:1, 8 + c_:9 + c_], mb[0:1, 0:128], ALU.max)
                SQ.wait(t_rd)
            vfence()
            t_red = VQ.tok()
            SQ.wait(t_red)
            t_bi = dma(SQ, bass.AP(bar_in[bar], 0, [[1, 1], [1, ncols]]),
                       scal[0:1, 8:8 + ncols])
            GQ.wait(t_bi)
            cc_cnt[0] += 1
            ccv = cc_cnt[0]
            GQ.ops.append(lambda e, i=bar_in[bar], o=bar_out[bar], s=ccs:
                          e.collective_compute("AllGather", ALU.bypass,
                                               replica_groups=[list(range(NCORES))],
                                               ins=[i.ap().opt()], outs=[o.ap().opt()])
                          .then_inc(s, 1))
            GQ.wait((ccs, ccv))
            t_bo = dma(GQ, scal[0:1, 32:64], bass.AP(bar_out[bar], 0, [[1, 1], [1, 32]]))
            VQ.wait(t_bo)
            for j, rc in enumerate(result_cols):
                red(VQ, scal[0:1, j:j + 1],
                    scal[0:1, 32:64].rearrange("p (a b) -> p a b", a=8)[:, :, j:j + 1].opt(),
                    ALU.max)
                vfence()
                ts(VQ, scal[0:1, j:j + 1], scal[0:1, j:j + 1], post_scale, 1e-12,
                   ALU.mult, ALU.max)
                vfence()
                recip(VQ, scal[0:1, 16 + j:17 + j], scal[0:1, j:j + 1])
                vfence()
                for src, dstc in ((scal[0:1, j:j + 1], rc), (scal[0:1, 16 + j:17 + j], rc + 1)):
                    bank_write_begin("sc")
                    TQ.wait(VQ.tok())
                    mm(pb_sc[:, 0:1], ones_row[:], src, True, True, inc=1)
                    bank_write_end("sc")
                    bank_read("sc", VQ)
                    cp(VQ, scol[:, dstc:dstc + 1], pb_sc[:, 0:1])
                    bank_read_done("sc", VQ)

        vfence()
        red(VQ, runmax[:], maxst[:, 0:64], ALU.max)
        vfence()
        finalize_scalar(runmax[:], 1, "A", 1.0 / 32767.0, [0])

        upW = ARENA[:].bitcast(BF16)[:, 0:24576]
        gateW = ARENA[:].bitcast(BF16)[:, 24576:49152]
        GQ.wait(t_gather)
        SQ.wait(GQ.tok())
        wload = []
        for icx in range(IC):
            wload.append(dma(SQ, upW[:, icx * H:(icx + 1) * H],
                             bass.AP(ex_out, icx * NEX, [[H, 128], [1, H]])))
            wload.append(dma(SQ, gateW[:, icx * H:(icx + 1) * H],
                             bass.AP(ex_out, icx * NEX + EX_UP, [[H, 128], [1, H]])))
        t_wl = SQ.tok()
        TQ.wait(t_wl)
        xln2f = SCR[:]
        i32v = wbuf[0][:].bitcast(I32)
        codebf = xln_bf[:]
        memset(VQ, maxst[:], 0.0)
        zst = [wbuf[1][:].bitcast(F32)[:, 0:2048], wbuf[1][:].bitcast(F32)[:, 2048:4096]]
        zst_cons = [None, None]
        memset(VQ, dcol[:], 0.0)
        for mi in range(NRMAC):
            rt0 = mi * MAC
            t_x2l = dma(SQ, xln2f.rearrange("p (a b) -> p a b", a=IC),
                        bass.AP(xln2_d, rt0, [[NTOK, 128], [128 * NTOK, IC], [1, MAC]]))
            VQ.wait(t_x2l)
            ts(VQ, i32v[:, 0:4096], xln2f, scol[:, 1:2], None, ALU.mult)
            t_code = cp(VQ, codebf, i32v[:, 0:4096])
            TQ.wait(t_code)
            for tix, (Wv_, zd, runc) in enumerate(((upW, zup_d, 0), (gateW, zgate_d, 1))):
                for g in range(6):
                    par_ = g % 2
                    zstage = zst[par_]
                    for t_ in (zst_cons[par_] or []):
                        AQ.wait(t_)
                    for oc4 in range(4):
                        oc = g * 4 + oc4
                        bk = "mm%d" % (oc % 2)
                        pb = pb_mm[oc % 2]
                        bank_write_begin(bk)
                        for icx in range(IC):
                            mm(pb[:], Wv_[:, icx * H + oc * 128: icx * H + (oc + 1) * 128],
                               codebf[:, icx * 512:(icx + 1) * 512],
                               start=(icx == 0), stop=(icx == IC - 1),
                               inc=(1 if icx == IC - 1 else 0))
                        bank_write_end(bk)
                        bank_read(bk, AQ)
                        act(AQ, zstage[:, oc4 * 512:(oc4 + 1) * 512], pb[:],
                            AF.Identity, scale=scol[:, 0:1])
                        bank_read_done(bk, AQ)
                    t_ev = AQ.tok()
                    VQ.wait(t_ev)
                    t_rm = red(VQ, maxst[:, tix * 48 + mi * 6 + g: tix * 48 + mi * 6 + g + 1],
                               zstage[:, 0:2048], ALU.max, absv=True)
                    SQ.wait(t_ev)
                    t_zs = dma(SQ, bass.AP(zd, g * 4 * 128 * NTOK + rt0,
                                           [[NTOK, 128], [128 * NTOK, 4], [1, MAC]]),
                               zstage[:, 0:2048].rearrange("p (a b) -> p a b", a=4))
                    zst_cons[par_] = [t_rm, t_zs]
                    if DEBUG_F and mi == 0 and tix == 0 and g == 0:
                        t_dz = dma(SQ, dbg_z, zstage[:, 0:2048])
                        zst_cons[par_].append(t_dz)
            SQ.wait(VQ.tok())
            SQ.wait(TQ.tok())
            VQ.wait(TQ.tok())
        barrier_all()

        vfence()
        if DEBUG_F:
            SQ.wait(VQ.tok())
            t_dmx = dma(SQ, dbg_mx, maxst[:])
            VQ.wait(t_dmx)
        red(VQ, dcol[:, 0:1], maxst[:, 0:48], ALU.max)
        red(VQ, dcol[:, 1:2], maxst[:, 48:96], ALU.max)
        vfence()
        finalize_scalar(dcol[:, 0:2], 2, "B", 1.0 / 32767.0, [2, 4])

        memset(VQ, maxst[:], 0.0)
        zgv = wbuf[1][:].bitcast(F32)
        gq_f = SCR[:, 0:2048]
        fs3 = SCR[:, 2048:4096]
        GW = 4
        i32g = wbuf[0][:].bitcast(I32)
        for mi in range(NRMAC):
            rt0 = mi * MAC
            for g in range(6):
                t_zl = dma(SQ, zgv[:, 0:GW * 512].rearrange("p (a b) -> p a b", a=GW),
                           bass.AP(zgate_d, g * GW * 128 * NTOK + rt0,
                                   [[NTOK, 128], [128 * NTOK, GW], [1, MAC]]))
                VQ.wait(t_zl)
                t_ci = ts(VQ, i32g[:, 0:GW * 512], zgv[:, 0:GW * 512], scol[:, 5:6], None, ALU.mult)
                GQ.wait(t_ci)
                t_gst = dma(GQ, bass.AP(gcode_d, g * GW * 128 * NTOK + rt0,
                                        [[NTOK, 128], [128 * NTOK, GW], [1, MAC]]),
                            i32g[:, 0:GW * 512].rearrange("p (a b) -> p a b", a=GW))
                t_gq = ts(VQ, gq_f, i32g[:, 0:GW * 512], scol[:, 4:5], None, ALU.mult)
                AQ.wait(t_gq)
                t_sg = act(AQ, fs3, gq_f, AF.Sigmoid)
                VQ.wait(t_sg)
                VQ.wait(t_gst)
                ts(VQ, i32g[:, 0:GW * 512], fs3, 65536.0, 65535.0, ALU.mult, ALU.min)
                cp(VQ, fs3, i32g[:, 0:GW * 512])
                t_g2 = tt(VQ, gq_f, gq_f, fs3, ALU.mult)
                red(VQ, maxst[:, mi * 6 + g: mi * 6 + g + 1], gq_f, ALU.max, absv=True)
                AQ.wait(VQ.tok())
                SQ.wait(VQ.tok())
        barrier_all()

        vfence()
        red(VQ, dcol[:, 2:3], maxst[:, 0:48], ALU.max)
        vfence()
        finalize_scalar(dcol[:, 2:3], 1, "C", (1.0 / 65536.0) / 32767.0, [6])
        vfence()
        ts(VQ, scol[:, 10:11], scol[:, 7:8], 1.0 / 65536.0, None, ALU.mult)
        vfence()

        memset(VQ, maxst[:], 0.0)
        for mi in range(NRMAC):
            rt0 = mi * MAC
            for g in range(6):
                base = g * GW * 128 * NTOK + rt0
                apz = [[NTOK, 128], [128 * NTOK, GW], [1, MAC]]
                t_gl = dma(GQ, gq_f.rearrange("p (a b) -> p a b", a=GW),
                           bass.AP(gcode_d, base, apz))
                t_ul = dma(SQ, zgv[:, 0:GW * 512].rearrange("p (a b) -> p a b", a=GW),
                           bass.AP(zup_d, base, apz))
                VQ.wait(t_gl)
                ts(VQ, gq_f, gq_f, scol[:, 4:5], None, ALU.mult)
                AQ.wait(VQ.tok())
                t_sg = act(AQ, fs3, gq_f, AF.Sigmoid)
                VQ.wait(t_sg)
                ts(VQ, i32g[:, 0:GW * 512], fs3, 65536.0, 65535.0, ALU.mult, ALU.min)
                cp(VQ, fs3, i32g[:, 0:GW * 512])
                tt(VQ, gq_f, gq_f, fs3, ALU.mult)
                t_g2c = ts(VQ, i32g[:, 0:GW * 512], gq_f, scol[:, 10:11], None, ALU.mult)
                GQ.wait(t_g2c)
                t_g2st = dma(GQ, bass.AP(g2code_d, base, apz),
                             i32g[:, 0:GW * 512].rearrange("p (a b) -> p a b", a=GW))
                ts(VQ, fs3, i32g[:, 0:GW * 512], scol[:, 6:7], None, ALU.mult)
                VQ.wait(t_ul)
                VQ.wait(t_g2st)
                t_uc = ts(VQ, i32g[:, 0:GW * 512], zgv[:, 0:GW * 512], scol[:, 3:4], None, ALU.mult)
                GQ.wait(t_uc)
                t_ust = dma(GQ, bass.AP(ucode_d, base, apz),
                            i32g[:, 0:GW * 512].rearrange("p (a b) -> p a b", a=GW))
                ts(VQ, zgv[:, 0:GW * 512], i32g[:, 0:GW * 512], scol[:, 2:3], None, ALU.mult)
                t_op = tt(VQ, gq_f, fs3, zgv[:, 0:GW * 512], ALU.mult)
                red(VQ, maxst[:, mi * 6 + g: mi * 6 + g + 1], gq_f, ALU.max, absv=True)
                VQ.wait(t_ust)
                AQ.wait(VQ.tok())
                GQ.wait(VQ.tok())
                SQ.wait(VQ.tok())
        barrier_all()

        vfence()
        red(VQ, dcol[:, 3:4], maxst[:, 0:48], ALU.max)
        vfence()
        finalize_scalar(dcol[:, 3:4], 1, "D", 1.0 / 32767.0, [8])

        downW = ARENA[:].bitcast(BF16)[:, 0:24576]
        GQ.wait(SQ.tok())
        dwl = []
        for icx in range(NCORES):
            for icl in range(3):
                icg = icx * 3 + icl
                dwl.append(dma(SQ, downW[:, icg * D:(icg + 1) * D],
                               bass.AP(ex_out, icx * NEX + EX_UP + EX_GATE + icl * 128 * D,
                                       [[D, 128], [1, D]])))
        t_dwl = SQ.tok()
        TQ.wait(t_dwl)
        Xs = ARENA[:, 12288:16384]
        y1s = ARENA[:, 16384:20480]
        y2s = ARENA[:, 20480:24576]

        def codebf_sl(ch):
            if ch < 16:
                return wbuf[1][:, ch * 512:(ch + 1) * 512]
            return o_bf[:, (ch - 16) * 512:(ch - 15) * 512]

        for mi in range(NRMAC):
            rt0 = mi * MAC
            t_y1l = dma(SQ, y1s.rearrange("p (a b) -> p a b", a=IC),
                        bass.AP(y1_d, rt0, [[NTOK, 128], [128 * NTOK, IC], [1, MAC]]))
            for g in range(6):
                base = g * GW * 128 * NTOK + rt0
                apz = [[NTOK, 128], [128 * NTOK, GW], [1, MAC]]
                t_g2l = dma(GQ, fs3.rearrange("p (a b) -> p a b", a=GW),
                            bass.AP(g2code_d, base, apz))
                t_ucl = dma(GQ, gq_f.rearrange("p (a b) -> p a b", a=GW),
                            bass.AP(ucode_d, base, apz))
                VQ.waits(t_g2l, t_ucl)
                ts(VQ, fs3, fs3, scol[:, 6:7], None, ALU.mult)
                ts(VQ, gq_f, gq_f, scol[:, 2:3], None, ALU.mult)
                tt(VQ, fs3, fs3, gq_f, ALU.mult)
                t_oc = ts(VQ, i32g[:, 0:GW * 512], fs3, scol[:, 9:10], None, ALU.mult)
                for oc6 in range(GW):
                    ch = g * GW + oc6
                    cp(VQ, codebf_sl(ch), i32g[:, oc6 * 512:(oc6 + 1) * 512])
                GQ.wait(VQ.tok())
            t_codes = VQ.tok()
            TQ.waits(t_codes, t_dwl)
            for dc in range(IC):
                bk = "mm%d" % (dc % 2)
                pb = pb_mm[dc % 2]
                bank_write_begin(bk)
                for ch in range(24):
                    mm(pb[:], downW[:, ch * D + dc * 128: ch * D + (dc + 1) * 128],
                       codebf_sl(ch), start=(ch == 0), stop=(ch == 23),
                       inc=(1 if ch == 23 else 0))
                bank_write_end(bk)
                bank_read(bk, AQ)
                act(AQ, Xs[:, dc * 512:(dc + 1) * 512], pb[:], AF.Identity,
                    scale=scol[:, 8:9])
                bank_read_done(bk, AQ)
            t_X4 = AQ.tok()
            VQ.wait(t_X4)
            row_stats(lambda i: Xs[:, i * 512:(i + 1) * 512], t_X4)
            ts(VQ, rows[:, 0:512], rows[:, 0:512], 1.0 / D, None, ALU.mult)
            ts(VQ, rows[:, 512:1024], rows[:, 512:1024], 1.0 / D, None, ALU.mult)
            tt(VQ, rows[:, 1024:1536], rows[:, 0:512], rows[:, 0:512], ALU.mult)
            t_v4 = tt(VQ, rows[:, 512:1024], rows[:, 512:1024], rows[:, 1024:1536],
                      ALU.subtract)
            AQ.wait(t_v4)
            t_r4 = rrow(rows[:, 512:1024], rows[:, 512:1024], 0)
            VQ.wait(t_r4)
            tt(VQ, rows[:, 0:512], rows[:, 0:512], rows[:, 512:1024], ALU.mult)
            ts(VQ, rows[:, 0:512], rows[:, 0:512], -1.0, None, ALU.mult)
            bcast_row(rows[:, 512:1024], ab, VQ.tok())
            bcast_row(rows[:, 0:512], bb, VQ.tok())
            VQ.wait(t_y1l)
            for dc in range(IC):
                tt(VQ, tst, Xs[:, dc * 512:(dc + 1) * 512], ab, ALU.mult)
                tt(VQ, tst2, tst, bb, ALU.add)
                tt(VQ, y2s[:, dc * 512:(dc + 1) * 512], tst2,
                   y1s[:, dc * 512:(dc + 1) * 512], ALU.add)
            t_y2d = VQ.tok()
            SQ.wait(t_y2d)
            dma(SQ, bass.AP(outT.tensor, rt0, [[NTOK, 128], [128 * NTOK, IC], [1, MAC]]),
                y2s.rearrange("p (a b) -> p a b", a=IC))
            GQ.wait(SQ.tok())
            VQ.wait(SQ.tok())
        if DEBUG_F:
            dma(SQ, dbg_scol, scol[:, 0:12])
            dma(SQ, dbg_w, upW[:, 0:3072])
            for mm_ in range(NRMAC):
                apv = [[NTOK, 128], [128 * NTOK, IC], [1, MAC]]
                tl1 = dma(SQ, SCR[:].rearrange("p (a b) -> p a b", a=IC),
                          bass.AP(y1_d, mm_ * MAC, apv))
                SQ.wait(tl1)
                dma(SQ, bass.AP(dbg_y1.tensor, mm_ * MAC, apv),
                    SCR[:].rearrange("p (a b) -> p a b", a=IC))
                tl2 = dma(SQ, SCR[:].rearrange("p (a b) -> p a b", a=IC),
                          bass.AP(xln2_d, mm_ * MAC, apv))
                SQ.wait(tl2)
                dma(SQ, bass.AP(dbg_xl2.tensor, mm_ * MAC, apv),
                    SCR[:].rearrange("p (a b) -> p a b", a=IC))
                SQ.wait(SQ.tok())
            for j_, mm_ in enumerate((0, 6)):
                apz3 = [[NTOK, 128], [128 * NTOK, 24], [1, MAC]]
                apd3 = [[1024, 128], [128 * 1024, 24], [1, MAC]]
                for (srcd, dstd) in ((zup_d, dbg_zu), (ucode_d, dbg_uc), (g2code_d, dbg_g2c)):
                    tlz = dma(GQ, wbuf[0][:].bitcast(F32)[:, 0:4096].rearrange(
                        "p (a b) -> p a b", a=8)[:, 0:8, 0:512],
                        bass.AP(srcd, mm_ * MAC, [[NTOK, 128], [128 * NTOK, 8], [1, MAC]]))
                    GQ.wait(tlz)
                    tsz = dma(GQ, bass.AP(dstd.tensor if hasattr(dstd, "tensor") else dstd,
                                          j_ * MAC, [[1024, 128], [128 * 1024, 8], [1, MAC]]),
                              wbuf[0][:].bitcast(F32)[:, 0:4096].rearrange(
                        "p (a b) -> p a b", a=8)[:, 0:8, 0:512])
                    GQ.wait(tsz)
        SQ.wait(SQ.tok())
        GQ.wait(GQ.tok())

        with nc.Block() as block:
            @block.sync
            def _(q):
                for f_ in SQ.ops:
                    f_(q)

            @block.gpsimd
            def _(q):
                for f_ in GQ.ops:
                    f_(q)

            @block.tensor
            def _(q):
                for f_ in TQ.ops:
                    f_(q)

            @block.vector
            def _(q):
                for f_ in VQ.ops:
                    f_(q)

            @block.scalar
            def _(q):
                for f_ in AQ.ops:
                    f_(q)

    return nc


def _stage_inputs(inputs):
    x = np.asarray(inputs["L0_attn_x"])
    xTs = [np.ascontiguousarray(x[s].T) for s in range(2)]
    eye = np.eye(128, dtype=np.float32)
    mask = np.zeros((C, C), dtype=np.uint8)
    for s_ in range(C):
        mask[s_, s_:] = 1
    nw = np.ascontiguousarray(np.asarray(inputs["L0_attn_norm_w"]).reshape(IC, 128).T)
    nw = np.ascontiguousarray(nw.astype(np.float32))
    w_up = np.ascontiguousarray(np.asarray(inputs["L0_ffn_w_up"], dtype=np.float32))
    w_gate = np.ascontiguousarray(np.asarray(inputs["L0_ffn_w_gate"], dtype=np.float32))
    w_down = np.ascontiguousarray(np.asarray(inputs["L0_ffn_w_down"], dtype=np.float32))
    base = dict(
        wqT=np.ascontiguousarray(np.asarray(inputs["L0_attn_Wq"], dtype=np.float32).T),
        wkT=np.ascontiguousarray(np.asarray(inputs["L0_attn_Wk"], dtype=np.float32).T),
        wvT=np.ascontiguousarray(np.asarray(inputs["L0_attn_Wv"], dtype=np.float32).T),
        woT=np.ascontiguousarray(np.asarray(inputs["L0_attn_Wo"], dtype=np.float32).T),
        nwfm=nw, eye_h=eye, mask_h=mask,
    )
    maps = []
    for core in range(NCORES):
        seq, seg = core // 4, core % 4
        t0 = seg * NTOK
        xs = np.zeros((D, TOT), dtype=np.float32)
        lo = t0 - HALO
        if lo < 0:
            xs[:, HALO:] = xTs[seq][:, t0:t0 + NTOK]
        else:
            xs[:, :] = xTs[seq][:, lo:t0 + NTOK]
        sl = []
        for j in range(OC3):
            sl.append(w_up[j * 128:(j + 1) * 128, core * 128:(core + 1) * 128])
        for j in range(OC3):
            sl.append(w_gate[j * 128:(j + 1) * 128, core * 128:(core + 1) * 128])
        for icl in range(3):
            icg = core * 3 + icl
            for hc in range(IC):
                sl.append(w_down[hc * 128:(hc + 1) * 128, icg * 128:(icg + 1) * 128])
        m = dict(base)
        m["xT"] = np.ascontiguousarray(xs)
        m["dq_sl"] = np.ascontiguousarray(np.concatenate(sl, axis=0))
        maps.append(m)
    return maps


_NC_CACHE = None


def kernel(L0_attn_x, L0_attn_Wq, L0_attn_Wk, L0_attn_Wv, L0_attn_Wo,
           L0_attn_norm_w, L0_ffn_w_gate, L0_ffn_w_up, L0_ffn_w_down):
    global _NC_CACHE
    inputs = dict(L0_attn_x=L0_attn_x, L0_attn_Wq=L0_attn_Wq, L0_attn_Wk=L0_attn_Wk,
                  L0_attn_Wv=L0_attn_Wv, L0_attn_Wo=L0_attn_Wo,
                  L0_attn_norm_w=L0_attn_norm_w, L0_ffn_w_gate=L0_ffn_w_gate,
                  L0_ffn_w_up=L0_ffn_w_up, L0_ffn_w_down=L0_ffn_w_down)
    if _NC_CACHE is None:
        _NC_CACHE = build_program()
    nc = _NC_CACHE
    maps = _stage_inputs(inputs)
    res = run_bass_kernel_spmd(nc, maps, core_ids=list(range(NCORES)))
    out = np.empty((2, 16384, D), dtype=np.float32)
    for core in range(NCORES):
        seq, seg = core // 4, core % 4
        o = np.asarray(res.results[core]["outT"])
        out[seq, seg * NTOK:(seg + 1) * NTOK, :] = o.T
    return out


# version 09
# speedup vs baseline: 1.0904x; 1.0087x over previous
import numpy as np
import ml_dtypes
from contextlib import ExitStack

import concourse.bass as bass
import concourse.mybir as mybir
from concourse.bass_utils import run_bass_kernel_spmd

F32 = mybir.dt.float32
BF16 = mybir.dt.bfloat16
I32 = mybir.dt.int32
I16 = mybir.dt.int16
AF = mybir.ActivationFunctionType
ALU = mybir.AluOpType
AX = mybir.AxisListType

D = 1024
H = 3072
NTOK = 4096
HALO = 512
TOT = NTOK + HALO
MAC = 512
NMAC = TOT // MAC
NRMAC = NMAC - 1
C = 64
NCH = MAC // C
IC = D // 128
OC3 = H // 128
NCORES = 8

EX_UP = 128 * H
EX_GATE = 128 * H
EX_DOWN = 3 * 128 * D
NEX = EX_UP + EX_GATE + EX_DOWN

STUB_F = False
DEBUG_DUMP = False
DEBUG_F = False


class Q:
    def __init__(self, sem):
        self.ops = []
        self.sem = sem
        self.cnt = 0

    def wait(self, tok):
        if tok is None:
            return
        sem, val = tok
        self.ops.append(lambda e, s=sem, v=val: e.wait_ge(s, v))

    def waits(self, *toks):
        for t in toks:
            self.wait(t)

    def run(self, fn, inc=1):
        self.cnt += inc
        tokv = self.cnt

        def cl(e, fn=fn, inc=inc, s=self.sem):
            ins = fn(e)
            if inc:
                ins.then_inc(s, inc)
        self.ops.append(cl)
        return (self.sem, tokv)

    def tok(self):
        return (self.sem, self.cnt)


def build_program():
    nc = bass.Bass("TRN2", target_bir_lowering=False)

    def din(name, shape, dtype=F32):
        return nc.dram_tensor(name, shape, dtype, kind="ExternalInput").ap()

    xT = din("xT", [D, TOT])
    wqT = din("wqT", [D, D])
    wkT = din("wkT", [D, D])
    wvT = din("wvT", [D, D])
    woT = din("woT", [D, D])
    nwfm = din("nwfm", [128, IC])
    eye_h = din("eye_h", [128, 128])
    mask_h = din("mask_h", [C, C], mybir.dt.uint8)
    dq_sl = din("dq_sl", [3 * OC3 * 128, 128])

    outT = nc.dram_tensor("outT", [D, NTOK], F32, kind="ExternalOutput").ap()
    if DEBUG_F:
        dbg_scol = nc.dram_tensor("dbg_scol", [128, 12], F32, kind="ExternalOutput").ap()
        dbg_z = nc.dram_tensor("dbg_z", [128, 2048], F32, kind="ExternalOutput").ap()
        dbg_w = nc.dram_tensor("dbg_w", [128, 3072], mybir.dt.bfloat16, kind="ExternalOutput").ap()
        dbg_mx = nc.dram_tensor("dbg_mx", [128, 96], F32, kind="ExternalOutput").ap()
        dbg_y1 = nc.dram_tensor("dbg_y1", [D, NTOK], F32, kind="ExternalOutput").ap()
        dbg_zu = nc.dram_tensor("dbg_zu", [H, 1024], F32, kind="ExternalOutput").ap()
        dbg_uc = nc.dram_tensor("dbg_uc", [H, 1024], F32, kind="ExternalOutput").ap()
        dbg_g2c = nc.dram_tensor("dbg_g2c", [H, 1024], F32, kind="ExternalOutput").ap()
        dbg_xl2 = nc.dram_tensor("dbg_xl2", [D, NTOK], F32, kind="ExternalOutput").ap()
    if DEBUG_DUMP:
        dbg_S = nc.dram_tensor("dbg_S", [128, 1024], F32, kind="ExternalOutput").ap()
        dbg_o = nc.dram_tensor("dbg_o", [128, 4096], F32, kind="ExternalOutput").ap()
        dbg_qt = nc.dram_tensor("dbg_qt", [128, 512], F32, kind="ExternalOutput").ap()
        dbg_kh = nc.dram_tensor("dbg_kh", [128, 512], F32, kind="ExternalOutput").ap()
        dbg_dc = nc.dram_tensor("dbg_dc", [128, 8], F32, kind="ExternalOutput").ap()
        dbg_scm = nc.dram_tensor("dbg_scm", [64, 512], F32, kind="ExternalOutput").ap()
        dbg_pst = nc.dram_tensor("dbg_pst", [128, 1024], F32, kind="ExternalOutput").ap()

    y1_d = nc.dram_tensor("y1_d", [D, NTOK], F32)
    xln2_d = nc.dram_tensor("xln2_d", [D, NTOK], F32)
    zup_d = nc.dram_tensor("zup_d", [H, NTOK], F32)
    zgate_d = nc.dram_tensor("zgate_d", [H, NTOK], F32)
    gcode_d = nc.dram_tensor("gcode_d", [H, NTOK], I16)
    g2code_d = nc.dram_tensor("g2code_d", [H, NTOK], I16)
    ucode_d = nc.dram_tensor("ucode_d", [H, NTOK], I16)
    ln2_d = nc.dram_tensor("ln2_d", [2, NTOK], F32)
    wq_bf_d = nc.dram_tensor("wq_bf_d", [D, D], BF16)
    wk_bf_d = nc.dram_tensor("wk_bf_d", [D, D], BF16)
    wv_bf_d = nc.dram_tensor("wv_bf_d", [D, D], BF16)
    wo_bf_d = nc.dram_tensor("wo_bf_d", [D, D], BF16)
    ex_in = nc.dram_tensor("ex_in", [1, NEX], BF16)
    ex_out = nc.dram_tensor("ex_out", [NCORES, NEX], BF16, addr_space="Shared")
    barsc = nc.dram_tensor("barsc", [4, 128], F32)
    bar_in = {}
    bar_out = {}
    for b in ("A", "B", "C", "D"):
        bar_in[b] = nc.dram_tensor(f"bar{b}_in", [1, 4], F32)
        bar_out[b] = nc.dram_tensor(f"bar{b}_out", [NCORES, 4], F32, addr_space="Shared")

    with ExitStack() as st:
        E = st.enter_context

        def sbt(name, shape, dtype=F32):
            return E(nc.sbuf_tensor(name, shape, dtype))

        ARENA = sbt("ARENA", [128, 24576], F32)
        SCR = sbt("SCR", [128, 4096], F32)
        ar = ARENA[:]
        xb = [ar[:, 0:4096], ar[:, 4096:8192]]
        qf = ar[:, 8192:12288]
        ff = ar[:, 12288:16384]
        Xf = ar[:, 16384:20480]
        y1f = ar[:, 20480:24576]
        sc_ = SCR[:]
        gp1 = sc_[:, 0:512]
        Apr = sc_[:, 512:1024]
        Em = sc_[:, 1024:1536]
        Ep = sc_[:, 1536:2048]
        ktf = sc_[:, 2048:2560]
        tst = sc_[:, 2560:3072]
        tst2 = sc_[:, 3072:3584]
        gps = sc_[:, 3584:4096]

        wbuf = [sbt(f"wbuf{i}", [128, 8192], BF16) for i in range(2)]
        xln_bf = sbt("xln_bf", [128, 4096], BF16)
        o_bf = sbt("o_bf", [128, 4096], BF16)
        v_bf = sbt("v_bf", [C, D], BF16)
        qt_bfp = [sbt(f"qt_bf{i}", [128, 8 * C], BF16) for i in range(2)]
        kt_bfp = [sbt(f"kt_bf{i}", [128, 8 * C], BF16) for i in range(2)]
        kh_bf = sbt("kh_bf", [128, 8 * C], BF16)
        khT_bf = sbt("khT_bf", [C, 8 * 128], BF16)
        scm_bf = sbt("scm_bf", [C, 8 * C], BF16)
        S_f = sbt("S_f", [128, 8 * 128], F32)
        S_bf = sbt("S_bf", [128, 8 * 128], BF16)
        BC = sbt("BC", [128, 2048], F32)
        mb, rb = BC[:, 0:512], BC[:, 512:1024]
        ab, bb = BC[:, 1024:1536], BC[:, 1536:2048]
        sqst = sbt("sqst", [128, 512], F32)
        dcolp = [sbt(f"dcol{i}", [128, 8], F32) for i in range(2)]
        dcol = dcolp[0]
        cc8 = sbt("cc8", [128, 4], F32)
        eye_bf = sbt("eye_bf", [128, 128], BF16)
        mask_c = sbt("mask_c", [C, C], mybir.dt.uint8)
        zrs = sbt("zrs", [128, C], F32)
        nw_c = sbt("nw_c", [128, IC], F32)
        ones_col = sbt("ones_col", [128, 1], F32)
        ones_row = sbt("ones_row", [1, 128], F32)
        rows = sbt("rows", [1, 1536], F32)
        runmax = sbt("runmax", [128, 1], F32)
        scal = sbt("scal", [1, 64], F32)
        scol = sbt("scol", [128, 12], F32)
        dq_nat = SCR[:, 0:128]
        dq_s4 = SCR[:, 256:260]
        dq_r4 = SCR[:, 260:264]
        dq_t = SCR[:, 128:256]
        dq_bf = SCR[:].bitcast(BF16)[:, 528:656]
        dq_st = SCR[:].bitcast(BF16)[:, 656:784]
        maxst = sbt("maxst", [128, 96], F32)

        pb_mm = [E(nc.psum_tensor(f"pb_mm{i}", [128, 512], F32)) for i in range(2)]
        pb_o = E(nc.psum_tensor("pb_o", [128, 512], F32))
        pb_sc = E(nc.psum_tensor("pb_sc", [128, 512], F32))
        pb_tr = E(nc.psum_tensor("pb_tr", [128, 1024], BF16))
        pb_st = E(nc.psum_tensor("pb_st", [128, 1024], F32))
        pb_row = E(nc.psum_tensor("pb_row", [1, 512], F32))

        SQ = Q(E(nc.semaphore("sS")))
        GQ = Q(E(nc.semaphore("sG")))
        TQ = Q(E(nc.semaphore("sT")))
        VQ = Q(E(nc.semaphore("sV")))
        AQ = Q(E(nc.semaphore("sA")))
        ccs = E(nc.semaphore("ccs"))
        cc_cnt = [0]
        ALLQ = [SQ, GQ, TQ, VQ, AQ]

        def barrier_all():
            toks = [q.tok() for q in ALLQ]
            for q in ALLQ:
                for t in toks:
                    if t[0] is not q.sem:
                        q.wait(t)

        def mm(out_ap, lhsT_ap, rhs_ap, start, stop, inc=0):
            return TQ.run(lambda e, o=out_ap, l=lhsT_ap, r=rhs_ap, a=start, b=stop:
                          e.matmul(o, l, r, start=a, stop=b), inc=inc)

        def tr(out_ap, in_ap, inc=1):
            return TQ.run(lambda e, o=out_ap, i=in_ap, d=eye_bf[:]:
                          e.transpose(o, i, d), inc=inc)

        def act(q, out_ap, in_ap, func, bias=0.0, scale=1.0, accum=None, inc=1):
            return q.run(lambda e, o=out_ap, i=in_ap, f=func, b=bias, s=scale, a=accum:
                         e.activation(o, i, f, bias=b, scale=s, accum_out=a), inc=inc)

        def tt(q, out_ap, a_ap, b_ap, op, inc=1):
            return q.run(lambda e, o=out_ap, a=a_ap, b=b_ap, p=op:
                         e.tensor_tensor(o, a, b, p), inc=inc)

        def ts(q, out_ap, in_ap, s1, s2, op0, op1=ALU.bypass, inc=1):
            return q.run(lambda e, o=out_ap, i=in_ap, a=s1, b=s2, p0=op0, p1=op1:
                         e.tensor_scalar(o, i, a, b, p0, p1), inc=inc)

        def stt(q, out_ap, in0_ap, scalar, in1_ap, op0, op1, inc=1):
            return q.run(lambda e, o=out_ap, i=in0_ap, s=scalar, j=in1_ap, p0=op0, p1=op1:
                         e.scalar_tensor_tensor(o, i, s, j, p0, p1), inc=inc)

        def cp(q, out_ap, in_ap, inc=1):
            def fn(e, o=out_ap, i=in_ap):
                if hasattr(e, "tensor_copy"):
                    return e.tensor_copy(out=o, in_=i)
                return e.copy(o, i)
            return q.run(fn, inc=inc)

        def red(q, out_ap, in_ap, op, absv=False, inc=1):
            return q.run(lambda e, o=out_ap, i=in_ap, p=op, a=absv:
                         e.tensor_reduce(o, i, axis=AX.X, op=p, apply_absolute_value=a),
                         inc=inc)

        def scan_op(q, out_ap, d0, d1, init, op0, op1, inc=1):
            return q.run(lambda e, o=out_ap, a=d0, b=d1, i=init, p0=op0, p1=op1:
                         e.tensor_tensor_scan(o, a, b, i, p0, p1), inc=inc)

        def dma(q, out_ap, in_ap):
            return q.run(lambda e, o=out_ap, i=in_ap: e.dma_start(out=o, in_=i), inc=16)

        def memset(q, ap_, val, inc=1):
            return q.run(lambda e, a=ap_, v=val: e.memset(a, v), inc=inc)

        def select(q, out_ap, mask_ap, t_ap, f_ap, inc=1):
            return q.run(lambda e, o=out_ap, m=mask_ap, a=t_ap, b=f_ap:
                         e.select(o, m, a, b), inc=inc)

        def recip(q, out_ap, in_ap, inc=1):
            return q.run(lambda e, o=out_ap, i=in_ap: e.reciprocal(o, i), inc=inc)

        def vfence():
            t = VQ.tok()
            GQ.wait(t)
            t2 = GQ.run(lambda e: e.engine_nop(), inc=1)
            VQ.wait(t2)

        def allgather(in_t, out_t):
            cc_cnt[0] += 1
            val = cc_cnt[0]
            GQ.ops.append(lambda e, i=in_t, o=out_t, s=ccs:
                          e.collective_compute("AllGather", ALU.bypass,
                                               replica_groups=[list(range(NCORES))],
                                               ins=[i.ap().opt()], outs=[o.ap().opt()])
                          .then_inc(s, 1))
            return (ccs, val)

        def rrow(dst, src, eps_col):
            act(AQ, dst, src, AF.Ln, bias=cc8[0:1, eps_col:eps_col + 1])
            return act(AQ, dst, dst, AF.Exp, scale=cc8[0:1, 2:3])

        bank_prod = {}
        bank_cons = {}

        def bank_write_begin(bk):
            for t in bank_cons.get(bk, []):
                TQ.wait(t)
            bank_cons[bk] = []

        def bank_write_end(bk):
            bank_prod[bk] = TQ.tok()

        def bank_read(bk, q):
            q.wait(bank_prod.get(bk))

        def bank_read_done(bk, q):
            bank_cons.setdefault(bk, []).append(q.tok())

        def bcast_row(row_ap, dst_ap, src_tok):
            bank_write_begin("sc")
            TQ.wait(src_tok)
            mm(pb_sc[:, 0:512], ones_row[:], row_ap, True, True, inc=1)
            bank_write_end("sc")
            bank_read("sc", VQ)
            t = cp(VQ, dst_ap, pb_sc[:, 0:512])
            bank_read_done("sc", VQ)
            return t

        t_eye = dma(GQ, eye_bf[:], eye_h)
        t_mask = dma(SQ, mask_c[:], mask_h)
        t_nw = dma(SQ, nw_c[:], nwfm)
        memset(VQ, zrs[:], 0.0)
        memset(VQ, ones_col[:], 1.0)
        memset(VQ, ones_row[:], 1.0)
        memset(VQ, S_f[:], 0.0)
        memset(VQ, S_bf[:], 0.0)
        memset(VQ, runmax[:], 0.0)
        memset(VQ, cc8[:, 0:1], 1e-5)
        memset(VQ, cc8[:, 1:2], 1e-6)
        memset(VQ, cc8[:, 2:3], -0.5)

        t_wq = dma(GQ, wq_bf_d.ap(), wqT)
        t_wk = dma(GQ, wk_bf_d.ap(), wkT)
        t_wv = dma(GQ, wv_bf_d.ap(), wvT)
        t_wo = None
        for half in range(2):
            rsl = slice(half * 512, (half + 1) * 512)
            tl = dma(SQ, xb[0].rearrange("p (a b) -> p a b", a=IC),
                     woT.rearrange("(a p) c -> p a c", p=128)[:, :, rsl])
            VQ.waits(tl, t_nw)
            for a in range(IC):
                ts(VQ, xln_bf[:, a * 512:(a + 1) * 512],
                   xb[0][:, a * 512:(a + 1) * 512], nw_c[:, a:a + 1], None, ALU.mult)
            SQ.wait(VQ.tok())
            t_wo = dma(SQ, wo_bf_d.ap().rearrange("(a p) c -> p a c", p=128)[:, :, rsl],
                       xln_bf[:].rearrange("p (a b) -> p a b", a=IC))
            VQ.wait(t_wo)
        t_wcast = (t_wq, t_wk, t_wv, t_wo)

        GQ.wait(t_eye)
        TQ.wait(t_eye)
        tile_row = 0
        t_last_nat_read = None
        t_last_st = None
        for tgroup, base_off, ostride in (("up", 0, H), ("gate", EX_UP, H),
                                          ("down", EX_UP + EX_GATE, D)):
            for j in range(OC3):
                SQ.wait(t_last_nat_read)
                tl = dma(SQ, dq_nat[:], dq_sl[tile_row * 128:(tile_row + 1) * 128, :])
                tile_row += 1
                VQ.wait(tl)
                red(VQ, dq_s4[:], dq_nat[:].rearrange("p (w k) -> p w k", k=32),
                    ALU.max, absv=True)
                vfence()
                ts(VQ, dq_s4[:], dq_s4[:], 1.0 / 7.0, 1e-12, ALU.mult, ALU.max)
                vfence()
                recip(VQ, dq_r4[:], dq_s4[:])
                vfence()
                t_last_nat_read = tt(VQ, dq_t[:].rearrange("p (w k) -> p w k", k=32),
                                     dq_nat[:].rearrange("p (w k) -> p w k", k=32),
                                     dq_r4[:, :, None].broadcast_to([128, 4, 32]), ALU.mult)
                cp(VQ, dq_t[:].bitcast(I32), dq_t[:])
                cp(VQ, dq_t[:], dq_t[:].bitcast(I32))
                t_dq = tt(VQ, dq_bf[:].rearrange("p (w k) -> p w k", k=32),
                          dq_t[:].rearrange("p (w k) -> p w k", k=32),
                          dq_s4[:, :, None].broadcast_to([128, 4, 32]), ALU.mult)
                bank_write_begin("tr")
                TQ.wait(t_dq)
                t_tp = tr(pb_tr[:, 0:128], dq_bf[:])
                bank_write_end("tr")
                VQ.wait(t_tp)
                bank_read("tr", AQ)
                AQ.wait(t_last_st)
                t_ev = cp(AQ, dq_st[:], pb_tr[:, 0:128])
                bank_read_done("tr", AQ)
                SQ.wait(t_ev)
                if tgroup == "down":
                    icl, hc = j // IC, j % IC
                    off = base_off + icl * (128 * D) + hc * 128
                else:
                    off = base_off + j * 128
                t_last_st = SQ.run(
                    lambda e, o=bass.AP(ex_in, off, [[ostride, 128], [1, 128]]),
                    i=dq_st[:]: e.dma_start(out=o, in_=i), inc=16)
        GQ.wait(SQ.tok())
        t_gather = allgather(ex_in, ex_out)

        def load_x(m):
            tok0 = m * MAC
            return dma(SQ, xb[m % 2].rearrange("p (a b) -> p a b", a=IC),
                       bass.AP(xT.tensor, tok0, [[TOT, 128], [128 * TOT, IC], [1, MAC]]))

        def load_w(dram, slot, after_tok):
            SQ.wait(after_tok)
            return dma(SQ, wbuf[slot][:].rearrange("p (a b) -> p a b", a=IC),
                       dram.ap().rearrange("(a p) c -> p a c", p=128))

        def row_stats(src_fn, src_tok):
            bank_write_begin("row")
            TQ.wait(src_tok)
            for i in range(IC):
                mm(pb_row[:, 0:512], ones_col[:], src_fn(i),
                   start=(i == 0), stop=(i == IC - 1), inc=(1 if i == IC - 1 else 0))
            bank_write_end("row")
            bank_read("row", VQ)
            cp(VQ, rows[:, 0:512], pb_row[:, 0:512])
            bank_read_done("row", VQ)
            sq_mm_tok = [None, None]
            for i in range(IC):
                buf = sqst[:] if i % 2 == 0 else tst
                AQ.wait(sq_mm_tok[i % 2])
                t_sq = act(AQ, buf, src_fn(i), AF.Square)
                bank_write_begin("row")
                TQ.wait(t_sq)
                mm(pb_row[:, 0:512], ones_col[:], buf,
                   start=(i == 0), stop=(i == IC - 1), inc=1)
                bank_write_end("row")
                sq_mm_tok[i % 2] = TQ.tok()
            bank_read("row", VQ)
            cp(VQ, rows[:, 512:1024], pb_row[:, 0:512])
            bank_read_done("row", VQ)

        SQ.waits(*t_wcast)
        SQ.wait(GQ.tok())
        x_tok = [None, None]
        x_tok[0] = load_x(0)
        w_tok = {}

        for m in range(NMAC):
            is_halo = (m == 0)
            par = m % 2
            x_cur = xb[par]
            tok0 = m * MAC
            rtok0 = tok0 - HALO

            w_tok["wk"] = load_w(wk_bf_d, 0, None)
            w_tok["wq"] = load_w(wq_bf_d, 1, None)
            if m + 1 < NMAC:
                x_tok[(m + 1) % 2] = load_x(m + 1)

            VQ.wait(x_tok[par])
            AQ.wait(x_tok[par])

            row_stats(lambda i: x_cur[:, i * 512:(i + 1) * 512], x_tok[par])
            ts(VQ, rows[:, 0:512], rows[:, 0:512], 1.0 / D, None, ALU.mult)
            ts(VQ, rows[:, 512:1024], rows[:, 512:1024], 1.0 / D, None, ALU.mult)
            tt(VQ, rows[:, 1024:1536], rows[:, 0:512], rows[:, 0:512], ALU.mult)
            t_v = tt(VQ, rows[:, 512:1024], rows[:, 512:1024], rows[:, 1024:1536],
                     ALU.subtract)
            AQ.wait(t_v)
            t_r = rrow(rows[:, 512:1024], rows[:, 512:1024], 0)
            VQ.wait(t_r)
            bcast_row(rows[:, 0:512], mb, VQ.tok())
            t_bc = bcast_row(rows[:, 512:1024], rb, t_r)
            for i in range(IC):
                tt(VQ, tst, x_cur[:, i * 512:(i + 1) * 512], mb, ALU.subtract)
                tt(VQ, xln_bf[:, i * 512:(i + 1) * 512], tst, rb, ALU.mult)
            t_xln = VQ.tok()

            def proj_fm(wtok, wslot, out_f32, fn):
                TQ.waits(wtok, t_xln)
                for oc in range(8):
                    bk = "mm%d" % (oc % 2)
                    pb = pb_mm[oc % 2]
                    bank_write_begin(bk)
                    for ic in range(IC):
                        mm(pb[:], wbuf[wslot][:, ic * 1024 + oc * 128: ic * 1024 + (oc + 1) * 128],
                           xln_bf[:, ic * 512:(ic + 1) * 512],
                           start=(ic == 0), stop=(ic == IC - 1), inc=(1 if ic == IC - 1 else 0))
                    bank_write_end(bk)
                    bank_read(bk, AQ)
                    act(AQ, out_f32[:, oc * 512:(oc + 1) * 512], pb[:], fn)
                    bank_read_done(bk, AQ)
                return AQ.tok()

            t_f_done = proj_fm(w_tok["wk"], 0, ff, AF.Sigmoid)
            w_tok["wv"] = load_w(wv_bf_d, 0, TQ.tok())
            if not is_halo:
                t_q_done = proj_fm(w_tok["wq"], 1, qf, AF.Silu)
            else:
                t_q_done = AQ.tok()
            w_tok["wo"] = load_w(wo_bf_d, 1, TQ.tok())

            pch = {"scan": None, "ep": None, "kh": None, "tr": None, "omm": [None, None],
                   "stMM": None, "sup": None, "sbf": None}
            pend_state = None
            for ch in range(NCH):
                par_c = ch % 2
                qt_bf = qt_bfp[par_c]
                kt_bf = kt_bfp[par_c]
                dcol = dcolp[par_c]
                c0 = ch * C
                fview = ff[:].rearrange("p (s t) -> p s t", s=8)[:, :, c0:c0 + C]
                qview = qf[:].rearrange("p (s t) -> p s t", s=8)[:, :, c0:c0 + C]
                AQ.wait(t_f_done)
                AQ.wait(pch["scan"])
                act(AQ, gp1[:].rearrange("p (s t) -> p s t", s=8), fview, AF.Exp, scale=-1.0)
                t_g = act(AQ, gps[:], gp1[:], AF.Ln, bias=1.0)
                VQ.waits(t_g, pch["ep"])
                for sl in range(8):
                    scan_op(VQ, Apr[:, sl * C:(sl + 1) * C], gps[:, sl * C:(sl + 1) * C],
                            zrs[:], 0.0, ALU.add, ALU.bypass)
                t_scan = VQ.tok()
                pch["scan"] = t_scan
                AQ.waits(t_scan, pch["kh"], pch["sup"])
                cp(AQ, dcol[:, 0:8],
                   Apr[:].rearrange("p (s t) -> p s t", s=8)[:, :, C - 1:C].opt())
                act(AQ, dcol[:, 0:8], dcol[:, 0:8], AF.Exp, scale=-1.0)
                act(AQ, Em[:], Apr[:], AF.Exp, scale=-1.0)
                t_ep = act(AQ, Ep[:], Apr[:], AF.Exp)
                pch["ep"] = t_ep
                VQ.waits(t_ep, pch["omm"][par_c])
                if not is_halo:
                    tt(VQ, qt_bf[:].rearrange("p (s t) -> p s t", s=8), qview,
                       Em[:].rearrange("p (s t) -> p s t", s=8), ALU.mult)
                stt(VQ, tst2[:].rearrange("p (s t) -> p s t", s=8), fview, -1.0,
                    Ep[:].rearrange("p (s t) -> p s t", s=8), ALU.mult, ALU.mult)
                tt(VQ, ktf[:], tst2[:], Ep[:], ALU.add)
                if not is_halo:
                    cp(VQ, kt_bf[:], ktf[:])
                VQ.wait(pch["tr"])
                for sl in range(8):
                    ts(VQ, kh_bf[:, sl * C:(sl + 1) * C], ktf[:, sl * C:(sl + 1) * C],
                       dcol[:, sl:sl + 1], None, ALU.mult)
                t_kh = VQ.tok()
                pch["kh"] = t_kh

                if pend_state is not None:
                    pv_par, pv_dcol = pend_state
                    bank_read("st", VQ)
                    for sl in range(8):
                        stt(VQ, S_f[:, sl * 128:(sl + 1) * 128], S_f[:, sl * 128:(sl + 1) * 128],
                            pv_dcol[:, sl:sl + 1], pb_st[:, sl * 128:(sl + 1) * 128],
                            ALU.mult, ALU.add)
                    t_sup = VQ.tok()
                    pch["sup"] = t_sup
                    bank_read_done("st", VQ)
                    AQ.wait(t_sup)
                    t_sbf = cp(AQ, S_bf[:], S_f[:])
                    pch["sbf"] = t_sbf
                    pend_state = None

                TQ.wait(w_tok["wv"])
                for hb in range(2):
                    bk = "mm%d" % hb
                    bank_write_begin(bk)
                    for ic in range(IC):
                        mm(pb_mm[hb][0:C, :],
                           xln_bf[:, ic * 512 + c0: ic * 512 + c0 + C],
                           wbuf[0][:, ic * 1024 + hb * 512: ic * 1024 + (hb + 1) * 512],
                           start=(ic == 0), stop=(ic == IC - 1), inc=(1 if ic == IC - 1 else 0))
                    bank_write_end(bk)
                    bank_read(bk, AQ)
                    act(AQ, v_bf[:, hb * 512:(hb + 1) * 512], pb_mm[hb][0:C, :], AF.Identity)
                    bank_read_done(bk, AQ)
                t_vev = AQ.tok()

                bank_write_begin("tr")
                TQ.wait(t_kh)
                for sl in range(8):
                    tr(pb_tr[0:C, sl * 128:(sl + 1) * 128], kh_bf[:, sl * C:(sl + 1) * C],
                       inc=(1 if sl == 7 else 0))
                bank_write_end("tr")
                pch["tr"] = TQ.tok()
                bank_read("tr", AQ)
                AQ.wait(t_vev)
                t_khT = cp(AQ, khT_bf[:], pb_tr[0:C, :])
                bank_read_done("tr", AQ)

                if not is_halo:
                    bank_write_begin("sc")
                    TQ.wait(t_kh)
                    for sl in range(8):
                        mm(pb_sc[0:C, sl * C:(sl + 1) * C], kt_bf[:, sl * C:(sl + 1) * C],
                           qt_bf[:, sl * C:(sl + 1) * C], start=True, stop=True,
                           inc=(1 if sl == 7 else 0))
                    bank_write_end("sc")
                    bank_read("sc", VQ)
                    VQ.wait(t_mask)
                    t_scm = select(VQ, scm_bf[:].rearrange("p (s t) -> p s t", s=8),
                                   mask_c[:, None, :].broadcast_to([C, 8, C]),
                                   pb_sc[0:C, :].rearrange("p (s t) -> p s t", s=8),
                                   zrs[0:C, None, 0:C].broadcast_to([C, 8, C]))
                    bank_read_done("sc", VQ)

                    bank_write_begin("o")
                    TQ.waits(t_scm, t_vev, t_khT, pch["sbf"])
                    for sl in range(8):
                        mm(pb_o[:, sl * C:(sl + 1) * C], v_bf[:, sl * 128:(sl + 1) * 128],
                           scm_bf[:, sl * C:(sl + 1) * C], start=True, stop=False)
                        mm(pb_o[:, sl * C:(sl + 1) * C], S_bf[:, sl * 128:(sl + 1) * 128],
                           qt_bf[:, sl * C:(sl + 1) * C], start=False, stop=True,
                           inc=(1 if sl == 7 else 0))
                    bank_write_end("o")
                    pch["omm"][par_c] = TQ.tok()
                else:
                    pch["omm"][par_c] = TQ.tok()

                bank_write_begin("st")
                TQ.waits(t_khT, t_vev, pch["sup"])
                for sl in range(8):
                    mm(pb_st[:, sl * 128:(sl + 1) * 128], khT_bf[:, sl * 128:(sl + 1) * 128],
                       v_bf[:, sl * 128:(sl + 1) * 128], start=True, stop=True,
                       inc=(1 if sl == 7 else 0))
                bank_write_end("st")
                pch["stMM"] = TQ.tok()
                pend_state = (par_c, dcol)

                if not is_halo:
                    oview = o_bf[:].rearrange("p (s t) -> p s t", s=8)[:, :, c0:c0 + C]
                    bank_read("o", AQ)
                    act(AQ, sqst[:], pb_o[:], AF.Square)
                    act(AQ, oview, pb_o[:].rearrange("p (s t) -> p s t", s=8), AF.Identity)
                    bank_read_done("o", AQ)
                    bank_write_begin("row")
                    TQ.wait(AQ.tok())
                    for sl in range(8):
                        mm(pb_row[:, c0:c0 + C], ones_col[:], sqst[:, sl * C:(sl + 1) * C],
                           start=(sl == 0), stop=(sl == 7), inc=(1 if sl == 7 else 0))
                    bank_write_end("row")
                    AQ.wait(TQ.tok())

            pv_par, pv_dcol = pend_state
            bank_read("st", VQ)
            for sl in range(8):
                stt(VQ, S_f[:, sl * 128:(sl + 1) * 128], S_f[:, sl * 128:(sl + 1) * 128],
                    pv_dcol[:, sl:sl + 1], pb_st[:, sl * 128:(sl + 1) * 128],
                    ALU.mult, ALU.add)
            t_sup = VQ.tok()
            bank_read_done("st", VQ)
            AQ.wait(t_sup)
            t_sbf = cp(AQ, S_bf[:], S_f[:])
            TQ.wait(t_sbf)
            pend_state = None

            if not is_halo:
                c_row = mb[0:1, 0:512]
                bank_read("row", VQ)
                cp(VQ, c_row, pb_row[:, 0:512])
                bank_read_done("row", VQ)
                t_ms = ts(VQ, c_row, c_row, 1.0 / D, None, ALU.mult)
                AQ.wait(t_ms)
                t_c = rrow(c_row, c_row, 1)
                VQ.wait(t_c)

                TQ.waits(w_tok["wo"], AQ.tok())
                for dc in range(IC):
                    bk = "mm%d" % (dc % 2)
                    pb = pb_mm[dc % 2]
                    bank_write_begin(bk)
                    for kc in range(IC):
                        mm(pb[:], wbuf[1][:, kc * 1024 + dc * 128: kc * 1024 + (dc + 1) * 128],
                           o_bf[:, kc * 512:(kc + 1) * 512],
                           start=(kc == 0), stop=(kc == IC - 1), inc=(1 if kc == IC - 1 else 0))
                    bank_write_end(bk)
                    bank_read(bk, AQ)
                    act(AQ, Xf[:, dc * 512:(dc + 1) * 512], pb[:], AF.Identity)
                    bank_read_done(bk, AQ)
                t_X = AQ.tok()

                VQ.wait(t_X)
                row_stats(lambda i: Xf[:, i * 512:(i + 1) * 512], t_X)
                ts(VQ, rows[:, 0:512], rows[:, 0:512], 1.0 / D, None, ALU.mult)
                ts(VQ, rows[:, 512:1024], rows[:, 512:1024], 1.0 / D, None, ALU.mult)
                tt(VQ, rows[:, 1024:1536], rows[:, 0:512], rows[:, 0:512], ALU.mult)
                tt(VQ, rows[:, 512:1024], rows[:, 512:1024], rows[:, 1024:1536], ALU.subtract)
                tt(VQ, rows[:, 1024:1536], c_row, c_row, ALU.mult)
                t_cv = tt(VQ, rows[:, 512:1024], rows[:, 512:1024], rows[:, 1024:1536], ALU.mult)
                AQ.wait(t_cv)
                t_r3 = rrow(rows[:, 512:1024], rows[:, 512:1024], 0)
                VQ.wait(t_r3)
                tt(VQ, rows[:, 1024:1536], c_row, rows[:, 512:1024], ALU.mult)
                tt(VQ, rows[:, 0:512], rows[:, 0:512], rows[:, 1024:1536], ALU.mult)
                ts(VQ, rows[:, 0:512], rows[:, 0:512], -1.0, None, ALU.mult)
                bcast_row(rows[:, 1024:1536], ab, VQ.tok())
                bcast_row(rows[:, 0:512], bb, VQ.tok())

                for dc in range(IC):
                    tt(VQ, tst, Xf[:, dc * 512:(dc + 1) * 512], ab, ALU.mult)
                    tt(VQ, tst2, tst, bb, ALU.add)
                    tt(VQ, y1f[:, dc * 512:(dc + 1) * 512], tst2,
                       x_cur[:, dc * 512:(dc + 1) * 512], ALU.add)
                t_y1 = VQ.tok()
                SQ.wait(t_y1)
                dma(SQ, bass.AP(y1_d, rtok0, [[NTOK, 128], [128 * NTOK, IC], [1, MAC]]),
                    y1f[:].rearrange("p (a b) -> p a b", a=IC))

                VQ.wait(t_y1)
                AQ.wait(t_y1)
                row_stats(lambda i: y1f[:, i * 512:(i + 1) * 512], t_y1)
                ts(VQ, rows[:, 0:512], rows[:, 0:512], 1.0 / D, None, ALU.mult)
                ts(VQ, rows[:, 512:1024], rows[:, 512:1024], 1.0 / D, None, ALU.mult)
                tt(VQ, rows[:, 1024:1536], rows[:, 0:512], rows[:, 0:512], ALU.mult)
                t_v2 = tt(VQ, rows[:, 512:1024], rows[:, 512:1024], rows[:, 1024:1536],
                          ALU.subtract)
                AQ.wait(t_v2)
                t_r2 = rrow(rows[:, 512:1024], rows[:, 512:1024], 0)
                VQ.wait(t_r2)
                mi = m - 1
                SQ.wait(VQ.tok())
                dma(SQ, bass.AP(ln2_d, mi * MAC, [[1, 1], [1, MAC]]), rows[0:1, 0:512])
                dma(SQ, bass.AP(ln2_d, NTOK + mi * MAC, [[1, 1], [1, MAC]]),
                    rows[0:1, 512:1024])
                bcast_row(rows[:, 0:512], mb, VQ.tok())
                bcast_row(rows[:, 512:1024], rb, VQ.tok())
                for dc in range(IC):
                    tt(VQ, tst, y1f[:, dc * 512:(dc + 1) * 512], mb, ALU.subtract)
                    t_x2 = tt(VQ, tst2, tst, rb, ALU.mult)
                    SQ.wait(t_x2)
                    t_st2 = dma(SQ, bass.AP(xln2_d, rtok0 + dc * 128 * NTOK,
                                            [[NTOK, 128], [1, MAC]]), tst2)
                    red(VQ, maxst[:, (m - 1) * 8 + dc:(m - 1) * 8 + dc + 1], tst2,
                        ALU.max, absv=True)
                    VQ.wait(t_st2)

            if DEBUG_DUMP and m == 1:
                barrier_all()
                cp(VQ, tst, S_f[:, 0:512])
                cp(VQ, tst2, S_f[:, 512:1024])
                t_d1 = VQ.tok()
                SQ.wait(t_d1)
                dma(SQ, bass.AP(dbg_S.tensor, 0, [[1024, 128], [1, 512]]), tst)
                dma(SQ, bass.AP(dbg_S.tensor, 512, [[1024, 128], [1, 512]]), tst2)
                VQ.wait(SQ.tok())
                cp(VQ, tst, pb_st[:, 0:512])
                cp(VQ, tst2, pb_st[:, 512:1024])
                t_d1b = VQ.tok()
                SQ.wait(t_d1b)
                dma(SQ, bass.AP(dbg_pst.tensor, 0, [[1024, 128], [1, 512]]), tst)
                dma(SQ, bass.AP(dbg_pst.tensor, 512, [[1024, 128], [1, 512]]), tst2)
                cp(VQ, SCR[:, 0:512], qt_bf[:])
                cp(VQ, SCR[:, 512:1024], kh_bf[:])
                cp(VQ, SCR[:, 1024:1536].rearrange("p (a b) -> p a b", a=8)[0:64],
                   scm_bf[:].rearrange("p (a b) -> p a b", a=8))
                t_d2 = VQ.tok()
                SQ.wait(t_d2)
                dma(SQ, dbg_qt, SCR[:, 0:512])
                dma(SQ, dbg_kh, SCR[:, 512:1024])
                dma(SQ, dbg_dc, dcol[:])
                dma(SQ, bass.AP(dbg_scm.tensor, 0, [[512, 64], [1, 512]]), SCR[0:64, 1024:1536])
                for dcb in range(IC):
                    cp(VQ, tst, o_bf[:, dcb * 512:(dcb + 1) * 512])
                    t_d3 = VQ.tok()
                    SQ.wait(t_d3)
                    dma(SQ, bass.AP(dbg_o.tensor, dcb * 512, [[4096, 128], [1, 512]]), tst)
                    VQ.wait(SQ.tok())
            barrier_all()

        def finalize_scalar(run_ap, ncols, bar, post_scale, result_cols):
            SQ.wait(VQ.tok())
            for c_ in range(ncols):
                dma(SQ, bass.AP(barsc, c_ * 128, [[1, 1], [1, 128]]),
                    run_ap[:, c_:c_ + 1])
            t_all = SQ.tok()
            SQ.wait(t_all)
            VQ.wait(t_all)
            for c_ in range(ncols):
                t_bl = dma(SQ, mb[0:1, 0:128], bass.AP(barsc, c_ * 128, [[1, 1], [1, 128]]))
                VQ.wait(t_bl)
                t_rd = red(VQ, scal[0:1, 8 + c_:9 + c_], mb[0:1, 0:128], ALU.max)
                SQ.wait(t_rd)
            vfence()
            t_red = VQ.tok()
            SQ.wait(t_red)
            t_bi = dma(SQ, bass.AP(bar_in[bar], 0, [[1, 1], [1, ncols]]),
                       scal[0:1, 8:8 + ncols])
            GQ.wait(t_bi)
            cc_cnt[0] += 1
            ccv = cc_cnt[0]
            GQ.ops.append(lambda e, i=bar_in[bar], o=bar_out[bar], s=ccs:
                          e.collective_compute("AllGather", ALU.bypass,
                                               replica_groups=[list(range(NCORES))],
                                               ins=[i.ap().opt()], outs=[o.ap().opt()])
                          .then_inc(s, 1))
            GQ.wait((ccs, ccv))
            t_bo = dma(GQ, scal[0:1, 32:64], bass.AP(bar_out[bar], 0, [[1, 1], [1, 32]]))
            VQ.wait(t_bo)
            for j, rc in enumerate(result_cols):
                red(VQ, scal[0:1, j:j + 1],
                    scal[0:1, 32:64].rearrange("p (a b) -> p a b", a=8)[:, :, j:j + 1].opt(),
                    ALU.max)
                vfence()
                ts(VQ, scal[0:1, j:j + 1], scal[0:1, j:j + 1], post_scale, 1e-12,
                   ALU.mult, ALU.max)
                vfence()
                recip(VQ, scal[0:1, 16 + j:17 + j], scal[0:1, j:j + 1])
                vfence()
                for src, dstc in ((scal[0:1, j:j + 1], rc), (scal[0:1, 16 + j:17 + j], rc + 1)):
                    bank_write_begin("sc")
                    TQ.wait(VQ.tok())
                    mm(pb_sc[:, 0:1], ones_row[:], src, True, True, inc=1)
                    bank_write_end("sc")
                    bank_read("sc", VQ)
                    cp(VQ, scol[:, dstc:dstc + 1], pb_sc[:, 0:1])
                    bank_read_done("sc", VQ)

        vfence()
        red(VQ, runmax[:], maxst[:, 0:64], ALU.max)
        vfence()
        finalize_scalar(runmax[:], 1, "A", 1.0 / 32767.0, [0])

        upW = ARENA[:].bitcast(BF16)[:, 0:24576]
        gateW = ARENA[:].bitcast(BF16)[:, 24576:49152]
        GQ.wait(t_gather)
        SQ.wait(GQ.tok())
        wload = []
        for icx in range(IC):
            wload.append(dma(SQ, upW[:, icx * H:(icx + 1) * H],
                             bass.AP(ex_out, icx * NEX, [[H, 128], [1, H]])))
            wload.append(dma(SQ, gateW[:, icx * H:(icx + 1) * H],
                             bass.AP(ex_out, icx * NEX + EX_UP, [[H, 128], [1, H]])))
        t_wl = SQ.tok()
        TQ.wait(t_wl)
        xln2f = SCR[:]
        i32v = wbuf[0][:].bitcast(I32)
        codebf = xln_bf[:]
        memset(VQ, maxst[:], 0.0)
        zst = [wbuf[1][:].bitcast(F32)[:, 0:2048], wbuf[1][:].bitcast(F32)[:, 2048:4096]]
        zst_cons = [None, None]
        memset(VQ, dcol[:], 0.0)
        for mi in range(NRMAC):
            rt0 = mi * MAC
            t_x2l = dma(SQ, xln2f.rearrange("p (a b) -> p a b", a=IC),
                        bass.AP(xln2_d, rt0, [[NTOK, 128], [128 * NTOK, IC], [1, MAC]]))
            VQ.wait(t_x2l)
            ts(VQ, i32v[:, 0:4096], xln2f, scol[:, 1:2], None, ALU.mult)
            t_code = cp(VQ, codebf, i32v[:, 0:4096])
            TQ.wait(t_code)
            for tix, (Wv_, zd, runc) in enumerate(((upW, zup_d, 0), (gateW, zgate_d, 1))):
                for g in range(6):
                    par_ = g % 2
                    zstage = zst[par_]
                    for t_ in (zst_cons[par_] or []):
                        AQ.wait(t_)
                    for oc4 in range(4):
                        oc = g * 4 + oc4
                        bk = "mm%d" % (oc % 2)
                        pb = pb_mm[oc % 2]
                        bank_write_begin(bk)
                        for icx in range(IC):
                            mm(pb[:], Wv_[:, icx * H + oc * 128: icx * H + (oc + 1) * 128],
                               codebf[:, icx * 512:(icx + 1) * 512],
                               start=(icx == 0), stop=(icx == IC - 1),
                               inc=(1 if icx == IC - 1 else 0))
                        bank_write_end(bk)
                        bank_read(bk, AQ)
                        act(AQ, zstage[:, oc4 * 512:(oc4 + 1) * 512], pb[:],
                            AF.Identity, scale=scol[:, 0:1])
                        bank_read_done(bk, AQ)
                    t_ev = AQ.tok()
                    VQ.wait(t_ev)
                    t_rm = red(VQ, maxst[:, tix * 48 + mi * 6 + g: tix * 48 + mi * 6 + g + 1],
                               zstage[:, 0:2048], ALU.max, absv=True)
                    SQ.wait(t_ev)
                    t_zs = dma(SQ, bass.AP(zd, g * 4 * 128 * NTOK + rt0,
                                           [[NTOK, 128], [128 * NTOK, 4], [1, MAC]]),
                               zstage[:, 0:2048].rearrange("p (a b) -> p a b", a=4))
                    zst_cons[par_] = [t_rm, t_zs]
                    if DEBUG_F and mi == 0 and tix == 0 and g == 0:
                        t_dz = dma(SQ, dbg_z, zstage[:, 0:2048])
                        zst_cons[par_].append(t_dz)
            SQ.wait(VQ.tok())
            SQ.wait(TQ.tok())
            VQ.wait(TQ.tok())
        barrier_all()

        vfence()
        if DEBUG_F:
            SQ.wait(VQ.tok())
            t_dmx = dma(SQ, dbg_mx, maxst[:])
            VQ.wait(t_dmx)
        red(VQ, dcol[:, 0:1], maxst[:, 0:48], ALU.max)
        red(VQ, dcol[:, 1:2], maxst[:, 48:96], ALU.max)
        vfence()
        finalize_scalar(dcol[:, 0:2], 2, "B", 1.0 / 32767.0, [2, 4])

        memset(VQ, maxst[:], 0.0)
        zgv = wbuf[1][:].bitcast(F32)
        gq_f = SCR[:, 0:2048]
        fs3 = SCR[:, 2048:4096]
        GW = 4
        i32g = wbuf[0][:].bitcast(I32)
        for mi in range(NRMAC):
            rt0 = mi * MAC
            for g in range(6):
                t_zl = dma(SQ, zgv[:, 0:GW * 512].rearrange("p (a b) -> p a b", a=GW),
                           bass.AP(zgate_d, g * GW * 128 * NTOK + rt0,
                                   [[NTOK, 128], [128 * NTOK, GW], [1, MAC]]))
                VQ.wait(t_zl)
                t_ci = ts(VQ, i32g[:, 0:GW * 512], zgv[:, 0:GW * 512], scol[:, 5:6], None, ALU.mult)
                GQ.wait(t_ci)
                t_gst = dma(GQ, bass.AP(gcode_d, g * GW * 128 * NTOK + rt0,
                                        [[NTOK, 128], [128 * NTOK, GW], [1, MAC]]),
                            i32g[:, 0:GW * 512].rearrange("p (a b) -> p a b", a=GW))
                t_gq = ts(VQ, gq_f, i32g[:, 0:GW * 512], scol[:, 4:5], None, ALU.mult)
                AQ.wait(t_gq)
                t_sg = act(AQ, fs3, gq_f, AF.Sigmoid)
                VQ.wait(t_sg)
                VQ.wait(t_gst)
                ts(VQ, i32g[:, 0:GW * 512], fs3, 65536.0, 65535.0, ALU.mult, ALU.min)
                cp(VQ, fs3, i32g[:, 0:GW * 512])
                t_g2 = tt(VQ, gq_f, gq_f, fs3, ALU.mult)
                red(VQ, maxst[:, mi * 6 + g: mi * 6 + g + 1], gq_f, ALU.max, absv=True)
                AQ.wait(VQ.tok())
                SQ.wait(VQ.tok())
        barrier_all()

        vfence()
        red(VQ, dcol[:, 2:3], maxst[:, 0:48], ALU.max)
        vfence()
        finalize_scalar(dcol[:, 2:3], 1, "C", (1.0 / 65536.0) / 32767.0, [6])
        vfence()
        ts(VQ, scol[:, 10:11], scol[:, 7:8], 1.0 / 65536.0, None, ALU.mult)
        vfence()

        memset(VQ, maxst[:], 0.0)
        for mi in range(NRMAC):
            rt0 = mi * MAC
            for g in range(6):
                base = g * GW * 128 * NTOK + rt0
                apz = [[NTOK, 128], [128 * NTOK, GW], [1, MAC]]
                t_gl = dma(GQ, gq_f.rearrange("p (a b) -> p a b", a=GW),
                           bass.AP(gcode_d, base, apz))
                t_ul = dma(SQ, zgv[:, 0:GW * 512].rearrange("p (a b) -> p a b", a=GW),
                           bass.AP(zup_d, base, apz))
                VQ.wait(t_gl)
                ts(VQ, gq_f, gq_f, scol[:, 4:5], None, ALU.mult)
                AQ.wait(VQ.tok())
                t_sg = act(AQ, fs3, gq_f, AF.Sigmoid)
                VQ.wait(t_sg)
                ts(VQ, i32g[:, 0:GW * 512], fs3, 65536.0, 65535.0, ALU.mult, ALU.min)
                cp(VQ, fs3, i32g[:, 0:GW * 512])
                tt(VQ, gq_f, gq_f, fs3, ALU.mult)
                t_g2c = ts(VQ, i32g[:, 0:GW * 512], gq_f, scol[:, 10:11], None, ALU.mult)
                GQ.wait(t_g2c)
                t_g2st = dma(GQ, bass.AP(g2code_d, base, apz),
                             i32g[:, 0:GW * 512].rearrange("p (a b) -> p a b", a=GW))
                ts(VQ, fs3, i32g[:, 0:GW * 512], scol[:, 6:7], None, ALU.mult)
                VQ.wait(t_ul)
                VQ.wait(t_g2st)
                t_uc = ts(VQ, i32g[:, 0:GW * 512], zgv[:, 0:GW * 512], scol[:, 3:4], None, ALU.mult)
                GQ.wait(t_uc)
                t_ust = dma(GQ, bass.AP(ucode_d, base, apz),
                            i32g[:, 0:GW * 512].rearrange("p (a b) -> p a b", a=GW))
                ts(VQ, zgv[:, 0:GW * 512], i32g[:, 0:GW * 512], scol[:, 2:3], None, ALU.mult)
                t_op = tt(VQ, gq_f, fs3, zgv[:, 0:GW * 512], ALU.mult)
                red(VQ, maxst[:, mi * 6 + g: mi * 6 + g + 1], gq_f, ALU.max, absv=True)
                VQ.wait(t_ust)
                AQ.wait(VQ.tok())
                GQ.wait(VQ.tok())
                SQ.wait(VQ.tok())
        barrier_all()

        vfence()
        red(VQ, dcol[:, 3:4], maxst[:, 0:48], ALU.max)
        vfence()
        finalize_scalar(dcol[:, 3:4], 1, "D", 1.0 / 32767.0, [8])

        downW = ARENA[:].bitcast(BF16)[:, 0:24576]
        GQ.wait(SQ.tok())
        dwl = []
        for icx in range(NCORES):
            for icl in range(3):
                icg = icx * 3 + icl
                dwl.append(dma(SQ, downW[:, icg * D:(icg + 1) * D],
                               bass.AP(ex_out, icx * NEX + EX_UP + EX_GATE + icl * 128 * D,
                                       [[D, 128], [1, D]])))
        t_dwl = SQ.tok()
        TQ.wait(t_dwl)
        Xs = ARENA[:, 12288:16384]
        y1s = ARENA[:, 16384:20480]
        y2s = ARENA[:, 20480:24576]

        def codebf_sl(ch):
            if ch < 16:
                return wbuf[1][:, ch * 512:(ch + 1) * 512]
            return o_bf[:, (ch - 16) * 512:(ch - 15) * 512]

        for mi in range(NRMAC):
            rt0 = mi * MAC
            t_y1l = dma(SQ, y1s.rearrange("p (a b) -> p a b", a=IC),
                        bass.AP(y1_d, rt0, [[NTOK, 128], [128 * NTOK, IC], [1, MAC]]))
            waveA = [pb_mm[0], pb_mm[1], pb_o, pb_sc]
            waveB = [pb_mm[0], pb_mm[1], pb_st[:, 0:512], pb_st[:, 512:1024]]
            for bk in ("mm0", "mm1", "o", "sc"):
                bank_write_begin(bk)
            code_toks = []
            for g in range(6):
                base = g * GW * 128 * NTOK + rt0
                apz = [[NTOK, 128], [128 * NTOK, GW], [1, MAC]]
                t_g2l = dma(GQ, fs3.rearrange("p (a b) -> p a b", a=GW),
                            bass.AP(g2code_d, base, apz))
                t_ucl = dma(GQ, gq_f.rearrange("p (a b) -> p a b", a=GW),
                            bass.AP(ucode_d, base, apz))
                VQ.waits(t_g2l, t_ucl)
                ts(VQ, fs3, fs3, scol[:, 6:7], None, ALU.mult)
                ts(VQ, gq_f, gq_f, scol[:, 2:3], None, ALU.mult)
                tt(VQ, fs3, fs3, gq_f, ALU.mult)
                t_oc = ts(VQ, i32g[:, 0:GW * 512], fs3, scol[:, 9:10], None, ALU.mult)
                for oc6 in range(GW):
                    ch = g * GW + oc6
                    cp(VQ, codebf_sl(ch), i32g[:, oc6 * 512:(oc6 + 1) * 512])
                t_cg = VQ.tok()
                code_toks.append(t_cg)
                GQ.wait(t_cg)
                TQ.waits(t_cg, t_dwl)
                for oc6 in range(GW):
                    ch = g * GW + oc6
                    for dc in range(4):
                        mm(waveA[dc][:, 0:512], downW[:, ch * D + dc * 128: ch * D + (dc + 1) * 128],
                           codebf_sl(ch), start=(ch == 0), stop=(ch == 23),
                           inc=(1 if ch == 23 and dc == 3 else 0))
            for bk in ("mm0", "mm1", "o", "sc"):
                bank_write_end(bk)
            t_ev01 = None
            for dc in range(4):
                bkn = "mm0" if dc == 0 else ("mm1" if dc == 1 else ("o" if dc == 2 else "sc"))
                bank_read(bkn, AQ)
                act(AQ, Xs[:, dc * 512:(dc + 1) * 512], waveA[dc][:, 0:512], AF.Identity,
                    scale=scol[:, 8:9])
                bank_read_done(bkn, AQ)
                if dc == 1:
                    t_ev01 = AQ.tok()
            TQ.wait(t_ev01)
            for bk in ("mm0", "mm1", "st"):
                bank_write_begin(bk)
            for ch in range(24):
                for dc in range(4, 8):
                    mm(waveB[dc - 4][:, 0:512] if dc < 6 else waveB[dc - 4],
                       downW[:, ch * D + dc * 128: ch * D + (dc + 1) * 128],
                       codebf_sl(ch), start=(ch == 0), stop=(ch == 23),
                       inc=(1 if ch == 23 and dc == 7 else 0))
            for bk in ("mm0", "mm1", "st"):
                bank_write_end(bk)
            for dc in range(4, 8):
                bkn = "mm0" if dc == 4 else ("mm1" if dc == 5 else "st")
                bank_read(bkn, AQ)
                act(AQ, Xs[:, dc * 512:(dc + 1) * 512],
                    waveB[dc - 4][:, 0:512] if dc < 6 else waveB[dc - 4],
                    AF.Identity, scale=scol[:, 8:9])
                bank_read_done(bkn, AQ)
            t_X4 = AQ.tok()
            VQ.wait(t_X4)
            row_stats(lambda i: Xs[:, i * 512:(i + 1) * 512], t_X4)
            ts(VQ, rows[:, 0:512], rows[:, 0:512], 1.0 / D, None, ALU.mult)
            ts(VQ, rows[:, 512:1024], rows[:, 512:1024], 1.0 / D, None, ALU.mult)
            tt(VQ, rows[:, 1024:1536], rows[:, 0:512], rows[:, 0:512], ALU.mult)
            t_v4 = tt(VQ, rows[:, 512:1024], rows[:, 512:1024], rows[:, 1024:1536],
                      ALU.subtract)
            AQ.wait(t_v4)
            t_r4 = rrow(rows[:, 512:1024], rows[:, 512:1024], 0)
            VQ.wait(t_r4)
            tt(VQ, rows[:, 0:512], rows[:, 0:512], rows[:, 512:1024], ALU.mult)
            ts(VQ, rows[:, 0:512], rows[:, 0:512], -1.0, None, ALU.mult)
            bcast_row(rows[:, 512:1024], ab, VQ.tok())
            bcast_row(rows[:, 0:512], bb, VQ.tok())
            VQ.wait(t_y1l)
            for dc in range(IC):
                tt(VQ, tst, Xs[:, dc * 512:(dc + 1) * 512], ab, ALU.mult)
                tt(VQ, tst2, tst, bb, ALU.add)
                tt(VQ, y2s[:, dc * 512:(dc + 1) * 512], tst2,
                   y1s[:, dc * 512:(dc + 1) * 512], ALU.add)
            t_y2d = VQ.tok()
            SQ.wait(t_y2d)
            dma(SQ, bass.AP(outT.tensor, rt0, [[NTOK, 128], [128 * NTOK, IC], [1, MAC]]),
                y2s.rearrange("p (a b) -> p a b", a=IC))
            GQ.wait(SQ.tok())
            VQ.wait(SQ.tok())
        if DEBUG_F:
            dma(SQ, dbg_scol, scol[:, 0:12])
            dma(SQ, dbg_w, upW[:, 0:3072])
            for mm_ in range(NRMAC):
                apv = [[NTOK, 128], [128 * NTOK, IC], [1, MAC]]
                tl1 = dma(SQ, SCR[:].rearrange("p (a b) -> p a b", a=IC),
                          bass.AP(y1_d, mm_ * MAC, apv))
                SQ.wait(tl1)
                dma(SQ, bass.AP(dbg_y1.tensor, mm_ * MAC, apv),
                    SCR[:].rearrange("p (a b) -> p a b", a=IC))
                tl2 = dma(SQ, SCR[:].rearrange("p (a b) -> p a b", a=IC),
                          bass.AP(xln2_d, mm_ * MAC, apv))
                SQ.wait(tl2)
                dma(SQ, bass.AP(dbg_xl2.tensor, mm_ * MAC, apv),
                    SCR[:].rearrange("p (a b) -> p a b", a=IC))
                SQ.wait(SQ.tok())
            for j_, mm_ in enumerate((0, 6)):
                apz3 = [[NTOK, 128], [128 * NTOK, 24], [1, MAC]]
                apd3 = [[1024, 128], [128 * 1024, 24], [1, MAC]]
                for (srcd, dstd) in ((zup_d, dbg_zu), (ucode_d, dbg_uc), (g2code_d, dbg_g2c)):
                    tlz = dma(GQ, wbuf[0][:].bitcast(F32)[:, 0:4096].rearrange(
                        "p (a b) -> p a b", a=8)[:, 0:8, 0:512],
                        bass.AP(srcd, mm_ * MAC, [[NTOK, 128], [128 * NTOK, 8], [1, MAC]]))
                    GQ.wait(tlz)
                    tsz = dma(GQ, bass.AP(dstd.tensor if hasattr(dstd, "tensor") else dstd,
                                          j_ * MAC, [[1024, 128], [128 * 1024, 8], [1, MAC]]),
                              wbuf[0][:].bitcast(F32)[:, 0:4096].rearrange(
                        "p (a b) -> p a b", a=8)[:, 0:8, 0:512])
                    GQ.wait(tsz)
        SQ.wait(SQ.tok())
        GQ.wait(GQ.tok())

        with nc.Block() as block:
            @block.sync
            def _(q):
                for f_ in SQ.ops:
                    f_(q)

            @block.gpsimd
            def _(q):
                for f_ in GQ.ops:
                    f_(q)

            @block.tensor
            def _(q):
                for f_ in TQ.ops:
                    f_(q)

            @block.vector
            def _(q):
                for f_ in VQ.ops:
                    f_(q)

            @block.scalar
            def _(q):
                for f_ in AQ.ops:
                    f_(q)

    return nc


def _stage_inputs(inputs):
    x = np.asarray(inputs["L0_attn_x"])
    xTs = [np.ascontiguousarray(x[s].T) for s in range(2)]
    eye = np.eye(128, dtype=np.float32)
    mask = np.zeros((C, C), dtype=np.uint8)
    for s_ in range(C):
        mask[s_, s_:] = 1
    nw = np.ascontiguousarray(np.asarray(inputs["L0_attn_norm_w"]).reshape(IC, 128).T)
    nw = np.ascontiguousarray(nw.astype(np.float32))
    w_up = np.ascontiguousarray(np.asarray(inputs["L0_ffn_w_up"], dtype=np.float32))
    w_gate = np.ascontiguousarray(np.asarray(inputs["L0_ffn_w_gate"], dtype=np.float32))
    w_down = np.ascontiguousarray(np.asarray(inputs["L0_ffn_w_down"], dtype=np.float32))
    base = dict(
        wqT=np.ascontiguousarray(np.asarray(inputs["L0_attn_Wq"], dtype=np.float32).T),
        wkT=np.ascontiguousarray(np.asarray(inputs["L0_attn_Wk"], dtype=np.float32).T),
        wvT=np.ascontiguousarray(np.asarray(inputs["L0_attn_Wv"], dtype=np.float32).T),
        woT=np.ascontiguousarray(np.asarray(inputs["L0_attn_Wo"], dtype=np.float32).T),
        nwfm=nw, eye_h=eye, mask_h=mask,
    )
    maps = []
    for core in range(NCORES):
        seq, seg = core // 4, core % 4
        t0 = seg * NTOK
        xs = np.zeros((D, TOT), dtype=np.float32)
        lo = t0 - HALO
        if lo < 0:
            xs[:, HALO:] = xTs[seq][:, t0:t0 + NTOK]
        else:
            xs[:, :] = xTs[seq][:, lo:t0 + NTOK]
        sl = []
        for j in range(OC3):
            sl.append(w_up[j * 128:(j + 1) * 128, core * 128:(core + 1) * 128])
        for j in range(OC3):
            sl.append(w_gate[j * 128:(j + 1) * 128, core * 128:(core + 1) * 128])
        for icl in range(3):
            icg = core * 3 + icl
            for hc in range(IC):
                sl.append(w_down[hc * 128:(hc + 1) * 128, icg * 128:(icg + 1) * 128])
        m = dict(base)
        m["xT"] = np.ascontiguousarray(xs)
        m["dq_sl"] = np.ascontiguousarray(np.concatenate(sl, axis=0))
        maps.append(m)
    return maps


_NC_CACHE = None


def kernel(L0_attn_x, L0_attn_Wq, L0_attn_Wk, L0_attn_Wv, L0_attn_Wo,
           L0_attn_norm_w, L0_ffn_w_gate, L0_ffn_w_up, L0_ffn_w_down):
    global _NC_CACHE
    inputs = dict(L0_attn_x=L0_attn_x, L0_attn_Wq=L0_attn_Wq, L0_attn_Wk=L0_attn_Wk,
                  L0_attn_Wv=L0_attn_Wv, L0_attn_Wo=L0_attn_Wo,
                  L0_attn_norm_w=L0_attn_norm_w, L0_ffn_w_gate=L0_ffn_w_gate,
                  L0_ffn_w_up=L0_ffn_w_up, L0_ffn_w_down=L0_ffn_w_down)
    if _NC_CACHE is None:
        _NC_CACHE = build_program()
    nc = _NC_CACHE
    maps = _stage_inputs(inputs)
    res = run_bass_kernel_spmd(nc, maps, core_ids=list(range(NCORES)))
    out = np.empty((2, 16384, D), dtype=np.float32)
    for core in range(NCORES):
        seq, seg = core // 4, core % 4
        o = np.asarray(res.results[core]["outT"])
        out[seq, seg * NTOK:(seg + 1) * NTOK, :] = o.T
    return out


# version 10
# speedup vs baseline: 1.1770x; 1.0794x over previous
import numpy as np
import ml_dtypes
from contextlib import ExitStack

import concourse.bass as bass
import concourse.mybir as mybir
from concourse.bass_utils import run_bass_kernel_spmd

F32 = mybir.dt.float32
BF16 = mybir.dt.bfloat16
I32 = mybir.dt.int32
I16 = mybir.dt.int16
AF = mybir.ActivationFunctionType
ALU = mybir.AluOpType
AX = mybir.AxisListType

D = 1024
H = 3072
NTOK = 4096
HALO = 512
TOT = NTOK + HALO
MAC = 512
NMAC = TOT // MAC
NRMAC = NMAC - 1
C = 64
NCH = MAC // C
IC = D // 128
OC3 = H // 128
NCORES = 8

EX_UP = 128 * H
EX_GATE = 128 * H
EX_DOWN = 3 * 128 * D
NEX = EX_UP + EX_GATE + EX_DOWN

STUB_F = False
DEBUG_DUMP = False
DEBUG_F = False


class Q:
    def __init__(self, sem):
        self.ops = []
        self.sem = sem
        self.cnt = 0

    def wait(self, tok):
        if tok is None:
            return
        sem, val = tok
        self.ops.append(lambda e, s=sem, v=val: e.wait_ge(s, v))

    def waits(self, *toks):
        for t in toks:
            self.wait(t)

    def run(self, fn, inc=1):
        self.cnt += inc
        tokv = self.cnt

        def cl(e, fn=fn, inc=inc, s=self.sem):
            ins = fn(e)
            if inc:
                ins.then_inc(s, inc)
        self.ops.append(cl)
        return (self.sem, tokv)

    def tok(self):
        return (self.sem, self.cnt)


def build_program():
    nc = bass.Bass("TRN2", target_bir_lowering=False)

    def din(name, shape, dtype=F32):
        return nc.dram_tensor(name, shape, dtype, kind="ExternalInput").ap()

    xT = din("xT", [D, TOT])
    wqT = din("wqT", [D, D])
    wkT = din("wkT", [D, D])
    wvT = din("wvT", [D, D])
    woT = din("woT", [D, D])
    nwfm = din("nwfm", [128, IC])
    eye_h = din("eye_h", [128, 128])
    mask_h = din("mask_h", [C, C], mybir.dt.uint8)
    dq_sl = din("dq_sl", [3 * OC3 * 128, 128])

    outT = nc.dram_tensor("outT", [D, NTOK], F32, kind="ExternalOutput").ap()
    if DEBUG_F:
        dbg_scol = nc.dram_tensor("dbg_scol", [128, 12], F32, kind="ExternalOutput").ap()
        dbg_z = nc.dram_tensor("dbg_z", [128, 2048], F32, kind="ExternalOutput").ap()
        dbg_w = nc.dram_tensor("dbg_w", [128, 3072], mybir.dt.bfloat16, kind="ExternalOutput").ap()
        dbg_mx = nc.dram_tensor("dbg_mx", [128, 96], F32, kind="ExternalOutput").ap()
        dbg_y1 = nc.dram_tensor("dbg_y1", [D, NTOK], F32, kind="ExternalOutput").ap()
        dbg_zu = nc.dram_tensor("dbg_zu", [H, 1024], F32, kind="ExternalOutput").ap()
        dbg_uc = nc.dram_tensor("dbg_uc", [H, 1024], F32, kind="ExternalOutput").ap()
        dbg_g2c = nc.dram_tensor("dbg_g2c", [H, 1024], F32, kind="ExternalOutput").ap()
        dbg_xl2 = nc.dram_tensor("dbg_xl2", [D, NTOK], F32, kind="ExternalOutput").ap()
    if DEBUG_DUMP:
        dbg_S = nc.dram_tensor("dbg_S", [128, 1024], F32, kind="ExternalOutput").ap()
        dbg_o = nc.dram_tensor("dbg_o", [128, 4096], F32, kind="ExternalOutput").ap()
        dbg_qt = nc.dram_tensor("dbg_qt", [128, 512], F32, kind="ExternalOutput").ap()
        dbg_kh = nc.dram_tensor("dbg_kh", [128, 512], F32, kind="ExternalOutput").ap()
        dbg_dc = nc.dram_tensor("dbg_dc", [128, 8], F32, kind="ExternalOutput").ap()
        dbg_scm = nc.dram_tensor("dbg_scm", [64, 512], F32, kind="ExternalOutput").ap()
        dbg_pst = nc.dram_tensor("dbg_pst", [128, 1024], F32, kind="ExternalOutput").ap()

    y1_d = nc.dram_tensor("y1_d", [D, NTOK], F32)
    xln2_d = nc.dram_tensor("xln2_d", [D, NTOK], F32)
    zup_d = nc.dram_tensor("zup_d", [H, NTOK], F32)
    zgate_d = nc.dram_tensor("zgate_d", [H, NTOK], F32)
    gcode_d = nc.dram_tensor("gcode_d", [H, NTOK], I16)
    g2code_d = nc.dram_tensor("g2code_d", [H, NTOK], I16)
    ucode_d = nc.dram_tensor("ucode_d", [H, NTOK], I16)
    ln2_d = nc.dram_tensor("ln2_d", [2, NTOK], F32)
    wq_bf_d = nc.dram_tensor("wq_bf_d", [D, D], BF16)
    wk_bf_d = nc.dram_tensor("wk_bf_d", [D, D], BF16)
    wv_bf_d = nc.dram_tensor("wv_bf_d", [D, D], BF16)
    wo_bf_d = nc.dram_tensor("wo_bf_d", [D, D], BF16)
    ex_in = nc.dram_tensor("ex_in", [1, NEX], BF16)
    ex_out = nc.dram_tensor("ex_out", [NCORES, NEX], BF16, addr_space="Shared")
    barsc = nc.dram_tensor("barsc", [4, 128], F32)
    bar_in = {}
    bar_out = {}
    for b in ("A", "B", "C", "D"):
        bar_in[b] = nc.dram_tensor(f"bar{b}_in", [1, 4], F32)
        bar_out[b] = nc.dram_tensor(f"bar{b}_out", [NCORES, 4], F32, addr_space="Shared")

    with ExitStack() as st:
        E = st.enter_context

        def sbt(name, shape, dtype=F32):
            return E(nc.sbuf_tensor(name, shape, dtype))

        ARENA = sbt("ARENA", [128, 24576], F32)
        SCR = sbt("SCR", [128, 4096], F32)
        ar = ARENA[:]
        xb = [ar[:, 0:4096], ar[:, 4096:8192]]
        qf = ar[:, 8192:12288]
        ff = ar[:, 12288:16384]
        Xf = ar[:, 16384:20480]
        y1f = ar[:, 20480:24576]
        sc_ = SCR[:]
        gp1 = sc_[:, 0:512]
        Apr = sc_[:, 512:1024]
        Em = sc_[:, 1024:1536]
        Ep = sc_[:, 1536:2048]
        ktf = sc_[:, 2048:2560]
        tst = sc_[:, 2560:3072]
        tst2 = sc_[:, 3072:3584]
        gps = sc_[:, 3584:4096]

        wbuf = [sbt(f"wbuf{i}", [128, 8192], BF16) for i in range(2)]
        xln_bf = sbt("xln_bf", [128, 4096], BF16)
        o_bf = sbt("o_bf", [128, 4096], BF16)
        v_bf = sbt("v_bf", [C, D], BF16)
        qt_bfp = [sbt(f"qt_bf{i}", [128, 8 * C], BF16) for i in range(2)]
        kt_bfp = [sbt(f"kt_bf{i}", [128, 8 * C], BF16) for i in range(2)]
        kh_bf = sbt("kh_bf", [128, 8 * C], BF16)
        khT_bf = sbt("khT_bf", [C, 8 * 128], BF16)
        scm_bf = sbt("scm_bf", [C, 8 * C], BF16)
        S_f = sbt("S_f", [128, 8 * 128], F32)
        S_bf = sbt("S_bf", [128, 8 * 128], BF16)
        BC = sbt("BC", [128, 2048], F32)
        mb, rb = BC[:, 0:512], BC[:, 512:1024]
        ab, bb = BC[:, 1024:1536], BC[:, 1536:2048]
        sqst = sbt("sqst", [128, 512], F32)
        dcolp = [sbt(f"dcol{i}", [128, 8], F32) for i in range(2)]
        dcol = dcolp[0]
        cc8 = sbt("cc8", [128, 4], F32)
        eye_bf = sbt("eye_bf", [128, 128], BF16)
        mask_c = sbt("mask_c", [C, C], mybir.dt.uint8)
        zrs = sbt("zrs", [128, C], F32)
        nw_c = sbt("nw_c", [128, IC], F32)
        ones_col = sbt("ones_col", [128, 1], F32)
        ones_row = sbt("ones_row", [1, 128], F32)
        rows = sbt("rows", [1, 1536], F32)
        runmax = sbt("runmax", [128, 1], F32)
        scal = sbt("scal", [1, 64], F32)
        scol = sbt("scol", [128, 12], F32)
        dq_nat = SCR[:, 0:128]
        dq_s4 = SCR[:, 256:260]
        dq_r4 = SCR[:, 260:264]
        dq_t = SCR[:, 128:256]
        dq_bf = SCR[:].bitcast(BF16)[:, 528:656]
        dq_st = SCR[:].bitcast(BF16)[:, 656:784]
        maxst = sbt("maxst", [128, 96], F32)

        pb_mm = [E(nc.psum_tensor(f"pb_mm{i}", [128, 512], F32)) for i in range(2)]
        pb_o = E(nc.psum_tensor("pb_o", [128, 512], F32))
        pb_sc = E(nc.psum_tensor("pb_sc", [128, 512], F32))
        pb_tr = E(nc.psum_tensor("pb_tr", [128, 1024], BF16))
        pb_st = E(nc.psum_tensor("pb_st", [128, 1024], F32))
        pb_row = E(nc.psum_tensor("pb_row", [1, 512], F32))

        SQ = Q(E(nc.semaphore("sS")))
        GQ = Q(E(nc.semaphore("sG")))
        TQ = Q(E(nc.semaphore("sT")))
        VQ = Q(E(nc.semaphore("sV")))
        AQ = Q(E(nc.semaphore("sA")))
        ccs = E(nc.semaphore("ccs"))
        cc_cnt = [0]
        ALLQ = [SQ, GQ, TQ, VQ, AQ]

        def barrier_all():
            toks = [q.tok() for q in ALLQ]
            for q in ALLQ:
                for t in toks:
                    if t[0] is not q.sem:
                        q.wait(t)

        def mm(out_ap, lhsT_ap, rhs_ap, start, stop, inc=0):
            return TQ.run(lambda e, o=out_ap, l=lhsT_ap, r=rhs_ap, a=start, b=stop:
                          e.matmul(o, l, r, start=a, stop=b), inc=inc)

        def tr(out_ap, in_ap, inc=1):
            return TQ.run(lambda e, o=out_ap, i=in_ap, d=eye_bf[:]:
                          e.transpose(o, i, d), inc=inc)

        def act(q, out_ap, in_ap, func, bias=0.0, scale=1.0, accum=None, inc=1):
            return q.run(lambda e, o=out_ap, i=in_ap, f=func, b=bias, s=scale, a=accum:
                         e.activation(o, i, f, bias=b, scale=s, accum_out=a), inc=inc)

        def tt(q, out_ap, a_ap, b_ap, op, inc=1):
            return q.run(lambda e, o=out_ap, a=a_ap, b=b_ap, p=op:
                         e.tensor_tensor(o, a, b, p), inc=inc)

        def ts(q, out_ap, in_ap, s1, s2, op0, op1=ALU.bypass, inc=1):
            return q.run(lambda e, o=out_ap, i=in_ap, a=s1, b=s2, p0=op0, p1=op1:
                         e.tensor_scalar(o, i, a, b, p0, p1), inc=inc)

        def stt(q, out_ap, in0_ap, scalar, in1_ap, op0, op1, inc=1):
            return q.run(lambda e, o=out_ap, i=in0_ap, s=scalar, j=in1_ap, p0=op0, p1=op1:
                         e.scalar_tensor_tensor(o, i, s, j, p0, p1), inc=inc)

        def cp(q, out_ap, in_ap, inc=1):
            def fn(e, o=out_ap, i=in_ap):
                if hasattr(e, "tensor_copy"):
                    return e.tensor_copy(out=o, in_=i)
                return e.copy(o, i)
            return q.run(fn, inc=inc)

        def red(q, out_ap, in_ap, op, absv=False, inc=1):
            return q.run(lambda e, o=out_ap, i=in_ap, p=op, a=absv:
                         e.tensor_reduce(o, i, axis=AX.X, op=p, apply_absolute_value=a),
                         inc=inc)

        def scan_op(q, out_ap, d0, d1, init, op0, op1, inc=1):
            return q.run(lambda e, o=out_ap, a=d0, b=d1, i=init, p0=op0, p1=op1:
                         e.tensor_tensor_scan(o, a, b, i, p0, p1), inc=inc)

        def dma(q, out_ap, in_ap):
            return q.run(lambda e, o=out_ap, i=in_ap: e.dma_start(out=o, in_=i), inc=16)

        def memset(q, ap_, val, inc=1):
            return q.run(lambda e, a=ap_, v=val: e.memset(a, v), inc=inc)

        def select(q, out_ap, mask_ap, t_ap, f_ap, inc=1):
            return q.run(lambda e, o=out_ap, m=mask_ap, a=t_ap, b=f_ap:
                         e.select(o, m, a, b), inc=inc)

        def recip(q, out_ap, in_ap, inc=1):
            return q.run(lambda e, o=out_ap, i=in_ap: e.reciprocal(o, i), inc=inc)

        def vfence():
            t = VQ.tok()
            GQ.wait(t)
            t2 = GQ.run(lambda e: e.engine_nop(), inc=1)
            VQ.wait(t2)

        def allgather(in_t, out_t):
            cc_cnt[0] += 1
            val = cc_cnt[0]
            GQ.ops.append(lambda e, i=in_t, o=out_t, s=ccs:
                          e.collective_compute("AllGather", ALU.bypass,
                                               replica_groups=[list(range(NCORES))],
                                               ins=[i.ap().opt()], outs=[o.ap().opt()])
                          .then_inc(s, 1))
            return (ccs, val)

        def rrow(dst, src, eps_col):
            act(AQ, dst, src, AF.Ln, bias=cc8[0:1, eps_col:eps_col + 1])
            return act(AQ, dst, dst, AF.Exp, scale=cc8[0:1, 2:3])

        bank_prod = {}
        bank_cons = {}

        def bank_write_begin(bk):
            for t in bank_cons.get(bk, []):
                TQ.wait(t)
            bank_cons[bk] = []

        def bank_write_end(bk):
            bank_prod[bk] = TQ.tok()

        def bank_read(bk, q):
            q.wait(bank_prod.get(bk))

        def bank_read_done(bk, q):
            bank_cons.setdefault(bk, []).append(q.tok())

        def bcast_row(row_ap, dst_ap, src_tok):
            bank_write_begin("sc")
            TQ.wait(src_tok)
            mm(pb_sc[:, 0:512], ones_row[:], row_ap, True, True, inc=1)
            bank_write_end("sc")
            bank_read("sc", VQ)
            t = cp(VQ, dst_ap, pb_sc[:, 0:512])
            bank_read_done("sc", VQ)
            return t

        t_eye = dma(GQ, eye_bf[:], eye_h)
        t_mask = dma(SQ, mask_c[:], mask_h)
        t_nw = dma(SQ, nw_c[:], nwfm)
        memset(VQ, zrs[:], 0.0)
        memset(VQ, ones_col[:], 1.0)
        memset(VQ, ones_row[:], 1.0)
        memset(VQ, S_f[:], 0.0)
        memset(VQ, S_bf[:], 0.0)
        memset(VQ, runmax[:], 0.0)
        memset(VQ, cc8[:, 0:1], 1e-5)
        memset(VQ, cc8[:, 1:2], 1e-6)
        memset(VQ, cc8[:, 2:3], -0.5)

        t_wq = dma(GQ, wq_bf_d.ap(), wqT)
        t_wk = dma(GQ, wk_bf_d.ap(), wkT)
        t_wv = dma(GQ, wv_bf_d.ap(), wvT)
        t_wo = None
        for half in range(2):
            rsl = slice(half * 512, (half + 1) * 512)
            tl = dma(SQ, xb[0].rearrange("p (a b) -> p a b", a=IC),
                     woT.rearrange("(a p) c -> p a c", p=128)[:, :, rsl])
            VQ.waits(tl, t_nw)
            for a in range(IC):
                ts(VQ, xln_bf[:, a * 512:(a + 1) * 512],
                   xb[0][:, a * 512:(a + 1) * 512], nw_c[:, a:a + 1], None, ALU.mult)
            SQ.wait(VQ.tok())
            t_wo = dma(SQ, wo_bf_d.ap().rearrange("(a p) c -> p a c", p=128)[:, :, rsl],
                       xln_bf[:].rearrange("p (a b) -> p a b", a=IC))
            VQ.wait(t_wo)
        t_wcast = (t_wq, t_wk, t_wv, t_wo)

        GQ.wait(t_eye)
        TQ.wait(t_eye)
        tile_row = 0
        t_last_nat_read = None
        t_last_st = None
        for tgroup, base_off, ostride in (("up", 0, H), ("gate", EX_UP, H),
                                          ("down", EX_UP + EX_GATE, D)):
            for j in range(OC3):
                SQ.wait(t_last_nat_read)
                tl = dma(SQ, dq_nat[:], dq_sl[tile_row * 128:(tile_row + 1) * 128, :])
                tile_row += 1
                VQ.wait(tl)
                red(VQ, dq_s4[:], dq_nat[:].rearrange("p (w k) -> p w k", k=32),
                    ALU.max, absv=True)
                vfence()
                ts(VQ, dq_s4[:], dq_s4[:], 1.0 / 7.0, 1e-12, ALU.mult, ALU.max)
                vfence()
                recip(VQ, dq_r4[:], dq_s4[:])
                vfence()
                t_last_nat_read = tt(VQ, dq_t[:].rearrange("p (w k) -> p w k", k=32),
                                     dq_nat[:].rearrange("p (w k) -> p w k", k=32),
                                     dq_r4[:, :, None].broadcast_to([128, 4, 32]), ALU.mult)
                cp(VQ, dq_t[:].bitcast(I32), dq_t[:])
                cp(VQ, dq_t[:], dq_t[:].bitcast(I32))
                t_dq = tt(VQ, dq_bf[:].rearrange("p (w k) -> p w k", k=32),
                          dq_t[:].rearrange("p (w k) -> p w k", k=32),
                          dq_s4[:, :, None].broadcast_to([128, 4, 32]), ALU.mult)
                bank_write_begin("tr")
                TQ.wait(t_dq)
                t_tp = tr(pb_tr[:, 0:128], dq_bf[:])
                bank_write_end("tr")
                VQ.wait(t_tp)
                bank_read("tr", AQ)
                AQ.wait(t_last_st)
                t_ev = cp(AQ, dq_st[:], pb_tr[:, 0:128])
                bank_read_done("tr", AQ)
                SQ.wait(t_ev)
                if tgroup == "down":
                    icl, hc = j // IC, j % IC
                    off = base_off + icl * (128 * D) + hc * 128
                else:
                    off = base_off + j * 128
                t_last_st = SQ.run(
                    lambda e, o=bass.AP(ex_in, off, [[ostride, 128], [1, 128]]),
                    i=dq_st[:]: e.dma_start(out=o, in_=i), inc=16)
        GQ.wait(SQ.tok())
        t_gather = allgather(ex_in, ex_out)

        def load_x(m):
            tok0 = m * MAC
            return dma(SQ, xb[m % 2].rearrange("p (a b) -> p a b", a=IC),
                       bass.AP(xT.tensor, tok0, [[TOT, 128], [128 * TOT, IC], [1, MAC]]))

        def load_w(dram, slot, after_tok):
            SQ.wait(after_tok)
            return dma(SQ, wbuf[slot][:].rearrange("p (a b) -> p a b", a=IC),
                       dram.ap().rearrange("(a p) c -> p a c", p=128))

        def row_stats(src_fn, src_tok):
            bank_write_begin("row")
            TQ.wait(src_tok)
            for i in range(IC):
                mm(pb_row[:, 0:512], ones_col[:], src_fn(i),
                   start=(i == 0), stop=(i == IC - 1), inc=(1 if i == IC - 1 else 0))
            bank_write_end("row")
            bank_read("row", VQ)
            cp(VQ, rows[:, 0:512], pb_row[:, 0:512])
            bank_read_done("row", VQ)
            sq_mm_tok = [None, None]
            for i in range(IC):
                buf = sqst[:] if i % 2 == 0 else tst
                AQ.wait(sq_mm_tok[i % 2])
                t_sq = act(AQ, buf, src_fn(i), AF.Square)
                bank_write_begin("row")
                TQ.wait(t_sq)
                mm(pb_row[:, 0:512], ones_col[:], buf,
                   start=(i == 0), stop=(i == IC - 1), inc=1)
                bank_write_end("row")
                sq_mm_tok[i % 2] = TQ.tok()
            bank_read("row", VQ)
            cp(VQ, rows[:, 512:1024], pb_row[:, 0:512])
            bank_read_done("row", VQ)

        SQ.waits(*t_wcast)
        SQ.wait(GQ.tok())
        x_tok = [None, None]
        x_tok[0] = load_x(0)
        w_tok = {}

        for m in range(NMAC):
            is_halo = (m == 0)
            par = m % 2
            x_cur = xb[par]
            tok0 = m * MAC
            rtok0 = tok0 - HALO

            w_tok["wk"] = load_w(wk_bf_d, 0, None)
            w_tok["wq"] = load_w(wq_bf_d, 1, None)
            if m + 1 < NMAC:
                x_tok[(m + 1) % 2] = load_x(m + 1)

            VQ.wait(x_tok[par])
            AQ.wait(x_tok[par])

            row_stats(lambda i: x_cur[:, i * 512:(i + 1) * 512], x_tok[par])
            ts(VQ, rows[:, 0:512], rows[:, 0:512], 1.0 / D, None, ALU.mult)
            ts(VQ, rows[:, 512:1024], rows[:, 512:1024], 1.0 / D, None, ALU.mult)
            tt(VQ, rows[:, 1024:1536], rows[:, 0:512], rows[:, 0:512], ALU.mult)
            t_v = tt(VQ, rows[:, 512:1024], rows[:, 512:1024], rows[:, 1024:1536],
                     ALU.subtract)
            AQ.wait(t_v)
            t_r = rrow(rows[:, 512:1024], rows[:, 512:1024], 0)
            VQ.wait(t_r)
            bcast_row(rows[:, 0:512], mb, VQ.tok())
            t_bc = bcast_row(rows[:, 512:1024], rb, t_r)
            for i in range(IC):
                tt(VQ, tst, x_cur[:, i * 512:(i + 1) * 512], mb, ALU.subtract)
                tt(VQ, xln_bf[:, i * 512:(i + 1) * 512], tst, rb, ALU.mult)
            t_xln = VQ.tok()

            def proj_fm(wtok, wslot, out_f32, fn):
                TQ.waits(wtok, t_xln)
                for oc in range(8):
                    bk = "mm%d" % (oc % 2)
                    pb = pb_mm[oc % 2]
                    bank_write_begin(bk)
                    for ic in range(IC):
                        mm(pb[:], wbuf[wslot][:, ic * 1024 + oc * 128: ic * 1024 + (oc + 1) * 128],
                           xln_bf[:, ic * 512:(ic + 1) * 512],
                           start=(ic == 0), stop=(ic == IC - 1), inc=(1 if ic == IC - 1 else 0))
                    bank_write_end(bk)
                    bank_read(bk, AQ)
                    act(AQ, out_f32[:, oc * 512:(oc + 1) * 512], pb[:], fn)
                    bank_read_done(bk, AQ)
                return AQ.tok()

            t_f_done = proj_fm(w_tok["wk"], 0, ff, AF.Sigmoid)
            w_tok["wv"] = load_w(wv_bf_d, 0, TQ.tok())
            if not is_halo:
                t_q_done = proj_fm(w_tok["wq"], 1, qf, AF.Silu)
            else:
                t_q_done = AQ.tok()
            w_tok["wo"] = load_w(wo_bf_d, 1, TQ.tok())

            pch = {"scan": None, "ep": None, "kh": None, "tr": None, "omm": [None, None],
                   "stMM": None, "sup": None, "sbf": None}
            pend_state = None
            for ch in range(NCH):
                par_c = ch % 2
                qt_bf = qt_bfp[par_c]
                kt_bf = kt_bfp[par_c]
                dcol = dcolp[par_c]
                c0 = ch * C
                fview = ff[:].rearrange("p (s t) -> p s t", s=8)[:, :, c0:c0 + C]
                qview = qf[:].rearrange("p (s t) -> p s t", s=8)[:, :, c0:c0 + C]
                AQ.wait(t_f_done)
                AQ.wait(pch["scan"])
                act(AQ, gp1[:].rearrange("p (s t) -> p s t", s=8), fview, AF.Exp, scale=-1.0)
                t_g = act(AQ, gps[:], gp1[:], AF.Ln, bias=1.0)
                VQ.waits(t_g, pch["ep"])
                for sl in range(8):
                    scan_op(VQ, Apr[:, sl * C:(sl + 1) * C], gps[:, sl * C:(sl + 1) * C],
                            zrs[:], 0.0, ALU.add, ALU.bypass)
                t_scan = VQ.tok()
                pch["scan"] = t_scan
                AQ.waits(t_scan, pch["kh"], pch["sup"])
                cp(AQ, dcol[:, 0:8],
                   Apr[:].rearrange("p (s t) -> p s t", s=8)[:, :, C - 1:C].opt())
                act(AQ, dcol[:, 0:8], dcol[:, 0:8], AF.Exp, scale=-1.0)
                act(AQ, Em[:], Apr[:], AF.Exp, scale=-1.0)
                t_ep = act(AQ, Ep[:], Apr[:], AF.Exp)
                pch["ep"] = t_ep
                VQ.waits(t_ep, pch["omm"][par_c])
                if not is_halo:
                    tt(VQ, qt_bf[:].rearrange("p (s t) -> p s t", s=8), qview,
                       Em[:].rearrange("p (s t) -> p s t", s=8), ALU.mult)
                stt(VQ, tst2[:].rearrange("p (s t) -> p s t", s=8), fview, -1.0,
                    Ep[:].rearrange("p (s t) -> p s t", s=8), ALU.mult, ALU.mult)
                tt(VQ, ktf[:], tst2[:], Ep[:], ALU.add)
                if not is_halo:
                    cp(VQ, kt_bf[:], ktf[:])
                VQ.wait(pch["tr"])
                for sl in range(8):
                    ts(VQ, kh_bf[:, sl * C:(sl + 1) * C], ktf[:, sl * C:(sl + 1) * C],
                       dcol[:, sl:sl + 1], None, ALU.mult)
                t_kh = VQ.tok()
                pch["kh"] = t_kh

                if pend_state is not None:
                    pv_par, pv_dcol = pend_state
                    bank_read("st", VQ)
                    for sl in range(8):
                        stt(VQ, S_f[:, sl * 128:(sl + 1) * 128], S_f[:, sl * 128:(sl + 1) * 128],
                            pv_dcol[:, sl:sl + 1], pb_st[:, sl * 128:(sl + 1) * 128],
                            ALU.mult, ALU.add)
                    t_sup = VQ.tok()
                    pch["sup"] = t_sup
                    bank_read_done("st", VQ)
                    AQ.wait(t_sup)
                    t_sbf = cp(AQ, S_bf[:], S_f[:])
                    pch["sbf"] = t_sbf
                    pend_state = None

                TQ.wait(w_tok["wv"])
                for hb in range(2):
                    bk = "mm%d" % hb
                    bank_write_begin(bk)
                    for ic in range(IC):
                        mm(pb_mm[hb][0:C, :],
                           xln_bf[:, ic * 512 + c0: ic * 512 + c0 + C],
                           wbuf[0][:, ic * 1024 + hb * 512: ic * 1024 + (hb + 1) * 512],
                           start=(ic == 0), stop=(ic == IC - 1), inc=(1 if ic == IC - 1 else 0))
                    bank_write_end(bk)
                    bank_read(bk, AQ)
                    act(AQ, v_bf[:, hb * 512:(hb + 1) * 512], pb_mm[hb][0:C, :], AF.Identity)
                    bank_read_done(bk, AQ)
                t_vev = AQ.tok()

                bank_write_begin("tr")
                TQ.wait(t_kh)
                for sl in range(8):
                    tr(pb_tr[0:C, sl * 128:(sl + 1) * 128], kh_bf[:, sl * C:(sl + 1) * C],
                       inc=(1 if sl == 7 else 0))
                bank_write_end("tr")
                pch["tr"] = TQ.tok()
                bank_read("tr", AQ)
                AQ.wait(t_vev)
                t_khT = cp(AQ, khT_bf[:], pb_tr[0:C, :])
                bank_read_done("tr", AQ)

                if not is_halo:
                    bank_write_begin("sc")
                    TQ.wait(t_kh)
                    for sl in range(8):
                        mm(pb_sc[0:C, sl * C:(sl + 1) * C], kt_bf[:, sl * C:(sl + 1) * C],
                           qt_bf[:, sl * C:(sl + 1) * C], start=True, stop=True,
                           inc=(1 if sl == 7 else 0))
                    bank_write_end("sc")
                    bank_read("sc", VQ)
                    VQ.wait(t_mask)
                    t_scm = select(VQ, scm_bf[:].rearrange("p (s t) -> p s t", s=8),
                                   mask_c[:, None, :].broadcast_to([C, 8, C]),
                                   pb_sc[0:C, :].rearrange("p (s t) -> p s t", s=8),
                                   zrs[0:C, None, 0:C].broadcast_to([C, 8, C]))
                    bank_read_done("sc", VQ)

                    bank_write_begin("o")
                    TQ.waits(t_scm, t_vev, t_khT, pch["sbf"])
                    for sl in range(8):
                        mm(pb_o[:, sl * C:(sl + 1) * C], v_bf[:, sl * 128:(sl + 1) * 128],
                           scm_bf[:, sl * C:(sl + 1) * C], start=True, stop=False)
                        mm(pb_o[:, sl * C:(sl + 1) * C], S_bf[:, sl * 128:(sl + 1) * 128],
                           qt_bf[:, sl * C:(sl + 1) * C], start=False, stop=True,
                           inc=(1 if sl == 7 else 0))
                    bank_write_end("o")
                    pch["omm"][par_c] = TQ.tok()
                else:
                    pch["omm"][par_c] = TQ.tok()

                bank_write_begin("st")
                TQ.waits(t_khT, t_vev, pch["sup"])
                for sl in range(8):
                    mm(pb_st[:, sl * 128:(sl + 1) * 128], khT_bf[:, sl * 128:(sl + 1) * 128],
                       v_bf[:, sl * 128:(sl + 1) * 128], start=True, stop=True,
                       inc=(1 if sl == 7 else 0))
                bank_write_end("st")
                pch["stMM"] = TQ.tok()
                pend_state = (par_c, dcol)

                if not is_halo:
                    oview = o_bf[:].rearrange("p (s t) -> p s t", s=8)[:, :, c0:c0 + C]
                    bank_read("o", AQ)
                    act(AQ, sqst[:], pb_o[:], AF.Square)
                    act(AQ, oview, pb_o[:].rearrange("p (s t) -> p s t", s=8), AF.Identity)
                    bank_read_done("o", AQ)
                    bank_write_begin("row")
                    TQ.wait(AQ.tok())
                    for sl in range(8):
                        mm(pb_row[:, c0:c0 + C], ones_col[:], sqst[:, sl * C:(sl + 1) * C],
                           start=(sl == 0), stop=(sl == 7), inc=(1 if sl == 7 else 0))
                    bank_write_end("row")
                    AQ.wait(TQ.tok())

            pv_par, pv_dcol = pend_state
            bank_read("st", VQ)
            for sl in range(8):
                stt(VQ, S_f[:, sl * 128:(sl + 1) * 128], S_f[:, sl * 128:(sl + 1) * 128],
                    pv_dcol[:, sl:sl + 1], pb_st[:, sl * 128:(sl + 1) * 128],
                    ALU.mult, ALU.add)
            t_sup = VQ.tok()
            bank_read_done("st", VQ)
            AQ.wait(t_sup)
            t_sbf = cp(AQ, S_bf[:], S_f[:])
            TQ.wait(t_sbf)
            pend_state = None

            if not is_halo:
                c_row = mb[0:1, 0:512]
                bank_read("row", VQ)
                cp(VQ, c_row, pb_row[:, 0:512])
                bank_read_done("row", VQ)
                t_ms = ts(VQ, c_row, c_row, 1.0 / D, None, ALU.mult)
                AQ.wait(t_ms)
                t_c = rrow(c_row, c_row, 1)
                VQ.wait(t_c)

                TQ.waits(w_tok["wo"], AQ.tok())
                for dc in range(IC):
                    bk = "mm%d" % (dc % 2)
                    pb = pb_mm[dc % 2]
                    bank_write_begin(bk)
                    for kc in range(IC):
                        mm(pb[:], wbuf[1][:, kc * 1024 + dc * 128: kc * 1024 + (dc + 1) * 128],
                           o_bf[:, kc * 512:(kc + 1) * 512],
                           start=(kc == 0), stop=(kc == IC - 1), inc=(1 if kc == IC - 1 else 0))
                    bank_write_end(bk)
                    bank_read(bk, AQ)
                    act(AQ, Xf[:, dc * 512:(dc + 1) * 512], pb[:], AF.Identity)
                    bank_read_done(bk, AQ)
                t_X = AQ.tok()

                VQ.wait(t_X)
                row_stats(lambda i: Xf[:, i * 512:(i + 1) * 512], t_X)
                ts(VQ, rows[:, 0:512], rows[:, 0:512], 1.0 / D, None, ALU.mult)
                ts(VQ, rows[:, 512:1024], rows[:, 512:1024], 1.0 / D, None, ALU.mult)
                tt(VQ, rows[:, 1024:1536], rows[:, 0:512], rows[:, 0:512], ALU.mult)
                tt(VQ, rows[:, 512:1024], rows[:, 512:1024], rows[:, 1024:1536], ALU.subtract)
                tt(VQ, rows[:, 1024:1536], c_row, c_row, ALU.mult)
                t_cv = tt(VQ, rows[:, 512:1024], rows[:, 512:1024], rows[:, 1024:1536], ALU.mult)
                AQ.wait(t_cv)
                t_r3 = rrow(rows[:, 512:1024], rows[:, 512:1024], 0)
                VQ.wait(t_r3)
                tt(VQ, rows[:, 1024:1536], c_row, rows[:, 512:1024], ALU.mult)
                tt(VQ, rows[:, 0:512], rows[:, 0:512], rows[:, 1024:1536], ALU.mult)
                ts(VQ, rows[:, 0:512], rows[:, 0:512], -1.0, None, ALU.mult)
                bcast_row(rows[:, 1024:1536], ab, VQ.tok())
                bcast_row(rows[:, 0:512], bb, VQ.tok())

                for dc in range(IC):
                    tt(VQ, tst, Xf[:, dc * 512:(dc + 1) * 512], ab, ALU.mult)
                    tt(VQ, tst2, tst, bb, ALU.add)
                    tt(VQ, y1f[:, dc * 512:(dc + 1) * 512], tst2,
                       x_cur[:, dc * 512:(dc + 1) * 512], ALU.add)
                t_y1 = VQ.tok()
                SQ.wait(t_y1)
                dma(SQ, bass.AP(y1_d, rtok0, [[NTOK, 128], [128 * NTOK, IC], [1, MAC]]),
                    y1f[:].rearrange("p (a b) -> p a b", a=IC))

                VQ.wait(t_y1)
                AQ.wait(t_y1)
                row_stats(lambda i: y1f[:, i * 512:(i + 1) * 512], t_y1)
                ts(VQ, rows[:, 0:512], rows[:, 0:512], 1.0 / D, None, ALU.mult)
                ts(VQ, rows[:, 512:1024], rows[:, 512:1024], 1.0 / D, None, ALU.mult)
                tt(VQ, rows[:, 1024:1536], rows[:, 0:512], rows[:, 0:512], ALU.mult)
                t_v2 = tt(VQ, rows[:, 512:1024], rows[:, 512:1024], rows[:, 1024:1536],
                          ALU.subtract)
                AQ.wait(t_v2)
                t_r2 = rrow(rows[:, 512:1024], rows[:, 512:1024], 0)
                VQ.wait(t_r2)
                mi = m - 1
                SQ.wait(VQ.tok())
                dma(SQ, bass.AP(ln2_d, mi * MAC, [[1, 1], [1, MAC]]), rows[0:1, 0:512])
                dma(SQ, bass.AP(ln2_d, NTOK + mi * MAC, [[1, 1], [1, MAC]]),
                    rows[0:1, 512:1024])
                bcast_row(rows[:, 0:512], mb, VQ.tok())
                bcast_row(rows[:, 512:1024], rb, VQ.tok())
                for dc in range(IC):
                    tt(VQ, tst, y1f[:, dc * 512:(dc + 1) * 512], mb, ALU.subtract)
                    t_x2 = tt(VQ, tst2, tst, rb, ALU.mult)
                    SQ.wait(t_x2)
                    t_st2 = dma(SQ, bass.AP(xln2_d, rtok0 + dc * 128 * NTOK,
                                            [[NTOK, 128], [1, MAC]]), tst2)
                    red(VQ, maxst[:, (m - 1) * 8 + dc:(m - 1) * 8 + dc + 1], tst2,
                        ALU.max, absv=True)
                    VQ.wait(t_st2)

            if DEBUG_DUMP and m == 1:
                barrier_all()
                cp(VQ, tst, S_f[:, 0:512])
                cp(VQ, tst2, S_f[:, 512:1024])
                t_d1 = VQ.tok()
                SQ.wait(t_d1)
                dma(SQ, bass.AP(dbg_S.tensor, 0, [[1024, 128], [1, 512]]), tst)
                dma(SQ, bass.AP(dbg_S.tensor, 512, [[1024, 128], [1, 512]]), tst2)
                VQ.wait(SQ.tok())
                cp(VQ, tst, pb_st[:, 0:512])
                cp(VQ, tst2, pb_st[:, 512:1024])
                t_d1b = VQ.tok()
                SQ.wait(t_d1b)
                dma(SQ, bass.AP(dbg_pst.tensor, 0, [[1024, 128], [1, 512]]), tst)
                dma(SQ, bass.AP(dbg_pst.tensor, 512, [[1024, 128], [1, 512]]), tst2)
                cp(VQ, SCR[:, 0:512], qt_bf[:])
                cp(VQ, SCR[:, 512:1024], kh_bf[:])
                cp(VQ, SCR[:, 1024:1536].rearrange("p (a b) -> p a b", a=8)[0:64],
                   scm_bf[:].rearrange("p (a b) -> p a b", a=8))
                t_d2 = VQ.tok()
                SQ.wait(t_d2)
                dma(SQ, dbg_qt, SCR[:, 0:512])
                dma(SQ, dbg_kh, SCR[:, 512:1024])
                dma(SQ, dbg_dc, dcol[:])
                dma(SQ, bass.AP(dbg_scm.tensor, 0, [[512, 64], [1, 512]]), SCR[0:64, 1024:1536])
                for dcb in range(IC):
                    cp(VQ, tst, o_bf[:, dcb * 512:(dcb + 1) * 512])
                    t_d3 = VQ.tok()
                    SQ.wait(t_d3)
                    dma(SQ, bass.AP(dbg_o.tensor, dcb * 512, [[4096, 128], [1, 512]]), tst)
                    VQ.wait(SQ.tok())
            barrier_all()

        def finalize_scalar(run_ap, ncols, bar, post_scale, result_cols):
            SQ.wait(VQ.tok())
            for c_ in range(ncols):
                dma(SQ, bass.AP(barsc, c_ * 128, [[1, 1], [1, 128]]),
                    run_ap[:, c_:c_ + 1])
            t_all = SQ.tok()
            SQ.wait(t_all)
            VQ.wait(t_all)
            for c_ in range(ncols):
                t_bl = dma(SQ, mb[0:1, 0:128], bass.AP(barsc, c_ * 128, [[1, 1], [1, 128]]))
                VQ.wait(t_bl)
                t_rd = red(VQ, scal[0:1, 8 + c_:9 + c_], mb[0:1, 0:128], ALU.max)
                SQ.wait(t_rd)
            vfence()
            t_red = VQ.tok()
            SQ.wait(t_red)
            t_bi = dma(SQ, bass.AP(bar_in[bar], 0, [[1, 1], [1, ncols]]),
                       scal[0:1, 8:8 + ncols])
            GQ.wait(t_bi)
            cc_cnt[0] += 1
            ccv = cc_cnt[0]
            GQ.ops.append(lambda e, i=bar_in[bar], o=bar_out[bar], s=ccs:
                          e.collective_compute("AllGather", ALU.bypass,
                                               replica_groups=[list(range(NCORES))],
                                               ins=[i.ap().opt()], outs=[o.ap().opt()])
                          .then_inc(s, 1))
            GQ.wait((ccs, ccv))
            t_bo = dma(GQ, scal[0:1, 32:64], bass.AP(bar_out[bar], 0, [[1, 1], [1, 32]]))
            VQ.wait(t_bo)
            for j, rc in enumerate(result_cols):
                red(VQ, scal[0:1, j:j + 1],
                    scal[0:1, 32:64].rearrange("p (a b) -> p a b", a=8)[:, :, j:j + 1].opt(),
                    ALU.max)
                vfence()
                ts(VQ, scal[0:1, j:j + 1], scal[0:1, j:j + 1], post_scale, 1e-12,
                   ALU.mult, ALU.max)
                vfence()
                recip(VQ, scal[0:1, 16 + j:17 + j], scal[0:1, j:j + 1])
                vfence()
                for src, dstc in ((scal[0:1, j:j + 1], rc), (scal[0:1, 16 + j:17 + j], rc + 1)):
                    bank_write_begin("sc")
                    TQ.wait(VQ.tok())
                    mm(pb_sc[:, 0:1], ones_row[:], src, True, True, inc=1)
                    bank_write_end("sc")
                    bank_read("sc", VQ)
                    cp(VQ, scol[:, dstc:dstc + 1], pb_sc[:, 0:1])
                    bank_read_done("sc", VQ)

        vfence()
        red(VQ, runmax[:], maxst[:, 0:64], ALU.max)
        vfence()
        finalize_scalar(runmax[:], 1, "A", 1.0 / 32767.0, [0])

        upW = ARENA[:].bitcast(BF16)[:, 0:24576]
        gateW = ARENA[:].bitcast(BF16)[:, 24576:49152]
        GQ.wait(t_gather)
        SQ.wait(GQ.tok())
        wload = []
        for icx in range(IC):
            wload.append(dma(SQ, upW[:, icx * H:(icx + 1) * H],
                             bass.AP(ex_out, icx * NEX, [[H, 128], [1, H]])))
            wload.append(dma(SQ, gateW[:, icx * H:(icx + 1) * H],
                             bass.AP(ex_out, icx * NEX + EX_UP, [[H, 128], [1, H]])))
        t_wl = SQ.tok()
        TQ.wait(t_wl)
        xln2f = SCR[:]
        i32v = wbuf[0][:].bitcast(I32)
        codebf = xln_bf[:]
        memset(VQ, maxst[:], 0.0)
        zst = [wbuf[1][:].bitcast(F32)[:, 0:2048], wbuf[1][:].bitcast(F32)[:, 2048:4096]]
        zst_cons = [None, None]
        memset(VQ, dcol[:], 0.0)
        for mi in range(NRMAC):
            rt0 = mi * MAC
            t_x2l = dma(SQ, xln2f.rearrange("p (a b) -> p a b", a=IC),
                        bass.AP(xln2_d, rt0, [[NTOK, 128], [128 * NTOK, IC], [1, MAC]]))
            VQ.wait(t_x2l)
            ts(VQ, i32v[:, 0:4096], xln2f, scol[:, 1:2], None, ALU.mult)
            t_code = cp(VQ, codebf, i32v[:, 0:4096])
            TQ.wait(t_code)
            for tix, (Wv_, zd, runc) in enumerate(((upW, zup_d, 0), (gateW, zgate_d, 1))):
                for g in range(6):
                    par_ = g % 2
                    zstage = zst[par_]
                    for t_ in (zst_cons[par_] or []):
                        AQ.wait(t_)
                    for oc4 in range(4):
                        oc = g * 4 + oc4
                        bk = "mm%d" % (oc % 2)
                        pb = pb_mm[oc % 2]
                        bank_write_begin(bk)
                        for icx in range(IC):
                            mm(pb[:], Wv_[:, icx * H + oc * 128: icx * H + (oc + 1) * 128],
                               codebf[:, icx * 512:(icx + 1) * 512],
                               start=(icx == 0), stop=(icx == IC - 1),
                               inc=(1 if icx == IC - 1 else 0))
                        bank_write_end(bk)
                        bank_read(bk, AQ)
                        act(AQ, zstage[:, oc4 * 512:(oc4 + 1) * 512], pb[:],
                            AF.Identity, scale=scol[:, 0:1])
                        bank_read_done(bk, AQ)
                    t_ev = AQ.tok()
                    VQ.wait(t_ev)
                    t_rm = red(VQ, maxst[:, tix * 48 + mi * 6 + g: tix * 48 + mi * 6 + g + 1],
                               zstage[:, 0:2048], ALU.max, absv=True)
                    SQ.wait(t_ev)
                    t_zs = dma(SQ, bass.AP(zd, g * 4 * 128 * NTOK + rt0,
                                           [[NTOK, 128], [128 * NTOK, 4], [1, MAC]]),
                               zstage[:, 0:2048].rearrange("p (a b) -> p a b", a=4))
                    zst_cons[par_] = [t_rm, t_zs]
                    if DEBUG_F and mi == 0 and tix == 0 and g == 0:
                        t_dz = dma(SQ, dbg_z, zstage[:, 0:2048])
                        zst_cons[par_].append(t_dz)
            SQ.wait(VQ.tok())
            SQ.wait(TQ.tok())
            VQ.wait(TQ.tok())
        barrier_all()

        vfence()
        if DEBUG_F:
            SQ.wait(VQ.tok())
            t_dmx = dma(SQ, dbg_mx, maxst[:])
            VQ.wait(t_dmx)
        red(VQ, dcol[:, 0:1], maxst[:, 0:48], ALU.max)
        red(VQ, dcol[:, 1:2], maxst[:, 48:96], ALU.max)
        vfence()
        finalize_scalar(dcol[:, 0:2], 2, "B", 1.0 / 32767.0, [2, 4])

        memset(VQ, maxst[:], 0.0)
        zgv = wbuf[1][:].bitcast(F32)
        zgi = wbuf[1][:].bitcast(I32)
        gq_f = SCR[:, 0:2048]
        fs3 = SCR[:, 2048:4096]
        GW = 4
        i32g = wbuf[0][:].bitcast(I32)
        u32 = i32g[:, 2048:4096]
        for mi in range(NRMAC):
            rt0 = mi * MAC
            for g in range(6):
                t_zl = dma(SQ, zgv[:, 0:GW * 512].rearrange("p (a b) -> p a b", a=GW),
                           bass.AP(zgate_d, g * GW * 128 * NTOK + rt0,
                                   [[NTOK, 128], [128 * NTOK, GW], [1, MAC]]))
                VQ.wait(t_zl)
                t_ci = ts(VQ, i32g[:, 0:GW * 512], zgv[:, 0:GW * 512], scol[:, 5:6], None, ALU.mult)
                GQ.wait(t_ci)
                t_gst = dma(GQ, bass.AP(gcode_d, g * GW * 128 * NTOK + rt0,
                                        [[NTOK, 128], [128 * NTOK, GW], [1, MAC]]),
                            i32g[:, 0:GW * 512].rearrange("p (a b) -> p a b", a=GW))
                AQ.wait(t_ci)
                t_sg = act(AQ, fs3, i32g[:, 0:GW * 512], AF.Sigmoid, scale=scol[:, 4:5])
                t_gqa = act(AQ, gq_f, i32g[:, 0:GW * 512], AF.Identity, scale=scol[:, 4:5])
                VQ.wait(t_sg)
                ts(VQ, zgi[:, 0:GW * 512], fs3, 65536.0, 65535.0, ALU.mult, ALU.min)
                VQ.wait(t_gqa)
                t_g2 = tt(VQ, fs3, gq_f, zgi[:, 0:GW * 512], ALU.mult)
                red(VQ, maxst[:, mi * 6 + g: mi * 6 + g + 1], fs3, ALU.max, absv=True)
                VQ.wait(t_gst)
                AQ.wait(VQ.tok())
                SQ.wait(VQ.tok())
        barrier_all()

        vfence()
        red(VQ, dcol[:, 2:3], maxst[:, 0:48], ALU.max)
        vfence()
        finalize_scalar(dcol[:, 2:3], 1, "C", (1.0 / 65536.0) / 32767.0, [6])
        vfence()
        ts(VQ, scol[:, 10:11], scol[:, 7:8], 1.0 / 65536.0, None, ALU.mult)
        vfence()

        memset(VQ, maxst[:], 0.0)
        for mi in range(NRMAC):
            rt0 = mi * MAC
            for g in range(6):
                base = g * GW * 128 * NTOK + rt0
                apz = [[NTOK, 128], [128 * NTOK, GW], [1, MAC]]
                t_gl = dma(GQ, gq_f.rearrange("p (a b) -> p a b", a=GW),
                           bass.AP(gcode_d, base, apz))
                t_ul = dma(SQ, zgv[:, 0:GW * 512].rearrange("p (a b) -> p a b", a=GW),
                           bass.AP(zup_d, base, apz))
                AQ.wait(t_gl)
                t_sg = act(AQ, fs3, gq_f, AF.Sigmoid, scale=scol[:, 4:5])
                t_gqa = act(AQ, gq_f, gq_f, AF.Identity, scale=scol[:, 4:5])
                VQ.wait(t_sg)
                ts(VQ, i32g[:, 0:GW * 512], fs3, 65536.0, 65535.0, ALU.mult, ALU.min)
                VQ.wait(t_gqa)
                tt(VQ, fs3, gq_f, i32g[:, 0:GW * 512], ALU.mult)
                t_g2c = ts(VQ, i32g[:, 0:GW * 512], fs3, scol[:, 10:11], None, ALU.mult)
                GQ.wait(t_g2c)
                t_g2st = dma(GQ, bass.AP(g2code_d, base, apz),
                             i32g[:, 0:GW * 512].rearrange("p (a b) -> p a b", a=GW))
                AQ.wait(t_g2c)
                t_g2qa = act(AQ, fs3, i32g[:, 0:GW * 512], AF.Identity,
                             scale=scol[:, 6:7])
                VQ.wait(t_ul)
                t_uc = ts(VQ, u32[:, 0:GW * 512], zgv[:, 0:GW * 512], scol[:, 3:4], None, ALU.mult)
                GQ.wait(t_uc)
                t_ust = dma(GQ, bass.AP(ucode_d, base, apz),
                            u32[:, 0:GW * 512].rearrange("p (a b) -> p a b", a=GW))
                VQ.wait(t_g2qa)
                t_op = tt(VQ, gq_f, fs3, u32[:, 0:GW * 512], ALU.mult)
                red(VQ, maxst[:, mi * 6 + g: mi * 6 + g + 1], gq_f, ALU.max, absv=True)
                VQ.wait(t_ust)
                VQ.wait(t_g2st)
                AQ.wait(VQ.tok())
                GQ.wait(VQ.tok())
                SQ.wait(VQ.tok())
        barrier_all()

        vfence()
        red(VQ, dcol[:, 3:4], maxst[:, 0:48], ALU.max)
        vfence()
        tt(VQ, dcol[:, 3:4], dcol[:, 3:4], scol[:, 2:3], ALU.mult)
        vfence()
        finalize_scalar(dcol[:, 3:4], 1, "D", 1.0 / 32767.0, [8])
        vfence()
        tt(VQ, scol[:, 11:12], scol[:, 9:10], scol[:, 2:3], ALU.mult)
        vfence()

        downW = ARENA[:].bitcast(BF16)[:, 0:24576]
        GQ.wait(SQ.tok())
        dwl = []
        for icx in range(NCORES):
            for icl in range(3):
                icg = icx * 3 + icl
                dwl.append(dma(SQ, downW[:, icg * D:(icg + 1) * D],
                               bass.AP(ex_out, icx * NEX + EX_UP + EX_GATE + icl * 128 * D,
                                       [[D, 128], [1, D]])))
        t_dwl = SQ.tok()
        TQ.wait(t_dwl)
        Xs = ARENA[:, 12288:16384]
        y1s = ARENA[:, 16384:20480]
        y2s = ARENA[:, 20480:24576]

        def codebf_sl(ch):
            if ch < 16:
                return wbuf[1][:, ch * 512:(ch + 1) * 512]
            return o_bf[:, (ch - 16) * 512:(ch - 15) * 512]

        for mi in range(NRMAC):
            rt0 = mi * MAC
            t_y1l = dma(SQ, y1s.rearrange("p (a b) -> p a b", a=IC),
                        bass.AP(y1_d, rt0, [[NTOK, 128], [128 * NTOK, IC], [1, MAC]]))
            waveA = [pb_mm[0], pb_mm[1], pb_o, pb_sc]
            waveB = [pb_mm[0], pb_mm[1], pb_st[:, 0:512], pb_st[:, 512:1024]]
            for bk in ("mm0", "mm1", "o", "sc"):
                bank_write_begin(bk)
            code_toks = []
            for g in range(6):
                base = g * GW * 128 * NTOK + rt0
                apz = [[NTOK, 128], [128 * NTOK, GW], [1, MAC]]
                t_g2l = dma(GQ, fs3.rearrange("p (a b) -> p a b", a=GW),
                            bass.AP(g2code_d, base, apz))
                t_ucl = dma(GQ, gq_f.rearrange("p (a b) -> p a b", a=GW),
                            bass.AP(ucode_d, base, apz))
                AQ.wait(t_g2l)
                t_g2qa = act(AQ, fs3, fs3, AF.Identity, scale=scol[:, 6:7])
                VQ.waits(t_g2qa, t_ucl)
                tt(VQ, fs3, fs3, gq_f, ALU.mult)
                t_oc = ts(VQ, i32g[:, 0:GW * 512], fs3, scol[:, 11:12], None, ALU.mult)
                for oc6 in range(GW):
                    ch = g * GW + oc6
                    cp(VQ, codebf_sl(ch), i32g[:, oc6 * 512:(oc6 + 1) * 512])
                t_cg = VQ.tok()
                code_toks.append(t_cg)
                GQ.wait(t_cg)
                TQ.waits(t_cg, t_dwl)
                for oc6 in range(GW):
                    ch = g * GW + oc6
                    for dc in range(4):
                        mm(waveA[dc][:, 0:512], downW[:, ch * D + dc * 128: ch * D + (dc + 1) * 128],
                           codebf_sl(ch), start=(ch == 0), stop=(ch == 23),
                           inc=(1 if ch == 23 and dc == 3 else 0))
            for bk in ("mm0", "mm1", "o", "sc"):
                bank_write_end(bk)
            t_ev01 = None
            for dc in range(4):
                bkn = "mm0" if dc == 0 else ("mm1" if dc == 1 else ("o" if dc == 2 else "sc"))
                bank_read(bkn, AQ)
                act(AQ, Xs[:, dc * 512:(dc + 1) * 512], waveA[dc][:, 0:512], AF.Identity,
                    scale=scol[:, 8:9])
                bank_read_done(bkn, AQ)
                if dc == 1:
                    t_ev01 = AQ.tok()
            TQ.wait(t_ev01)
            for bk in ("mm0", "mm1", "st"):
                bank_write_begin(bk)
            for ch in range(24):
                for dc in range(4, 8):
                    mm(waveB[dc - 4][:, 0:512] if dc < 6 else waveB[dc - 4],
                       downW[:, ch * D + dc * 128: ch * D + (dc + 1) * 128],
                       codebf_sl(ch), start=(ch == 0), stop=(ch == 23),
                       inc=(1 if ch == 23 and dc == 7 else 0))
            for bk in ("mm0", "mm1", "st"):
                bank_write_end(bk)
            for dc in range(4, 8):
                bkn = "mm0" if dc == 4 else ("mm1" if dc == 5 else "st")
                bank_read(bkn, AQ)
                act(AQ, Xs[:, dc * 512:(dc + 1) * 512],
                    waveB[dc - 4][:, 0:512] if dc < 6 else waveB[dc - 4],
                    AF.Identity, scale=scol[:, 8:9])
                bank_read_done(bkn, AQ)
            t_X4 = AQ.tok()
            VQ.wait(t_X4)
            row_stats(lambda i: Xs[:, i * 512:(i + 1) * 512], t_X4)
            ts(VQ, rows[:, 0:512], rows[:, 0:512], 1.0 / D, None, ALU.mult)
            ts(VQ, rows[:, 512:1024], rows[:, 512:1024], 1.0 / D, None, ALU.mult)
            tt(VQ, rows[:, 1024:1536], rows[:, 0:512], rows[:, 0:512], ALU.mult)
            t_v4 = tt(VQ, rows[:, 512:1024], rows[:, 512:1024], rows[:, 1024:1536],
                      ALU.subtract)
            AQ.wait(t_v4)
            t_r4 = rrow(rows[:, 512:1024], rows[:, 512:1024], 0)
            VQ.wait(t_r4)
            tt(VQ, rows[:, 0:512], rows[:, 0:512], rows[:, 512:1024], ALU.mult)
            ts(VQ, rows[:, 0:512], rows[:, 0:512], -1.0, None, ALU.mult)
            bcast_row(rows[:, 512:1024], ab, VQ.tok())
            bcast_row(rows[:, 0:512], bb, VQ.tok())
            VQ.wait(t_y1l)
            for dc in range(IC):
                tt(VQ, tst, Xs[:, dc * 512:(dc + 1) * 512], ab, ALU.mult)
                tt(VQ, tst2, tst, bb, ALU.add)
                tt(VQ, y2s[:, dc * 512:(dc + 1) * 512], tst2,
                   y1s[:, dc * 512:(dc + 1) * 512], ALU.add)
            t_y2d = VQ.tok()
            SQ.wait(t_y2d)
            dma(SQ, bass.AP(outT.tensor, rt0, [[NTOK, 128], [128 * NTOK, IC], [1, MAC]]),
                y2s.rearrange("p (a b) -> p a b", a=IC))
            GQ.wait(SQ.tok())
            VQ.wait(SQ.tok())
        if DEBUG_F:
            dma(SQ, dbg_scol, scol[:, 0:12])
            dma(SQ, dbg_w, upW[:, 0:3072])
            for mm_ in range(NRMAC):
                apv = [[NTOK, 128], [128 * NTOK, IC], [1, MAC]]
                tl1 = dma(SQ, SCR[:].rearrange("p (a b) -> p a b", a=IC),
                          bass.AP(y1_d, mm_ * MAC, apv))
                SQ.wait(tl1)
                dma(SQ, bass.AP(dbg_y1.tensor, mm_ * MAC, apv),
                    SCR[:].rearrange("p (a b) -> p a b", a=IC))
                tl2 = dma(SQ, SCR[:].rearrange("p (a b) -> p a b", a=IC),
                          bass.AP(xln2_d, mm_ * MAC, apv))
                SQ.wait(tl2)
                dma(SQ, bass.AP(dbg_xl2.tensor, mm_ * MAC, apv),
                    SCR[:].rearrange("p (a b) -> p a b", a=IC))
                SQ.wait(SQ.tok())
            for j_, mm_ in enumerate((0, 6)):
                apz3 = [[NTOK, 128], [128 * NTOK, 24], [1, MAC]]
                apd3 = [[1024, 128], [128 * 1024, 24], [1, MAC]]
                for (srcd, dstd) in ((zup_d, dbg_zu), (ucode_d, dbg_uc), (g2code_d, dbg_g2c)):
                    tlz = dma(GQ, wbuf[0][:].bitcast(F32)[:, 0:4096].rearrange(
                        "p (a b) -> p a b", a=8)[:, 0:8, 0:512],
                        bass.AP(srcd, mm_ * MAC, [[NTOK, 128], [128 * NTOK, 8], [1, MAC]]))
                    GQ.wait(tlz)
                    tsz = dma(GQ, bass.AP(dstd.tensor if hasattr(dstd, "tensor") else dstd,
                                          j_ * MAC, [[1024, 128], [128 * 1024, 8], [1, MAC]]),
                              wbuf[0][:].bitcast(F32)[:, 0:4096].rearrange(
                        "p (a b) -> p a b", a=8)[:, 0:8, 0:512])
                    GQ.wait(tsz)
        SQ.wait(SQ.tok())
        GQ.wait(GQ.tok())

        with nc.Block() as block:
            @block.sync
            def _(q):
                for f_ in SQ.ops:
                    f_(q)

            @block.gpsimd
            def _(q):
                for f_ in GQ.ops:
                    f_(q)

            @block.tensor
            def _(q):
                for f_ in TQ.ops:
                    f_(q)

            @block.vector
            def _(q):
                for f_ in VQ.ops:
                    f_(q)

            @block.scalar
            def _(q):
                for f_ in AQ.ops:
                    f_(q)

    return nc


def _stage_inputs(inputs):
    x = np.asarray(inputs["L0_attn_x"])
    xTs = [np.ascontiguousarray(x[s].T) for s in range(2)]
    eye = np.eye(128, dtype=np.float32)
    mask = np.zeros((C, C), dtype=np.uint8)
    for s_ in range(C):
        mask[s_, s_:] = 1
    nw = np.ascontiguousarray(np.asarray(inputs["L0_attn_norm_w"]).reshape(IC, 128).T)
    nw = np.ascontiguousarray(nw.astype(np.float32))
    w_up = np.ascontiguousarray(np.asarray(inputs["L0_ffn_w_up"], dtype=np.float32))
    w_gate = np.ascontiguousarray(np.asarray(inputs["L0_ffn_w_gate"], dtype=np.float32))
    w_down = np.ascontiguousarray(np.asarray(inputs["L0_ffn_w_down"], dtype=np.float32))
    base = dict(
        wqT=np.ascontiguousarray(np.asarray(inputs["L0_attn_Wq"], dtype=np.float32).T),
        wkT=np.ascontiguousarray(np.asarray(inputs["L0_attn_Wk"], dtype=np.float32).T),
        wvT=np.ascontiguousarray(np.asarray(inputs["L0_attn_Wv"], dtype=np.float32).T),
        woT=np.ascontiguousarray(np.asarray(inputs["L0_attn_Wo"], dtype=np.float32).T),
        nwfm=nw, eye_h=eye, mask_h=mask,
    )
    maps = []
    for core in range(NCORES):
        seq, seg = core // 4, core % 4
        t0 = seg * NTOK
        xs = np.zeros((D, TOT), dtype=np.float32)
        lo = t0 - HALO
        if lo < 0:
            xs[:, HALO:] = xTs[seq][:, t0:t0 + NTOK]
        else:
            xs[:, :] = xTs[seq][:, lo:t0 + NTOK]
        sl = []
        for j in range(OC3):
            sl.append(w_up[j * 128:(j + 1) * 128, core * 128:(core + 1) * 128])
        for j in range(OC3):
            sl.append(w_gate[j * 128:(j + 1) * 128, core * 128:(core + 1) * 128])
        for icl in range(3):
            icg = core * 3 + icl
            for hc in range(IC):
                sl.append(w_down[hc * 128:(hc + 1) * 128, icg * 128:(icg + 1) * 128])
        m = dict(base)
        m["xT"] = np.ascontiguousarray(xs)
        m["dq_sl"] = np.ascontiguousarray(np.concatenate(sl, axis=0))
        maps.append(m)
    return maps


_NC_CACHE = None


def kernel(L0_attn_x, L0_attn_Wq, L0_attn_Wk, L0_attn_Wv, L0_attn_Wo,
           L0_attn_norm_w, L0_ffn_w_gate, L0_ffn_w_up, L0_ffn_w_down):
    global _NC_CACHE
    inputs = dict(L0_attn_x=L0_attn_x, L0_attn_Wq=L0_attn_Wq, L0_attn_Wk=L0_attn_Wk,
                  L0_attn_Wv=L0_attn_Wv, L0_attn_Wo=L0_attn_Wo,
                  L0_attn_norm_w=L0_attn_norm_w, L0_ffn_w_gate=L0_ffn_w_gate,
                  L0_ffn_w_up=L0_ffn_w_up, L0_ffn_w_down=L0_ffn_w_down)
    if _NC_CACHE is None:
        _NC_CACHE = build_program()
    nc = _NC_CACHE
    maps = _stage_inputs(inputs)
    res = run_bass_kernel_spmd(nc, maps, core_ids=list(range(NCORES)))
    out = np.empty((2, 16384, D), dtype=np.float32)
    for core in range(NCORES):
        seq, seg = core // 4, core % 4
        o = np.asarray(res.results[core]["outT"])
        out[seq, seg * NTOK:(seg + 1) * NTOK, :] = o.T
    return out


# version 12
# speedup vs baseline: 1.2409x; 1.0543x over previous
import numpy as np
import ml_dtypes
from contextlib import ExitStack

import concourse.bass as bass
import concourse.mybir as mybir
from concourse.bass_utils import run_bass_kernel_spmd

F32 = mybir.dt.float32
BF16 = mybir.dt.bfloat16
I32 = mybir.dt.int32
I16 = mybir.dt.int16
AF = mybir.ActivationFunctionType
ALU = mybir.AluOpType
AX = mybir.AxisListType

D = 1024
H = 3072
NTOK = 4096
HALO = 512
TOT = NTOK + HALO
MAC = 512
NMAC = TOT // MAC
NRMAC = NMAC - 1
C = 64
NCH = MAC // C
IC = D // 128
OC3 = H // 128
NCORES = 8

EX_UP = 128 * H
EX_GATE = 128 * H
EX_DOWN = 3 * 128 * D
NEX = EX_UP + EX_GATE + EX_DOWN

STUB_F = False
DEBUG_DUMP = False
DEBUG_F = False
DEBUG_C = False


class Q:
    def __init__(self, sem):
        self.ops = []
        self.sem = sem
        self.cnt = 0

    def wait(self, tok):
        if tok is None:
            return
        sem, val = tok
        self.ops.append(lambda e, s=sem, v=val: e.wait_ge(s, v))

    def waits(self, *toks):
        for t in toks:
            self.wait(t)

    def run(self, fn, inc=1):
        self.cnt += inc
        tokv = self.cnt

        def cl(e, fn=fn, inc=inc, s=self.sem):
            ins = fn(e)
            if inc:
                ins.then_inc(s, inc)
        self.ops.append(cl)
        return (self.sem, tokv)

    def tok(self):
        return (self.sem, self.cnt)


def build_program():
    nc = bass.Bass("TRN2", target_bir_lowering=False)

    def din(name, shape, dtype=F32):
        return nc.dram_tensor(name, shape, dtype, kind="ExternalInput").ap()

    xT = din("xT", [D, TOT])
    wqT = din("wqT", [D, D])
    wkT = din("wkT", [D, D])
    wvT = din("wvT", [D, D])
    woT = din("woT", [D, D])
    nwfm = din("nwfm", [128, IC])
    eye_h = din("eye_h", [128, 128])
    mask_h = din("mask_h", [C, C], mybir.dt.uint8)
    dq_sl = din("dq_sl", [3 * OC3 * 128, 128])

    outT = nc.dram_tensor("outT", [D, NTOK], F32, kind="ExternalOutput").ap()
    if DEBUG_F:
        dbg_scol = nc.dram_tensor("dbg_scol", [128, 12], F32, kind="ExternalOutput").ap()
        dbg_z = nc.dram_tensor("dbg_z", [128, 2048], F32, kind="ExternalOutput").ap()
        dbg_w = nc.dram_tensor("dbg_w", [128, 3072], mybir.dt.bfloat16, kind="ExternalOutput").ap()
        dbg_mx = nc.dram_tensor("dbg_mx", [128, 96], F32, kind="ExternalOutput").ap()
        dbg_y1 = nc.dram_tensor("dbg_y1", [D, NTOK], F32, kind="ExternalOutput").ap()
        dbg_zu = nc.dram_tensor("dbg_zu", [H, 1024], F32, kind="ExternalOutput").ap()
        dbg_uc = nc.dram_tensor("dbg_uc", [H, 1024], F32, kind="ExternalOutput").ap()
        dbg_g2c = nc.dram_tensor("dbg_g2c", [H, 1024], F32, kind="ExternalOutput").ap()
        dbg_xl2 = nc.dram_tensor("dbg_xl2", [D, NTOK], F32, kind="ExternalOutput").ap()
    if DEBUG_DUMP:
        dbg_S = nc.dram_tensor("dbg_S", [128, 1024], F32, kind="ExternalOutput").ap()
        dbg_o = nc.dram_tensor("dbg_o", [128, 4096], F32, kind="ExternalOutput").ap()
        dbg_qt = nc.dram_tensor("dbg_qt", [128, 512], F32, kind="ExternalOutput").ap()
        dbg_kh = nc.dram_tensor("dbg_kh", [128, 512], F32, kind="ExternalOutput").ap()
        dbg_dc = nc.dram_tensor("dbg_dc", [128, 8], F32, kind="ExternalOutput").ap()
        dbg_scm = nc.dram_tensor("dbg_scm", [64, 512], F32, kind="ExternalOutput").ap()
        dbg_pst = nc.dram_tensor("dbg_pst", [128, 1024], F32, kind="ExternalOutput").ap()

    y1_d = nc.dram_tensor("y1_d", [D, NTOK], F32)
    xln2_d = nc.dram_tensor("xln2_d", [D, NTOK], F32)
    _kw = dict(kind="ExternalOutput") if DEBUG_C else {}
    zup_d = nc.dram_tensor("zup_d", [H, NTOK], F32, **_kw)
    zgate_d = nc.dram_tensor("zgate_d", [H, NTOK], F32, **_kw)
    gcode_d = nc.dram_tensor("gcode_d", [H, NTOK], I16, **_kw)
    g2code_d = nc.dram_tensor("g2code_d", [H, NTOK], I16, **_kw)
    ucode_d = nc.dram_tensor("ucode_d", [H, NTOK], I16, **_kw)
    ln2_d = nc.dram_tensor("ln2_d", [2, NTOK], F32)
    wq_bf_d = nc.dram_tensor("wq_bf_d", [D, D], BF16)
    wk_bf_d = nc.dram_tensor("wk_bf_d", [D, D], BF16)
    wv_bf_d = nc.dram_tensor("wv_bf_d", [D, D], BF16)
    wo_bf_d = nc.dram_tensor("wo_bf_d", [D, D], BF16)
    ex_in = nc.dram_tensor("ex_in", [1, NEX], BF16)
    ex_out = nc.dram_tensor("ex_out", [NCORES, NEX], BF16, addr_space="Shared")
    barsc = nc.dram_tensor("barsc", [4, 128], F32)
    bar_in = {}
    bar_out = {}
    for b in ("A", "B", "C", "D"):
        bar_in[b] = nc.dram_tensor(f"bar{b}_in", [1, 4], F32)
        bar_out[b] = nc.dram_tensor(f"bar{b}_out", [NCORES, 4], F32, addr_space="Shared")

    with ExitStack() as st:
        E = st.enter_context

        def sbt(name, shape, dtype=F32):
            return E(nc.sbuf_tensor(name, shape, dtype))

        ARENA = sbt("ARENA", [128, 24576], F32)
        SCR = sbt("SCR", [128, 4096], F32)
        ar = ARENA[:]
        xb = [ar[:, 0:4096], ar[:, 4096:8192]]
        qf = ar[:, 8192:12288]
        ff = ar[:, 12288:16384]
        Xf = ar[:, 16384:20480]
        y1f = ar[:, 20480:24576]
        sc_ = SCR[:]
        gp1 = sc_[:, 0:512]
        Apr = sc_[:, 512:1024]
        Em = sc_[:, 1024:1536]
        Ep = sc_[:, 1536:2048]
        ktf = sc_[:, 2048:2560]
        tst = sc_[:, 2560:3072]
        tst2 = sc_[:, 3072:3584]
        gps = sc_[:, 3584:4096]

        wbuf = [sbt(f"wbuf{i}", [128, 8192], BF16) for i in range(2)]
        xln_bf = sbt("xln_bf", [128, 4096], BF16)
        o_bf = sbt("o_bf", [128, 4096], BF16)
        v_bf = sbt("v_bf", [C, D], BF16)
        qt_bfp = [sbt(f"qt_bf{i}", [128, 8 * C], BF16) for i in range(2)]
        kt_bfp = [sbt(f"kt_bf{i}", [128, 8 * C], BF16) for i in range(2)]
        kh_bf = sbt("kh_bf", [128, 8 * C], BF16)
        khT_bf = sbt("khT_bf", [C, 8 * 128], BF16)
        scm_bf = sbt("scm_bf", [C, 8 * C], BF16)
        S_f = sbt("S_f", [128, 8 * 128], F32)
        S_bf = sbt("S_bf", [128, 8 * 128], BF16)
        BC = sbt("BC", [128, 2048], F32)
        mb, rb = BC[:, 0:512], BC[:, 512:1024]
        ab, bb = BC[:, 1024:1536], BC[:, 1536:2048]
        sqst = sbt("sqst", [128, 512], F32)
        dcolp = [sbt(f"dcol{i}", [128, 8], F32) for i in range(2)]
        dcol = dcolp[0]
        cc8 = sbt("cc8", [128, 4], F32)
        eye_bf = sbt("eye_bf", [128, 128], BF16)
        mask_c = sbt("mask_c", [C, C], mybir.dt.uint8)
        zrs = sbt("zrs", [128, C], F32)
        nw_c = sbt("nw_c", [128, IC], F32)
        ones_col = sbt("ones_col", [128, 1], F32)
        ones_row = sbt("ones_row", [1, 128], F32)
        rows = sbt("rows", [1, 1536], F32)
        runmax = sbt("runmax", [128, 1], F32)
        scal = sbt("scal", [1, 64], F32)
        scol = sbt("scol", [128, 12], F32)
        dq_nat = SCR[:, 0:128]
        dq_s4 = SCR[:, 256:260]
        dq_r4 = SCR[:, 260:264]
        dq_t = SCR[:, 128:256]
        dq_bf = SCR[:].bitcast(BF16)[:, 528:656]
        dq_st = SCR[:].bitcast(BF16)[:, 656:784]
        maxst = sbt("maxst", [128, 96], F32)

        pb_mm = [E(nc.psum_tensor(f"pb_mm{i}", [128, 512], F32)) for i in range(2)]
        pb_o = E(nc.psum_tensor("pb_o", [128, 512], F32))
        pb_sc = E(nc.psum_tensor("pb_sc", [128, 512], F32))
        pb_tr = E(nc.psum_tensor("pb_tr", [128, 1024], BF16))
        pb_st = E(nc.psum_tensor("pb_st", [128, 1024], F32))
        pb_row = E(nc.psum_tensor("pb_row", [1, 512], F32))

        SQ = Q(E(nc.semaphore("sS")))
        GQ = Q(E(nc.semaphore("sG")))
        TQ = Q(E(nc.semaphore("sT")))
        VQ = Q(E(nc.semaphore("sV")))
        AQ = Q(E(nc.semaphore("sA")))
        ccs = E(nc.semaphore("ccs"))
        cc_cnt = [0]
        ALLQ = [SQ, GQ, TQ, VQ, AQ]

        def barrier_all():
            toks = [q.tok() for q in ALLQ]
            for q in ALLQ:
                for t in toks:
                    if t[0] is not q.sem:
                        q.wait(t)

        def mm(out_ap, lhsT_ap, rhs_ap, start, stop, inc=0):
            return TQ.run(lambda e, o=out_ap, l=lhsT_ap, r=rhs_ap, a=start, b=stop:
                          e.matmul(o, l, r, start=a, stop=b), inc=inc)

        def tr(out_ap, in_ap, inc=1):
            return TQ.run(lambda e, o=out_ap, i=in_ap, d=eye_bf[:]:
                          e.transpose(o, i, d), inc=inc)

        def act(q, out_ap, in_ap, func, bias=0.0, scale=1.0, accum=None, inc=1):
            return q.run(lambda e, o=out_ap, i=in_ap, f=func, b=bias, s=scale, a=accum:
                         e.activation(o, i, f, bias=b, scale=s, accum_out=a), inc=inc)

        def tt(q, out_ap, a_ap, b_ap, op, inc=1):
            return q.run(lambda e, o=out_ap, a=a_ap, b=b_ap, p=op:
                         e.tensor_tensor(o, a, b, p), inc=inc)

        def ts(q, out_ap, in_ap, s1, s2, op0, op1=ALU.bypass, inc=1):
            return q.run(lambda e, o=out_ap, i=in_ap, a=s1, b=s2, p0=op0, p1=op1:
                         e.tensor_scalar(o, i, a, b, p0, p1), inc=inc)

        def stt(q, out_ap, in0_ap, scalar, in1_ap, op0, op1, inc=1):
            return q.run(lambda e, o=out_ap, i=in0_ap, s=scalar, j=in1_ap, p0=op0, p1=op1:
                         e.scalar_tensor_tensor(o, i, s, j, p0, p1), inc=inc)

        def cp(q, out_ap, in_ap, inc=1):
            def fn(e, o=out_ap, i=in_ap):
                if hasattr(e, "tensor_copy"):
                    return e.tensor_copy(out=o, in_=i)
                return e.copy(o, i)
            return q.run(fn, inc=inc)

        def red(q, out_ap, in_ap, op, absv=False, inc=1):
            return q.run(lambda e, o=out_ap, i=in_ap, p=op, a=absv:
                         e.tensor_reduce(o, i, axis=AX.X, op=p, apply_absolute_value=a),
                         inc=inc)

        def scan_op(q, out_ap, d0, d1, init, op0, op1, inc=1):
            return q.run(lambda e, o=out_ap, a=d0, b=d1, i=init, p0=op0, p1=op1:
                         e.tensor_tensor_scan(o, a, b, i, p0, p1), inc=inc)

        def dma(q, out_ap, in_ap):
            return q.run(lambda e, o=out_ap, i=in_ap: e.dma_start(out=o, in_=i), inc=16)

        def memset(q, ap_, val, inc=1):
            return q.run(lambda e, a=ap_, v=val: e.memset(a, v), inc=inc)

        def select(q, out_ap, mask_ap, t_ap, f_ap, inc=1):
            return q.run(lambda e, o=out_ap, m=mask_ap, a=t_ap, b=f_ap:
                         e.select(o, m, a, b), inc=inc)

        def recip(q, out_ap, in_ap, inc=1):
            return q.run(lambda e, o=out_ap, i=in_ap: e.reciprocal(o, i), inc=inc)

        def vfence():
            t = VQ.tok()
            GQ.wait(t)
            t2 = GQ.run(lambda e: e.engine_nop(), inc=1)
            VQ.wait(t2)

        def allgather(in_t, out_t):
            cc_cnt[0] += 1
            val = cc_cnt[0]
            GQ.ops.append(lambda e, i=in_t, o=out_t, s=ccs:
                          e.collective_compute("AllGather", ALU.bypass,
                                               replica_groups=[list(range(NCORES))],
                                               ins=[i.ap().opt()], outs=[o.ap().opt()])
                          .then_inc(s, 1))
            return (ccs, val)

        def rrow(dst, src, eps_col):
            act(AQ, dst, src, AF.Ln, bias=cc8[0:1, eps_col:eps_col + 1])
            return act(AQ, dst, dst, AF.Exp, scale=cc8[0:1, 2:3])

        bank_prod = {}
        bank_cons = {}

        def bank_write_begin(bk):
            for t in bank_cons.get(bk, []):
                TQ.wait(t)
            bank_cons[bk] = []

        def bank_write_end(bk):
            bank_prod[bk] = TQ.tok()

        def bank_read(bk, q):
            q.wait(bank_prod.get(bk))

        def bank_read_done(bk, q):
            bank_cons.setdefault(bk, []).append(q.tok())

        def bcast_row(row_ap, dst_ap, src_tok):
            bank_write_begin("sc")
            TQ.wait(src_tok)
            mm(pb_sc[:, 0:512], ones_row[:], row_ap, True, True, inc=1)
            bank_write_end("sc")
            bank_read("sc", VQ)
            t = cp(VQ, dst_ap, pb_sc[:, 0:512])
            bank_read_done("sc", VQ)
            return t

        t_eye = dma(GQ, eye_bf[:], eye_h)
        t_mask = dma(SQ, mask_c[:], mask_h)
        t_nw = dma(SQ, nw_c[:], nwfm)
        memset(VQ, zrs[:], 0.0)
        memset(VQ, ones_col[:], 1.0)
        memset(VQ, ones_row[:], 1.0)
        memset(VQ, S_f[:], 0.0)
        memset(VQ, S_bf[:], 0.0)
        memset(VQ, runmax[:], 0.0)
        memset(VQ, cc8[:, 0:1], 1e-5)
        memset(VQ, cc8[:, 1:2], 1e-6)
        memset(VQ, cc8[:, 2:3], -0.5)

        t_wq = dma(GQ, wq_bf_d.ap(), wqT)
        t_wk = dma(GQ, wk_bf_d.ap(), wkT)
        t_wv = dma(GQ, wv_bf_d.ap(), wvT)
        t_wo = None
        for half in range(2):
            rsl = slice(half * 512, (half + 1) * 512)
            tl = dma(SQ, xb[0].rearrange("p (a b) -> p a b", a=IC),
                     woT.rearrange("(a p) c -> p a c", p=128)[:, :, rsl])
            VQ.waits(tl, t_nw)
            for a in range(IC):
                ts(VQ, xln_bf[:, a * 512:(a + 1) * 512],
                   xb[0][:, a * 512:(a + 1) * 512], nw_c[:, a:a + 1], None, ALU.mult)
            SQ.wait(VQ.tok())
            t_wo = dma(SQ, wo_bf_d.ap().rearrange("(a p) c -> p a c", p=128)[:, :, rsl],
                       xln_bf[:].rearrange("p (a b) -> p a b", a=IC))
            VQ.wait(t_wo)
        t_wcast = (t_wq, t_wk, t_wv, t_wo)

        GQ.wait(t_eye)
        TQ.wait(t_eye)
        tile_row = 0
        t_last_nat_read = None
        t_last_st = None
        for tgroup, base_off, ostride in (("up", 0, H), ("gate", EX_UP, H),
                                          ("down", EX_UP + EX_GATE, D)):
            for j in range(OC3):
                SQ.wait(t_last_nat_read)
                tl = dma(SQ, dq_nat[:], dq_sl[tile_row * 128:(tile_row + 1) * 128, :])
                tile_row += 1
                VQ.wait(tl)
                red(VQ, dq_s4[:], dq_nat[:].rearrange("p (w k) -> p w k", k=32),
                    ALU.max, absv=True)
                vfence()
                ts(VQ, dq_s4[:], dq_s4[:], 1.0 / 7.0, 1e-12, ALU.mult, ALU.max)
                vfence()
                recip(VQ, dq_r4[:], dq_s4[:])
                vfence()
                t_last_nat_read = tt(VQ, dq_t[:].rearrange("p (w k) -> p w k", k=32),
                                     dq_nat[:].rearrange("p (w k) -> p w k", k=32),
                                     dq_r4[:, :, None].broadcast_to([128, 4, 32]), ALU.mult)
                cp(VQ, dq_t[:].bitcast(I32), dq_t[:])
                cp(VQ, dq_t[:], dq_t[:].bitcast(I32))
                t_dq = tt(VQ, dq_bf[:].rearrange("p (w k) -> p w k", k=32),
                          dq_t[:].rearrange("p (w k) -> p w k", k=32),
                          dq_s4[:, :, None].broadcast_to([128, 4, 32]), ALU.mult)
                bank_write_begin("tr")
                TQ.wait(t_dq)
                t_tp = tr(pb_tr[:, 0:128], dq_bf[:])
                bank_write_end("tr")
                VQ.wait(t_tp)
                bank_read("tr", AQ)
                AQ.wait(t_last_st)
                t_ev = cp(AQ, dq_st[:], pb_tr[:, 0:128])
                bank_read_done("tr", AQ)
                SQ.wait(t_ev)
                if tgroup == "down":
                    icl, hc = j // IC, j % IC
                    off = base_off + icl * (128 * D) + hc * 128
                else:
                    off = base_off + j * 128
                t_last_st = SQ.run(
                    lambda e, o=bass.AP(ex_in, off, [[ostride, 128], [1, 128]]),
                    i=dq_st[:]: e.dma_start(out=o, in_=i), inc=16)
        GQ.wait(SQ.tok())
        t_gather = allgather(ex_in, ex_out)

        def load_x(m):
            tok0 = m * MAC
            return dma(SQ, xb[m % 2].rearrange("p (a b) -> p a b", a=IC),
                       bass.AP(xT.tensor, tok0, [[TOT, 128], [128 * TOT, IC], [1, MAC]]))

        def load_w(dram, slot, after_tok):
            SQ.wait(after_tok)
            return dma(SQ, wbuf[slot][:].rearrange("p (a b) -> p a b", a=IC),
                       dram.ap().rearrange("(a p) c -> p a c", p=128))

        def row_stats(src_fn, src_tok):
            bank_write_begin("row")
            TQ.wait(src_tok)
            for i in range(IC):
                mm(pb_row[:, 0:512], ones_col[:], src_fn(i),
                   start=(i == 0), stop=(i == IC - 1), inc=(1 if i == IC - 1 else 0))
            bank_write_end("row")
            bank_read("row", VQ)
            cp(VQ, rows[:, 0:512], pb_row[:, 0:512])
            bank_read_done("row", VQ)
            sq_mm_tok = [None, None]
            for i in range(IC):
                buf = sqst[:] if i % 2 == 0 else tst
                AQ.wait(sq_mm_tok[i % 2])
                t_sq = act(AQ, buf, src_fn(i), AF.Square)
                bank_write_begin("row")
                TQ.wait(t_sq)
                mm(pb_row[:, 0:512], ones_col[:], buf,
                   start=(i == 0), stop=(i == IC - 1), inc=1)
                bank_write_end("row")
                sq_mm_tok[i % 2] = TQ.tok()
            bank_read("row", VQ)
            cp(VQ, rows[:, 512:1024], pb_row[:, 0:512])
            bank_read_done("row", VQ)

        SQ.waits(*t_wcast)
        SQ.wait(GQ.tok())
        x_tok = [None, None]
        x_tok[0] = load_x(0)
        w_tok = {}

        for m in range(NMAC):
            is_halo = (m == 0)
            par = m % 2
            x_cur = xb[par]
            tok0 = m * MAC
            rtok0 = tok0 - HALO

            w_tok["wk"] = load_w(wk_bf_d, 0, None)
            w_tok["wq"] = load_w(wq_bf_d, 1, None)
            if m + 1 < NMAC:
                x_tok[(m + 1) % 2] = load_x(m + 1)

            VQ.wait(x_tok[par])
            AQ.wait(x_tok[par])

            row_stats(lambda i: x_cur[:, i * 512:(i + 1) * 512], x_tok[par])
            ts(VQ, rows[:, 0:512], rows[:, 0:512], 1.0 / D, None, ALU.mult)
            ts(VQ, rows[:, 512:1024], rows[:, 512:1024], 1.0 / D, None, ALU.mult)
            tt(VQ, rows[:, 1024:1536], rows[:, 0:512], rows[:, 0:512], ALU.mult)
            t_v = tt(VQ, rows[:, 512:1024], rows[:, 512:1024], rows[:, 1024:1536],
                     ALU.subtract)
            AQ.wait(t_v)
            t_r = rrow(rows[:, 512:1024], rows[:, 512:1024], 0)
            VQ.wait(t_r)
            bcast_row(rows[:, 0:512], mb, VQ.tok())
            t_bc = bcast_row(rows[:, 512:1024], rb, t_r)
            for i in range(IC):
                tt(VQ, tst, x_cur[:, i * 512:(i + 1) * 512], mb, ALU.subtract)
                tt(VQ, xln_bf[:, i * 512:(i + 1) * 512], tst, rb, ALU.mult)
            t_xln = VQ.tok()

            def proj_fm(wtok, wslot, out_f32, fn):
                TQ.waits(wtok, t_xln)
                for oc in range(8):
                    bk = "mm%d" % (oc % 2)
                    pb = pb_mm[oc % 2]
                    bank_write_begin(bk)
                    for ic in range(IC):
                        mm(pb[:], wbuf[wslot][:, ic * 1024 + oc * 128: ic * 1024 + (oc + 1) * 128],
                           xln_bf[:, ic * 512:(ic + 1) * 512],
                           start=(ic == 0), stop=(ic == IC - 1), inc=(1 if ic == IC - 1 else 0))
                    bank_write_end(bk)
                    bank_read(bk, AQ)
                    act(AQ, out_f32[:, oc * 512:(oc + 1) * 512], pb[:], fn)
                    bank_read_done(bk, AQ)
                return AQ.tok()

            t_f_done = proj_fm(w_tok["wk"], 0, ff, AF.Sigmoid)
            w_tok["wv"] = load_w(wv_bf_d, 0, TQ.tok())
            if not is_halo:
                t_q_done = proj_fm(w_tok["wq"], 1, qf, AF.Silu)
            else:
                t_q_done = AQ.tok()
            w_tok["wo"] = load_w(wo_bf_d, 1, TQ.tok())

            pch = {"scan": None, "ep": None, "kh": None, "tr": None, "omm": [None, None],
                   "stMM": None, "sup": None, "sbf": None}
            pend_state = None
            for ch in range(NCH):
                par_c = ch % 2
                qt_bf = qt_bfp[par_c]
                kt_bf = kt_bfp[par_c]
                dcol = dcolp[par_c]
                c0 = ch * C
                fview = ff[:].rearrange("p (s t) -> p s t", s=8)[:, :, c0:c0 + C]
                qview = qf[:].rearrange("p (s t) -> p s t", s=8)[:, :, c0:c0 + C]
                AQ.wait(t_f_done)
                AQ.wait(pch["scan"])
                act(AQ, gp1[:].rearrange("p (s t) -> p s t", s=8), fview, AF.Exp, scale=-1.0)
                t_g = act(AQ, gps[:], gp1[:], AF.Ln, bias=1.0)
                VQ.waits(t_g, pch["ep"])
                for sl in range(8):
                    scan_op(VQ, Apr[:, sl * C:(sl + 1) * C], gps[:, sl * C:(sl + 1) * C],
                            zrs[:], 0.0, ALU.add, ALU.bypass)
                t_scan = VQ.tok()
                pch["scan"] = t_scan
                AQ.waits(t_scan, pch["kh"], pch["sup"])
                cp(AQ, dcol[:, 0:8],
                   Apr[:].rearrange("p (s t) -> p s t", s=8)[:, :, C - 1:C].opt())
                act(AQ, dcol[:, 0:8], dcol[:, 0:8], AF.Exp, scale=-1.0)
                act(AQ, Em[:], Apr[:], AF.Exp, scale=-1.0)
                t_ep = act(AQ, Ep[:], Apr[:], AF.Exp)
                pch["ep"] = t_ep
                VQ.waits(t_ep, pch["omm"][par_c])
                if not is_halo:
                    tt(VQ, qt_bf[:].rearrange("p (s t) -> p s t", s=8), qview,
                       Em[:].rearrange("p (s t) -> p s t", s=8), ALU.mult)
                stt(VQ, tst2[:].rearrange("p (s t) -> p s t", s=8), fview, -1.0,
                    Ep[:].rearrange("p (s t) -> p s t", s=8), ALU.mult, ALU.mult)
                tt(VQ, ktf[:], tst2[:], Ep[:], ALU.add)
                if not is_halo:
                    cp(VQ, kt_bf[:], ktf[:])
                VQ.wait(pch["tr"])
                for sl in range(8):
                    ts(VQ, kh_bf[:, sl * C:(sl + 1) * C], ktf[:, sl * C:(sl + 1) * C],
                       dcol[:, sl:sl + 1], None, ALU.mult)
                t_kh = VQ.tok()
                pch["kh"] = t_kh

                if pend_state is not None:
                    pv_par, pv_dcol = pend_state
                    bank_read("st", VQ)
                    for sl in range(8):
                        stt(VQ, S_f[:, sl * 128:(sl + 1) * 128], S_f[:, sl * 128:(sl + 1) * 128],
                            pv_dcol[:, sl:sl + 1], pb_st[:, sl * 128:(sl + 1) * 128],
                            ALU.mult, ALU.add)
                    t_sup = VQ.tok()
                    pch["sup"] = t_sup
                    bank_read_done("st", VQ)
                    AQ.wait(t_sup)
                    t_sbf = cp(AQ, S_bf[:], S_f[:])
                    pch["sbf"] = t_sbf
                    pend_state = None

                TQ.wait(w_tok["wv"])
                for hb in range(2):
                    bk = "mm%d" % hb
                    bank_write_begin(bk)
                    for ic in range(IC):
                        mm(pb_mm[hb][0:C, :],
                           xln_bf[:, ic * 512 + c0: ic * 512 + c0 + C],
                           wbuf[0][:, ic * 1024 + hb * 512: ic * 1024 + (hb + 1) * 512],
                           start=(ic == 0), stop=(ic == IC - 1), inc=(1 if ic == IC - 1 else 0))
                    bank_write_end(bk)
                    bank_read(bk, AQ)
                    act(AQ, v_bf[:, hb * 512:(hb + 1) * 512], pb_mm[hb][0:C, :], AF.Identity)
                    bank_read_done(bk, AQ)
                t_vev = AQ.tok()

                bank_write_begin("tr")
                TQ.wait(t_kh)
                for sl in range(8):
                    tr(pb_tr[0:C, sl * 128:(sl + 1) * 128], kh_bf[:, sl * C:(sl + 1) * C],
                       inc=(1 if sl == 7 else 0))
                bank_write_end("tr")
                pch["tr"] = TQ.tok()
                bank_read("tr", AQ)
                AQ.wait(t_vev)
                t_khT = cp(AQ, khT_bf[:], pb_tr[0:C, :])
                bank_read_done("tr", AQ)

                if not is_halo:
                    bank_write_begin("sc")
                    TQ.wait(t_kh)
                    for sl in range(8):
                        mm(pb_sc[0:C, sl * C:(sl + 1) * C], kt_bf[:, sl * C:(sl + 1) * C],
                           qt_bf[:, sl * C:(sl + 1) * C], start=True, stop=True,
                           inc=(1 if sl == 7 else 0))
                    bank_write_end("sc")
                    bank_read("sc", VQ)
                    VQ.wait(t_mask)
                    t_scm = select(VQ, scm_bf[:].rearrange("p (s t) -> p s t", s=8),
                                   mask_c[:, None, :].broadcast_to([C, 8, C]),
                                   pb_sc[0:C, :].rearrange("p (s t) -> p s t", s=8),
                                   zrs[0:C, None, 0:C].broadcast_to([C, 8, C]))
                    bank_read_done("sc", VQ)

                    bank_write_begin("o")
                    TQ.waits(t_scm, t_vev, t_khT, pch["sbf"])
                    for sl in range(8):
                        mm(pb_o[:, sl * C:(sl + 1) * C], v_bf[:, sl * 128:(sl + 1) * 128],
                           scm_bf[:, sl * C:(sl + 1) * C], start=True, stop=False)
                        mm(pb_o[:, sl * C:(sl + 1) * C], S_bf[:, sl * 128:(sl + 1) * 128],
                           qt_bf[:, sl * C:(sl + 1) * C], start=False, stop=True,
                           inc=(1 if sl == 7 else 0))
                    bank_write_end("o")
                    pch["omm"][par_c] = TQ.tok()
                else:
                    pch["omm"][par_c] = TQ.tok()

                bank_write_begin("st")
                TQ.waits(t_khT, t_vev, pch["sup"])
                for sl in range(8):
                    mm(pb_st[:, sl * 128:(sl + 1) * 128], khT_bf[:, sl * 128:(sl + 1) * 128],
                       v_bf[:, sl * 128:(sl + 1) * 128], start=True, stop=True,
                       inc=(1 if sl == 7 else 0))
                bank_write_end("st")
                pch["stMM"] = TQ.tok()
                pend_state = (par_c, dcol)

                if not is_halo:
                    oview = o_bf[:].rearrange("p (s t) -> p s t", s=8)[:, :, c0:c0 + C]
                    bank_read("o", AQ)
                    act(AQ, sqst[:], pb_o[:], AF.Square)
                    act(AQ, oview, pb_o[:].rearrange("p (s t) -> p s t", s=8), AF.Identity)
                    bank_read_done("o", AQ)
                    bank_write_begin("row")
                    TQ.wait(AQ.tok())
                    for sl in range(8):
                        mm(pb_row[:, c0:c0 + C], ones_col[:], sqst[:, sl * C:(sl + 1) * C],
                           start=(sl == 0), stop=(sl == 7), inc=(1 if sl == 7 else 0))
                    bank_write_end("row")
                    AQ.wait(TQ.tok())

            pv_par, pv_dcol = pend_state
            bank_read("st", VQ)
            for sl in range(8):
                stt(VQ, S_f[:, sl * 128:(sl + 1) * 128], S_f[:, sl * 128:(sl + 1) * 128],
                    pv_dcol[:, sl:sl + 1], pb_st[:, sl * 128:(sl + 1) * 128],
                    ALU.mult, ALU.add)
            t_sup = VQ.tok()
            bank_read_done("st", VQ)
            AQ.wait(t_sup)
            t_sbf = cp(AQ, S_bf[:], S_f[:])
            TQ.wait(t_sbf)
            pend_state = None

            if not is_halo:
                c_row = mb[0:1, 0:512]
                bank_read("row", VQ)
                cp(VQ, c_row, pb_row[:, 0:512])
                bank_read_done("row", VQ)
                t_ms = ts(VQ, c_row, c_row, 1.0 / D, None, ALU.mult)
                AQ.wait(t_ms)
                t_c = rrow(c_row, c_row, 1)
                VQ.wait(t_c)

                TQ.waits(w_tok["wo"], AQ.tok())
                for dc in range(IC):
                    bk = "mm%d" % (dc % 2)
                    pb = pb_mm[dc % 2]
                    bank_write_begin(bk)
                    for kc in range(IC):
                        mm(pb[:], wbuf[1][:, kc * 1024 + dc * 128: kc * 1024 + (dc + 1) * 128],
                           o_bf[:, kc * 512:(kc + 1) * 512],
                           start=(kc == 0), stop=(kc == IC - 1), inc=(1 if kc == IC - 1 else 0))
                    bank_write_end(bk)
                    bank_read(bk, AQ)
                    act(AQ, Xf[:, dc * 512:(dc + 1) * 512], pb[:], AF.Identity)
                    bank_read_done(bk, AQ)
                t_X = AQ.tok()

                VQ.wait(t_X)
                row_stats(lambda i: Xf[:, i * 512:(i + 1) * 512], t_X)
                ts(VQ, rows[:, 0:512], rows[:, 0:512], 1.0 / D, None, ALU.mult)
                ts(VQ, rows[:, 512:1024], rows[:, 512:1024], 1.0 / D, None, ALU.mult)
                tt(VQ, rows[:, 1024:1536], rows[:, 0:512], rows[:, 0:512], ALU.mult)
                tt(VQ, rows[:, 512:1024], rows[:, 512:1024], rows[:, 1024:1536], ALU.subtract)
                tt(VQ, rows[:, 1024:1536], c_row, c_row, ALU.mult)
                t_cv = tt(VQ, rows[:, 512:1024], rows[:, 512:1024], rows[:, 1024:1536], ALU.mult)
                AQ.wait(t_cv)
                t_r3 = rrow(rows[:, 512:1024], rows[:, 512:1024], 0)
                VQ.wait(t_r3)
                tt(VQ, rows[:, 1024:1536], c_row, rows[:, 512:1024], ALU.mult)
                tt(VQ, rows[:, 0:512], rows[:, 0:512], rows[:, 1024:1536], ALU.mult)
                ts(VQ, rows[:, 0:512], rows[:, 0:512], -1.0, None, ALU.mult)
                bcast_row(rows[:, 1024:1536], ab, VQ.tok())
                bcast_row(rows[:, 0:512], bb, VQ.tok())

                for dc in range(IC):
                    tt(VQ, tst, Xf[:, dc * 512:(dc + 1) * 512], ab, ALU.mult)
                    tt(VQ, tst2, tst, bb, ALU.add)
                    tt(VQ, y1f[:, dc * 512:(dc + 1) * 512], tst2,
                       x_cur[:, dc * 512:(dc + 1) * 512], ALU.add)
                t_y1 = VQ.tok()
                SQ.wait(t_y1)
                dma(SQ, bass.AP(y1_d, rtok0, [[NTOK, 128], [128 * NTOK, IC], [1, MAC]]),
                    y1f[:].rearrange("p (a b) -> p a b", a=IC))

                VQ.wait(t_y1)
                AQ.wait(t_y1)
                row_stats(lambda i: y1f[:, i * 512:(i + 1) * 512], t_y1)
                ts(VQ, rows[:, 0:512], rows[:, 0:512], 1.0 / D, None, ALU.mult)
                ts(VQ, rows[:, 512:1024], rows[:, 512:1024], 1.0 / D, None, ALU.mult)
                tt(VQ, rows[:, 1024:1536], rows[:, 0:512], rows[:, 0:512], ALU.mult)
                t_v2 = tt(VQ, rows[:, 512:1024], rows[:, 512:1024], rows[:, 1024:1536],
                          ALU.subtract)
                AQ.wait(t_v2)
                t_r2 = rrow(rows[:, 512:1024], rows[:, 512:1024], 0)
                VQ.wait(t_r2)
                mi = m - 1
                SQ.wait(VQ.tok())
                dma(SQ, bass.AP(ln2_d, mi * MAC, [[1, 1], [1, MAC]]), rows[0:1, 0:512])
                dma(SQ, bass.AP(ln2_d, NTOK + mi * MAC, [[1, 1], [1, MAC]]),
                    rows[0:1, 512:1024])
                bcast_row(rows[:, 0:512], mb, VQ.tok())
                bcast_row(rows[:, 512:1024], rb, VQ.tok())
                for dc in range(IC):
                    tt(VQ, tst, y1f[:, dc * 512:(dc + 1) * 512], mb, ALU.subtract)
                    t_x2 = tt(VQ, tst2, tst, rb, ALU.mult)
                    SQ.wait(t_x2)
                    t_st2 = dma(SQ, bass.AP(xln2_d, rtok0 + dc * 128 * NTOK,
                                            [[NTOK, 128], [1, MAC]]), tst2)
                    red(VQ, maxst[:, (m - 1) * 8 + dc:(m - 1) * 8 + dc + 1], tst2,
                        ALU.max, absv=True)
                    VQ.wait(t_st2)

            if DEBUG_DUMP and m == 1:
                barrier_all()
                cp(VQ, tst, S_f[:, 0:512])
                cp(VQ, tst2, S_f[:, 512:1024])
                t_d1 = VQ.tok()
                SQ.wait(t_d1)
                dma(SQ, bass.AP(dbg_S.tensor, 0, [[1024, 128], [1, 512]]), tst)
                dma(SQ, bass.AP(dbg_S.tensor, 512, [[1024, 128], [1, 512]]), tst2)
                VQ.wait(SQ.tok())
                cp(VQ, tst, pb_st[:, 0:512])
                cp(VQ, tst2, pb_st[:, 512:1024])
                t_d1b = VQ.tok()
                SQ.wait(t_d1b)
                dma(SQ, bass.AP(dbg_pst.tensor, 0, [[1024, 128], [1, 512]]), tst)
                dma(SQ, bass.AP(dbg_pst.tensor, 512, [[1024, 128], [1, 512]]), tst2)
                cp(VQ, SCR[:, 0:512], qt_bf[:])
                cp(VQ, SCR[:, 512:1024], kh_bf[:])
                cp(VQ, SCR[:, 1024:1536].rearrange("p (a b) -> p a b", a=8)[0:64],
                   scm_bf[:].rearrange("p (a b) -> p a b", a=8))
                t_d2 = VQ.tok()
                SQ.wait(t_d2)
                dma(SQ, dbg_qt, SCR[:, 0:512])
                dma(SQ, dbg_kh, SCR[:, 512:1024])
                dma(SQ, dbg_dc, dcol[:])
                dma(SQ, bass.AP(dbg_scm.tensor, 0, [[512, 64], [1, 512]]), SCR[0:64, 1024:1536])
                for dcb in range(IC):
                    cp(VQ, tst, o_bf[:, dcb * 512:(dcb + 1) * 512])
                    t_d3 = VQ.tok()
                    SQ.wait(t_d3)
                    dma(SQ, bass.AP(dbg_o.tensor, dcb * 512, [[4096, 128], [1, 512]]), tst)
                    VQ.wait(SQ.tok())
            barrier_all()

        def finalize_scalar(run_ap, ncols, bar, post_scale, result_cols):
            SQ.wait(VQ.tok())
            for c_ in range(ncols):
                dma(SQ, bass.AP(barsc, c_ * 128, [[1, 1], [1, 128]]),
                    run_ap[:, c_:c_ + 1])
            t_all = SQ.tok()
            SQ.wait(t_all)
            VQ.wait(t_all)
            for c_ in range(ncols):
                t_bl = dma(SQ, mb[0:1, 0:128], bass.AP(barsc, c_ * 128, [[1, 1], [1, 128]]))
                VQ.wait(t_bl)
                t_rd = red(VQ, scal[0:1, 8 + c_:9 + c_], mb[0:1, 0:128], ALU.max)
                SQ.wait(t_rd)
            vfence()
            t_red = VQ.tok()
            SQ.wait(t_red)
            t_bi = dma(SQ, bass.AP(bar_in[bar], 0, [[1, 1], [1, ncols]]),
                       scal[0:1, 8:8 + ncols])
            GQ.wait(t_bi)
            cc_cnt[0] += 1
            ccv = cc_cnt[0]
            GQ.ops.append(lambda e, i=bar_in[bar], o=bar_out[bar], s=ccs:
                          e.collective_compute("AllGather", ALU.bypass,
                                               replica_groups=[list(range(NCORES))],
                                               ins=[i.ap().opt()], outs=[o.ap().opt()])
                          .then_inc(s, 1))
            GQ.wait((ccs, ccv))
            t_bo = dma(GQ, scal[0:1, 32:64], bass.AP(bar_out[bar], 0, [[1, 1], [1, 32]]))
            VQ.wait(t_bo)
            for j, rc in enumerate(result_cols):
                red(VQ, scal[0:1, j:j + 1],
                    scal[0:1, 32:64].rearrange("p (a b) -> p a b", a=8)[:, :, j:j + 1].opt(),
                    ALU.max)
                vfence()
                ts(VQ, scal[0:1, j:j + 1], scal[0:1, j:j + 1], post_scale, 1e-12,
                   ALU.mult, ALU.max)
                vfence()
                recip(VQ, scal[0:1, 16 + j:17 + j], scal[0:1, j:j + 1])
                vfence()
                for src, dstc in ((scal[0:1, j:j + 1], rc), (scal[0:1, 16 + j:17 + j], rc + 1)):
                    bank_write_begin("sc")
                    TQ.wait(VQ.tok())
                    mm(pb_sc[:, 0:1], ones_row[:], src, True, True, inc=1)
                    bank_write_end("sc")
                    bank_read("sc", VQ)
                    cp(VQ, scol[:, dstc:dstc + 1], pb_sc[:, 0:1])
                    bank_read_done("sc", VQ)

        vfence()
        red(VQ, runmax[:], maxst[:, 0:64], ALU.max)
        vfence()
        finalize_scalar(runmax[:], 1, "A", 1.0 / 32767.0, [0])

        upW = ARENA[:].bitcast(BF16)[:, 0:24576]
        gateW = ARENA[:].bitcast(BF16)[:, 24576:49152]
        GQ.wait(t_gather)
        SQ.wait(GQ.tok())
        wload = []
        for icx in range(IC):
            wload.append(dma(SQ, upW[:, icx * H:(icx + 1) * H],
                             bass.AP(ex_out, icx * NEX, [[H, 128], [1, H]])))
            wload.append(dma(SQ, gateW[:, icx * H:(icx + 1) * H],
                             bass.AP(ex_out, icx * NEX + EX_UP, [[H, 128], [1, H]])))
        t_wl = SQ.tok()
        TQ.wait(t_wl)
        xln2f = SCR[:]
        i32v = wbuf[0][:].bitcast(I32)
        codebf = xln_bf[:]
        memset(VQ, maxst[:], 0.0)
        zst = [wbuf[1][:].bitcast(F32)[:, 0:2048], wbuf[1][:].bitcast(F32)[:, 2048:4096]]
        zst_cons = [None, None]
        memset(VQ, dcol[:], 0.0)
        for mi in range(NRMAC):
            rt0 = mi * MAC
            t_x2l = dma(SQ, xln2f.rearrange("p (a b) -> p a b", a=IC),
                        bass.AP(xln2_d, rt0, [[NTOK, 128], [128 * NTOK, IC], [1, MAC]]))
            VQ.wait(t_x2l)
            ts(VQ, i32v[:, 0:4096], xln2f, scol[:, 1:2], None, ALU.mult)
            t_code = cp(VQ, codebf, i32v[:, 0:4096])
            TQ.wait(t_code)
            for tix, (Wv_, zd, runc) in enumerate(((upW, zup_d, 0), (gateW, zgate_d, 1))):
                for g in range(6):
                    par_ = g % 2
                    zstage = zst[par_]
                    for t_ in (zst_cons[par_] or []):
                        AQ.wait(t_)
                    for oc4 in range(4):
                        oc = g * 4 + oc4
                        bk = "mm%d" % (oc % 2)
                        pb = pb_mm[oc % 2]
                        bank_write_begin(bk)
                        for icx in range(IC):
                            mm(pb[:], Wv_[:, icx * H + oc * 128: icx * H + (oc + 1) * 128],
                               codebf[:, icx * 512:(icx + 1) * 512],
                               start=(icx == 0), stop=(icx == IC - 1),
                               inc=(1 if icx == IC - 1 else 0))
                        bank_write_end(bk)
                        bank_read(bk, AQ)
                        act(AQ, zstage[:, oc4 * 512:(oc4 + 1) * 512], pb[:],
                            AF.Identity, scale=scol[:, 0:1])
                        bank_read_done(bk, AQ)
                    t_ev = AQ.tok()
                    VQ.wait(t_ev)
                    t_rm = red(VQ, maxst[:, tix * 48 + mi * 6 + g: tix * 48 + mi * 6 + g + 1],
                               zstage[:, 0:2048], ALU.max, absv=True)
                    SQ.wait(t_ev)
                    t_zs = dma(SQ, bass.AP(zd, g * 4 * 128 * NTOK + rt0,
                                           [[NTOK, 128], [128 * NTOK, 4], [1, MAC]]),
                               zstage[:, 0:2048].rearrange("p (a b) -> p a b", a=4))
                    zst_cons[par_] = [t_rm, t_zs]
                    if DEBUG_F and mi == 0 and tix == 0 and g == 0:
                        t_dz = dma(SQ, dbg_z, zstage[:, 0:2048])
                        zst_cons[par_].append(t_dz)
            SQ.wait(VQ.tok())
            SQ.wait(TQ.tok())
            VQ.wait(TQ.tok())
        barrier_all()

        vfence()
        if DEBUG_F:
            SQ.wait(VQ.tok())
            t_dmx = dma(SQ, dbg_mx, maxst[:])
            VQ.wait(t_dmx)
        red(VQ, dcol[:, 0:1], maxst[:, 0:48], ALU.max)
        red(VQ, dcol[:, 1:2], maxst[:, 48:96], ALU.max)
        vfence()
        finalize_scalar(dcol[:, 0:2], 2, "B", 1.0 / 32767.0, [2, 4])

        memset(VQ, maxst[:], 0.0)
        zgv = wbuf[1][:].bitcast(F32)
        zgi = wbuf[1][:].bitcast(I32)
        gq_f = SCR[:, 0:2048]
        fs3 = SCR[:, 2048:4096]
        GW = 4
        i32g = wbuf[0][:].bitcast(I32)
        u32 = i32g[:, 2048:4096]
        AR_I = ARENA[:].bitcast(I32)
        fs3P = [ar[:, 0:2048], ar[:, 2048:4096]]
        gqP = [ar[:, 4096:6144], ar[:, 6144:8192]]
        u32P = [AR_I[:, 8192:10240], AR_I[:, 10240:12288]]
        pstF = [dict(t_vq=None, t_gst=None) for _ in range(2)]
        gi = 0
        for mi in range(NRMAC):
            rt0 = mi * MAC
            for g in range(6):
                p = gi & 1
                st = pstF[p]
                zgv_p = zgv[:, p * 2048:p * 2048 + 2048]
                zgi_p = zgi[:, p * 2048:p * 2048 + 2048]
                i32_p = i32g[:, p * 2048:p * 2048 + 2048]
                fs_p = fs3P[p]
                gq_p = gqP[p]
                SQ.wait(st["t_vq"])
                t_zl = dma(SQ, zgv_p.rearrange("p (a b) -> p a b", a=GW),
                           bass.AP(zgate_d, g * GW * 128 * NTOK + rt0,
                                   [[NTOK, 128], [128 * NTOK, GW], [1, MAC]]))
                VQ.waits(t_zl, st["t_gst"])
                t_ci = ts(VQ, i32_p, zgv_p, scol[:, 5:6], None, ALU.mult)
                GQ.wait(t_ci)
                t_gst = dma(GQ, bass.AP(gcode_d, g * GW * 128 * NTOK + rt0,
                                        [[NTOK, 128], [128 * NTOK, GW], [1, MAC]]),
                            i32_p.rearrange("p (a b) -> p a b", a=GW))
                AQ.waits(t_ci, st["t_vq"])
                t_sg = act(AQ, fs_p, i32_p, AF.Sigmoid, scale=scol[:, 4:5])
                t_gqa = act(AQ, gq_p, i32_p, AF.Identity, scale=scol[:, 4:5])
                VQ.wait(t_sg)
                ts(VQ, zgi_p, fs_p, 65536.0, 65535.0, ALU.mult, ALU.min)
                VQ.wait(t_gqa)
                tt(VQ, fs_p, gq_p, zgi_p, ALU.mult)
                red(VQ, maxst[:, mi * 6 + g: mi * 6 + g + 1], fs_p, ALU.max, absv=True)
                st["t_vq"] = VQ.tok()
                st["t_gst"] = t_gst
                gi += 1
        SQ.waits(pstF[0]["t_vq"], pstF[1]["t_vq"])
        GQ.waits(pstF[0]["t_vq"], pstF[1]["t_vq"])
        AQ.waits(pstF[0]["t_vq"], pstF[1]["t_vq"])
        barrier_all()

        vfence()
        red(VQ, dcol[:, 2:3], maxst[:, 0:48], ALU.max)
        vfence()
        finalize_scalar(dcol[:, 2:3], 1, "C", (1.0 / 65536.0) / 32767.0, [6])
        vfence()
        ts(VQ, scol[:, 10:11], scol[:, 7:8], 1.0 / 65536.0, None, ALU.mult)
        vfence()

        memset(VQ, maxst[:], 0.0)
        pst3 = [dict(t_vq=None, t_g2st=None, t_ust=None) for _ in range(2)]
        gi = 0
        for mi in range(NRMAC):
            rt0 = mi * MAC
            for g in range(6):
                p = gi & 1
                st = pst3[p]
                zgv_p = zgv[:, p * 2048:p * 2048 + 2048]
                i32_p = i32g[:, p * 2048:p * 2048 + 2048]
                fs_p = fs3P[p]
                gq_p = gqP[p]
                u32_p = u32P[p]
                base = g * GW * 128 * NTOK + rt0
                apz = [[NTOK, 128], [128 * NTOK, GW], [1, MAC]]
                GQ.wait(st["t_vq"])
                t_gl = dma(GQ, gq_p.rearrange("p (a b) -> p a b", a=GW),
                           bass.AP(gcode_d, base, apz))
                SQ.wait(st["t_vq"])
                t_ul = dma(SQ, zgv_p.rearrange("p (a b) -> p a b", a=GW),
                           bass.AP(zup_d, base, apz))
                AQ.waits(t_gl, st["t_vq"])
                t_sg = act(AQ, fs_p, gq_p, AF.Sigmoid, scale=scol[:, 4:5])
                t_gqa = act(AQ, gq_p, gq_p, AF.Identity, scale=scol[:, 4:5])
                VQ.waits(t_sg, st["t_g2st"])
                ts(VQ, i32_p, fs_p, 65536.0, 65535.0, ALU.mult, ALU.min)
                VQ.wait(t_gqa)
                tt(VQ, fs_p, gq_p, i32_p, ALU.mult)
                t_g2c = ts(VQ, i32_p, fs_p, scol[:, 10:11], None, ALU.mult)
                GQ.wait(t_g2c)
                t_g2st = dma(GQ, bass.AP(g2code_d, base, apz),
                             i32_p.rearrange("p (a b) -> p a b", a=GW))
                AQ.wait(t_g2c)
                t_g2qa = act(AQ, fs_p, i32_p, AF.Identity,
                             scale=scol[:, 6:7])
                VQ.waits(t_ul, st["t_ust"])
                t_uc = ts(VQ, u32_p, zgv_p, scol[:, 3:4], None, ALU.mult)
                GQ.wait(t_uc)
                t_ust = dma(GQ, bass.AP(ucode_d, base, apz),
                            u32_p.rearrange("p (a b) -> p a b", a=GW))
                VQ.wait(t_g2qa)
                t_op = tt(VQ, gq_p, fs_p, u32_p, ALU.mult)
                red(VQ, maxst[:, mi * 6 + g: mi * 6 + g + 1], gq_p, ALU.max, absv=True)
                st["t_vq"] = VQ.tok()
                st["t_g2st"] = t_g2st
                st["t_ust"] = t_ust
                gi += 1
        for st in pst3:
            SQ.wait(st["t_vq"])
            GQ.waits(st["t_vq"], st["t_g2st"], st["t_ust"])
            AQ.wait(st["t_vq"])
            VQ.waits(st["t_g2st"], st["t_ust"])
        barrier_all()

        vfence()
        red(VQ, dcol[:, 3:4], maxst[:, 0:48], ALU.max)
        vfence()
        tt(VQ, dcol[:, 3:4], dcol[:, 3:4], scol[:, 2:3], ALU.mult)
        vfence()
        finalize_scalar(dcol[:, 3:4], 1, "D", 1.0 / 32767.0, [8])
        vfence()
        tt(VQ, scol[:, 11:12], scol[:, 9:10], scol[:, 2:3], ALU.mult)
        vfence()

        downW = ARENA[:].bitcast(BF16)[:, 0:24576]
        GQ.wait(SQ.tok())
        dwl = []
        for icx in range(NCORES):
            for icl in range(3):
                icg = icx * 3 + icl
                dwl.append(dma(SQ, downW[:, icg * D:(icg + 1) * D],
                               bass.AP(ex_out, icx * NEX + EX_UP + EX_GATE + icl * 128 * D,
                                       [[D, 128], [1, D]])))
        t_dwl = SQ.tok()
        TQ.wait(t_dwl)
        Xs = ARENA[:, 12288:16384]
        y1s = ARENA[:, 16384:20480]
        y2s = ARENA[:, 20480:24576]

        def codebf_sl(ch):
            if ch < 16:
                return wbuf[1][:, ch * 512:(ch + 1) * 512]
            return o_bf[:, (ch - 16) * 512:(ch - 15) * 512]

        t_cg_prev = None
        for mi in range(NRMAC):
            rt0 = mi * MAC
            t_y1l = dma(SQ, y1s.rearrange("p (a b) -> p a b", a=IC),
                        bass.AP(y1_d, rt0, [[NTOK, 128], [128 * NTOK, IC], [1, MAC]]))
            code_toks = []
            for g in range(6):
                base = g * GW * 128 * NTOK + rt0
                apz = [[NTOK, 128], [128 * NTOK, GW], [1, MAC]]
                t_g2l = dma(GQ, fs3.rearrange("p (a b) -> p a b", a=GW),
                            bass.AP(g2code_d, base, apz))
                t_ucl = dma(GQ, gq_f.rearrange("p (a b) -> p a b", a=GW),
                            bass.AP(ucode_d, base, apz))
                AQ.wait(t_g2l)
                t_g2qa = act(AQ, fs3, fs3, AF.Identity, scale=scol[:, 6:7])
                VQ.waits(t_g2qa, t_ucl, t_cg_prev)
                tt(VQ, fs3, fs3, gq_f, ALU.mult)
                t_oc = ts(VQ, i32g[:, 0:GW * 512], fs3, scol[:, 11:12], None, ALU.mult)
                t_vqg = VQ.tok()
                AQ.wait(t_oc)
                for oc6 in range(GW):
                    ch = g * GW + oc6
                    act(AQ, codebf_sl(ch), i32g[:, oc6 * 512:(oc6 + 1) * 512],
                        AF.Identity)
                t_cg = AQ.tok()
                t_cg_prev = t_cg
                code_toks.append(t_cg)
                GQ.wait(t_vqg)
            for dc in range(IC):
                bk = "mm%d" % (dc % 2)
                pb = pb_mm[dc % 2]
                bank_write_begin(bk)
                for ch in range(24):
                    if dc == 0:
                        TQ.waits(code_toks[ch // 4], t_dwl)
                    mm(pb[:], downW[:, ch * D + dc * 128: ch * D + (dc + 1) * 128],
                       codebf_sl(ch), start=(ch == 0), stop=(ch == 23),
                       inc=(1 if ch == 23 else 0))
                bank_write_end(bk)
                bank_read(bk, AQ)
                act(AQ, Xs[:, dc * 512:(dc + 1) * 512], pb[:], AF.Identity,
                    scale=scol[:, 8:9])
                bank_read_done(bk, AQ)
            t_X4 = AQ.tok()
            VQ.wait(t_X4)
            row_stats(lambda i: Xs[:, i * 512:(i + 1) * 512], t_X4)
            ts(VQ, rows[:, 0:512], rows[:, 0:512], 1.0 / D, None, ALU.mult)
            ts(VQ, rows[:, 512:1024], rows[:, 512:1024], 1.0 / D, None, ALU.mult)
            tt(VQ, rows[:, 1024:1536], rows[:, 0:512], rows[:, 0:512], ALU.mult)
            t_v4 = tt(VQ, rows[:, 512:1024], rows[:, 512:1024], rows[:, 1024:1536],
                      ALU.subtract)
            AQ.wait(t_v4)
            t_r4 = rrow(rows[:, 512:1024], rows[:, 512:1024], 0)
            VQ.wait(t_r4)
            tt(VQ, rows[:, 0:512], rows[:, 0:512], rows[:, 512:1024], ALU.mult)
            ts(VQ, rows[:, 0:512], rows[:, 0:512], -1.0, None, ALU.mult)
            bcast_row(rows[:, 512:1024], ab, VQ.tok())
            bcast_row(rows[:, 0:512], bb, VQ.tok())
            VQ.wait(t_y1l)
            for dc in range(IC):
                tt(VQ, tst, Xs[:, dc * 512:(dc + 1) * 512], ab, ALU.mult)
                tt(VQ, tst2, tst, bb, ALU.add)
                tt(VQ, y2s[:, dc * 512:(dc + 1) * 512], tst2,
                   y1s[:, dc * 512:(dc + 1) * 512], ALU.add)
            t_y2d = VQ.tok()
            SQ.wait(t_y2d)
            dma(SQ, bass.AP(outT.tensor, rt0, [[NTOK, 128], [128 * NTOK, IC], [1, MAC]]),
                y2s.rearrange("p (a b) -> p a b", a=IC))
            GQ.wait(SQ.tok())
            VQ.wait(SQ.tok())
        if DEBUG_F:
            dma(SQ, dbg_scol, scol[:, 0:12])
            dma(SQ, dbg_w, upW[:, 0:3072])
            for mm_ in range(NRMAC):
                apv = [[NTOK, 128], [128 * NTOK, IC], [1, MAC]]
                tl1 = dma(SQ, SCR[:].rearrange("p (a b) -> p a b", a=IC),
                          bass.AP(y1_d, mm_ * MAC, apv))
                SQ.wait(tl1)
                dma(SQ, bass.AP(dbg_y1.tensor, mm_ * MAC, apv),
                    SCR[:].rearrange("p (a b) -> p a b", a=IC))
                tl2 = dma(SQ, SCR[:].rearrange("p (a b) -> p a b", a=IC),
                          bass.AP(xln2_d, mm_ * MAC, apv))
                SQ.wait(tl2)
                dma(SQ, bass.AP(dbg_xl2.tensor, mm_ * MAC, apv),
                    SCR[:].rearrange("p (a b) -> p a b", a=IC))
                SQ.wait(SQ.tok())
            for j_, mm_ in enumerate((0, 6)):
                apz3 = [[NTOK, 128], [128 * NTOK, 24], [1, MAC]]
                apd3 = [[1024, 128], [128 * 1024, 24], [1, MAC]]
                for (srcd, dstd) in ((zup_d, dbg_zu), (ucode_d, dbg_uc), (g2code_d, dbg_g2c)):
                    tlz = dma(GQ, wbuf[0][:].bitcast(F32)[:, 0:4096].rearrange(
                        "p (a b) -> p a b", a=8)[:, 0:8, 0:512],
                        bass.AP(srcd, mm_ * MAC, [[NTOK, 128], [128 * NTOK, 8], [1, MAC]]))
                    GQ.wait(tlz)
                    tsz = dma(GQ, bass.AP(dstd.tensor if hasattr(dstd, "tensor") else dstd,
                                          j_ * MAC, [[1024, 128], [128 * 1024, 8], [1, MAC]]),
                              wbuf[0][:].bitcast(F32)[:, 0:4096].rearrange(
                        "p (a b) -> p a b", a=8)[:, 0:8, 0:512])
                    GQ.wait(tsz)
        SQ.wait(SQ.tok())
        GQ.wait(GQ.tok())

        with nc.Block() as block:
            @block.sync
            def _(q):
                for f_ in SQ.ops:
                    f_(q)

            @block.gpsimd
            def _(q):
                for f_ in GQ.ops:
                    f_(q)

            @block.tensor
            def _(q):
                for f_ in TQ.ops:
                    f_(q)

            @block.vector
            def _(q):
                for f_ in VQ.ops:
                    f_(q)

            @block.scalar
            def _(q):
                for f_ in AQ.ops:
                    f_(q)

    return nc


def _stage_inputs(inputs):
    x = np.asarray(inputs["L0_attn_x"])
    xTs = [np.ascontiguousarray(x[s].T) for s in range(2)]
    eye = np.eye(128, dtype=np.float32)
    mask = np.zeros((C, C), dtype=np.uint8)
    for s_ in range(C):
        mask[s_, s_:] = 1
    nw = np.ascontiguousarray(np.asarray(inputs["L0_attn_norm_w"]).reshape(IC, 128).T)
    nw = np.ascontiguousarray(nw.astype(np.float32))
    w_up = np.ascontiguousarray(np.asarray(inputs["L0_ffn_w_up"], dtype=np.float32))
    w_gate = np.ascontiguousarray(np.asarray(inputs["L0_ffn_w_gate"], dtype=np.float32))
    w_down = np.ascontiguousarray(np.asarray(inputs["L0_ffn_w_down"], dtype=np.float32))
    base = dict(
        wqT=np.ascontiguousarray(np.asarray(inputs["L0_attn_Wq"], dtype=np.float32).T),
        wkT=np.ascontiguousarray(np.asarray(inputs["L0_attn_Wk"], dtype=np.float32).T),
        wvT=np.ascontiguousarray(np.asarray(inputs["L0_attn_Wv"], dtype=np.float32).T),
        woT=np.ascontiguousarray(np.asarray(inputs["L0_attn_Wo"], dtype=np.float32).T),
        nwfm=nw, eye_h=eye, mask_h=mask,
    )
    maps = []
    for core in range(NCORES):
        seq, seg = core // 4, core % 4
        t0 = seg * NTOK
        xs = np.zeros((D, TOT), dtype=np.float32)
        lo = t0 - HALO
        if lo < 0:
            xs[:, HALO:] = xTs[seq][:, t0:t0 + NTOK]
        else:
            xs[:, :] = xTs[seq][:, lo:t0 + NTOK]
        sl = []
        for j in range(OC3):
            sl.append(w_up[j * 128:(j + 1) * 128, core * 128:(core + 1) * 128])
        for j in range(OC3):
            sl.append(w_gate[j * 128:(j + 1) * 128, core * 128:(core + 1) * 128])
        for icl in range(3):
            icg = core * 3 + icl
            for hc in range(IC):
                sl.append(w_down[hc * 128:(hc + 1) * 128, icg * 128:(icg + 1) * 128])
        m = dict(base)
        m["xT"] = np.ascontiguousarray(xs)
        m["dq_sl"] = np.ascontiguousarray(np.concatenate(sl, axis=0))
        maps.append(m)
    return maps


_NC_CACHE = None


def kernel(L0_attn_x, L0_attn_Wq, L0_attn_Wk, L0_attn_Wv, L0_attn_Wo,
           L0_attn_norm_w, L0_ffn_w_gate, L0_ffn_w_up, L0_ffn_w_down):
    global _NC_CACHE
    inputs = dict(L0_attn_x=L0_attn_x, L0_attn_Wq=L0_attn_Wq, L0_attn_Wk=L0_attn_Wk,
                  L0_attn_Wv=L0_attn_Wv, L0_attn_Wo=L0_attn_Wo,
                  L0_attn_norm_w=L0_attn_norm_w, L0_ffn_w_gate=L0_ffn_w_gate,
                  L0_ffn_w_up=L0_ffn_w_up, L0_ffn_w_down=L0_ffn_w_down)
    if _NC_CACHE is None:
        _NC_CACHE = build_program()
    nc = _NC_CACHE
    maps = _stage_inputs(inputs)
    res = run_bass_kernel_spmd(nc, maps, core_ids=list(range(NCORES)))
    out = np.empty((2, 16384, D), dtype=np.float32)
    for core in range(NCORES):
        seq, seg = core // 4, core % 4
        o = np.asarray(res.results[core]["outT"])
        out[seq, seg * NTOK:(seg + 1) * NTOK, :] = o.T
    return out
